# version 08
# baseline (speedup 1.0000x reference)
import numpy as np
import ml_dtypes
from contextlib import ExitStack

import concourse.bass as bass
import concourse.mybir as mybir
from concourse.bass_utils import run_bass_kernel_spmd

F32 = mybir.dt.float32
BF16 = mybir.dt.bfloat16
I32 = mybir.dt.int32
AF = mybir.ActivationFunctionType
ALU = mybir.AluOpType
AX = mybir.AxisListType
BF = ml_dtypes.bfloat16

T = 2048
C = 1024
H = 16
D = 64
HID = 4096
NTC = T // 128
NCC = C // 128
NHC = HID // 128
EPS_LN = 1e-5
EPS_RMS = 1e-6
FT = 256
NFT = T // FT

DEBUG_TAPS = []


class Cell:
    __slots__ = ("sem", "v")

    def __init__(self):
        self.sem = None
        self.v = None


class Em:

    def __init__(self):
        self.q = {e: [] for e in ("tensor", "vector", "scalar", "gpsimd", "sync")}
        self.cnt = {}
        self.resp = {}
        self.resc = {}

    def fill(self, cell, sem):
        cell.sem = sem
        cell.v = self.at(sem)

    def at(self, sem):
        return self.cnt.get(sem, 0)

    def emit(self, eng, fn, waits=(), inc=None, incn=1):
        self.q[eng].append((fn, tuple(waits), inc, incn))
        if inc is not None:
            self.cnt[inc] = self.cnt.get(inc, 0) + incn

    def prod(self, res, sem):
        self.resp[res] = (sem, self.at(sem))
        self.resc[res] = {}

    def cons(self, res, sem):
        d = self.resc.setdefault(res, {})
        d[sem] = self.at(sem)

    def need(self, *ress):
        out = []
        for r in ress:
            p = self.resp.get(r)
            if p:
                out.append(p)
        return out

    def free(self, *ress):
        out = []
        for r in ress:
            out.extend(self.resc.get(r, {}).items())
        return out

    def finalize(self, block, sems):
        def mk(name):
            ops = self.q[name]

            def run(e):
                hw = {}
                for fn, waits, inc, incn in ops:
                    merged = {}
                    for w in waits:
                        if isinstance(w, Cell):
                            assert w.v is not None, "unfilled Cell wait"
                            s, v = w.sem, w.v
                        else:
                            s, v = w
                        if v > 0 and v > merged.get(s, 0):
                            merged[s] = v
                    for s, v in merged.items():
                        if v > hw.get(s, 0):
                            e.wait_ge(sems[s], v)
                            hw[s] = v
                    inst = fn(e)
                    if inc is not None:
                        inst.then_inc(sems[inc], incn)
            return run
        block.tensor(mk("tensor"))
        block.vector(mk("vector"))
        block.scalar(mk("scalar"))
        block.gpsimd(mk("gpsimd"))
        block.sync(mk("sync"))


def _host_consts():
    half = D // 2
    inv_freq = 1.0 / (10000.0 ** (np.arange(0, half, dtype=np.float64) * 2.0 / D))
    ang = np.arange(T, dtype=np.float64)[:, None] * inv_freq[None, :]
    cosv = np.cos(ang).astype(np.float32)
    sinv = np.sin(ang).astype(np.float32)
    cosrep = np.zeros((128, T), np.float32)
    sinpat = np.zeros((128, T), np.float32)
    for p in range(128):
        j = p % half
        cosrep[p] = cosv[:, j]
        sinpat[p] = (-sinv[:, j]) if (p % D) < half else sinv[:, j]
    trimask = np.zeros((128, 128), np.float32)
    for kt in range(128):
        trimask[kt, kt:] = 1.0
    ident = np.eye(128, dtype=np.float32)
    onesf = np.ones((128, 128), np.float32)
    ones2 = np.zeros((128, 2), np.float32)
    ones2[0:64, 0] = 1.0
    ones2[64:128, 1] = 1.0
    sel01 = np.zeros((2, 256), np.float32)
    sel01[0, 0:128] = 1.0
    sel01[1, 128:256] = 1.0
    smalls = np.zeros((128, 8), np.float32)
    smalls[:, 0] = 127.0
    smalls[:, 1] = 1e-5
    smalls[:, 2] = EPS_RMS
    smalls[:, 3] = 1.0
    smalls[:, 5] = EPS_LN
    return dict(cosrep=cosrep, sinpat=sinpat,
                trimask=np.ascontiguousarray(trimask.astype(BF)),
                ident=ident, onesf=onesf, ones2=ones2, sel01=sel01, smalls=smalls)


def _host_inputs(L0_attn_x, L0_attn_w_qkv, L0_attn_w_out, L0_attn_q_norm_w,
                 L0_attn_k_norm_w, L0_ffn_w1, L0_ffn_conv_w, L0_ffn_conv_b,
                 L0_ffn_w2):
    consts = _host_consts()
    wqkv = np.asarray(L0_attn_w_qkv)
    wqkT = np.ascontiguousarray(wqkv[0:2 * C].T)
    wvT = np.ascontiguousarray(wqkv[2 * C:3 * C].T)
    woutT = np.ascontiguousarray(np.asarray(L0_attn_w_out).T)
    w1T = np.ascontiguousarray(np.asarray(L0_ffn_w1).T)
    w2T = np.ascontiguousarray(np.asarray(L0_ffn_w2).T)
    qn = np.asarray(L0_attn_q_norm_w).astype(np.float32, copy=False)
    kn = np.asarray(L0_attn_k_norm_w).astype(np.float32, copy=False)
    qn_stat = np.zeros((2, 128), np.float32)
    qn_stat[0, 0:64] = qn
    qn_stat[1, 64:128] = qn
    kn_stat = np.zeros((2, 128), np.float32)
    kn_stat[0, 0:64] = kn
    kn_stat[1, 64:128] = kn
    cw = np.asarray(L0_ffn_conv_w).reshape(HID, 3)
    taps = np.zeros((128, 96), np.float32)
    convb = np.zeros((128, 32), np.float32)
    cb = np.asarray(L0_ffn_conv_b)
    for c in range(NHC):
        taps[:, c] = cw[128 * c:128 * (c + 1), 0]
        taps[:, 32 + c] = cw[128 * c:128 * (c + 1), 1]
        taps[:, 64 + c] = cw[128 * c:128 * (c + 1), 2]
        convb[:, c] = cb[128 * c:128 * (c + 1)]
    base = dict(wqkT=wqkT, wvT=wvT, woutT=woutT, w1T=w1T, w2T=w2T,
                qn_stat=qn_stat, kn_stat=kn_stat, taps=taps, convb=convb, **consts)
    xs = np.asarray(L0_attn_x)
    return [dict(base, x=np.ascontiguousarray(xs[b], dtype=np.float32))
            for b in range(8)]


def build_nc():
    nc = bass.Bass("TRN2", target_bir_lowering=False)

    dx = nc.dram_tensor("x", [T, C], F32, kind="ExternalInput").ap()
    dwqkT = nc.dram_tensor("wqkT", [C, 2 * C], F32, kind="ExternalInput").ap()
    dwvT = nc.dram_tensor("wvT", [C, C], F32, kind="ExternalInput").ap()
    dwoutT = nc.dram_tensor("woutT", [C, C], F32, kind="ExternalInput").ap()
    dw1T = nc.dram_tensor("w1T", [C, HID], F32, kind="ExternalInput").ap()
    dw2T = nc.dram_tensor("w2T", [HID, C], F32, kind="ExternalInput").ap()
    dcos = nc.dram_tensor("cosrep", [128, T], F32, kind="ExternalInput").ap()
    dsin = nc.dram_tensor("sinpat", [128, T], F32, kind="ExternalInput").ap()
    dtri = nc.dram_tensor("trimask", [128, 128], BF16, kind="ExternalInput").ap()
    dident = nc.dram_tensor("ident", [128, 128], F32, kind="ExternalInput").ap()
    donesf = nc.dram_tensor("onesf", [128, 128], F32, kind="ExternalInput").ap()
    dones2 = nc.dram_tensor("ones2", [128, 2], F32, kind="ExternalInput").ap()
    dsel01 = nc.dram_tensor("sel01", [2, 256], F32, kind="ExternalInput").ap()
    dsmalls = nc.dram_tensor("smalls", [128, 8], F32, kind="ExternalInput").ap()
    dqn = nc.dram_tensor("qn_stat", [2, 128], F32, kind="ExternalInput").ap()
    dkn = nc.dram_tensor("kn_stat", [2, 128], F32, kind="ExternalInput").ap()
    dtaps = nc.dram_tensor("taps", [128, 96], F32, kind="ExternalInput").ap()
    dconvb = nc.dram_tensor("convb", [128, 32], F32, kind="ExternalInput").ap()

    dxqs = nc.dram_tensor("xqs", [T, C], BF16, kind="Internal").ap()
    daccs = nc.dram_tensor("accs", [T, C], F32, kind="Internal").ap()
    dw1q = nc.dram_tensor("w1q", [C, HID], BF16, kind="Internal").ap()
    ds2 = nc.dram_tensor("ds2", [16, 256], F32, kind="Internal").ap()
    dout = nc.dram_tensor("out", [T, C], F32, kind="ExternalOutput").ap()

    dbg = {}
    if DEBUG_TAPS:
        shapes = {"xnf": ([128, NCC * T], BF16), "qhat": ([128, NCC * T], BF16),
                  "khat": ([128, NCC * T], BF16), "vt": ([128, NTC * H * 65], BF16),
                  "attnout": ([128, NCC * T], BF16), "biasM": ([128, 16], F32),
                  "wsc": ([128, 4], F32), "y0": ([128, NHC * FT], F32),
                  "xq2t0": ([128, NHC * FT], BF16), "s2": ([128, 16], F32),
                  "xqvf": ([128, NCC * 2052], BF16),
                  "abuf1": ([128, 1024], F32), "ob1": ([128, 1024], F32)}
        for name in DEBUG_TAPS:
            shp, dt = shapes[name]
            dbg[name] = nc.dram_tensor("dbg_" + name, shp, dt,
                                       kind="ExternalOutput").ap()

    em = Em()
    E, A = em.emit, em.at
    need, free, prod, cons = em.need, em.free, em.prod, em.cons

    st = ExitStack()
    with st:
        AB = st.enter_context(nc.sbuf_tensor("AB", [128, 70400], BF16))
        AF_ = st.enter_context(nc.sbuf_tensor("AFA", [128, 16064], F32))
        i32b = st.enter_context(nc.sbuf_tensor("i32b", [128, 512], I32))

        R_A, R_B, R_C, R_D, R_E = 0, 16384, 32768, 49152, 65792

        def ab(off, n):
            return AB[:, off:off + n]

        F_COS, F_SIN = 0, 2048
        F_S1, F_S2, F_S4 = 4096, 6144, 8192
        F_MSB = 10240
        F_WFS = 12288
        S_STATS = 13312
        S_S1COL = 13376
        S_R1COL = 13392
        S_MAXQ = 13408
        S_MAXK = 13424
        S_BIASM = 13440
        S_WSUMS = 13456
        S_WSC = 13590
        S_S2SC = 13594
        S_FSC = 13610
        S_TMX = 13626
        S_S2ROW = 13632
        F_WABS = 13632
        F_WRS1 = 14848
        F_WRS2 = 15360
        F_YT = 0
        F_CONV0 = 8192
        F_CONV1 = 8448
        F_FOLD = 8704
        F_XNC3 = 8960
        F_OUTC = 11008

        def af(off, n):
            return AF_[:, off:off + n]

        cos_t = af(F_COS, 2048)
        sin_t = af(F_SIN, 2048)
        tri_t = st.enter_context(nc.sbuf_tensor("tri_t", [128, 128], BF16))
        id_t = st.enter_context(nc.sbuf_tensor("id_t", [128, 128], F32))
        onesf_t = st.enter_context(nc.sbuf_tensor("onesf_t", [128, 128], F32))
        ones2_t = st.enter_context(nc.sbuf_tensor("ones2_t", [128, 2], F32))
        sel_t = st.enter_context(nc.sbuf_tensor("sel_t", [2, 256], F32))
        sm_t = st.enter_context(nc.sbuf_tensor("sm_t", [128, 8], F32))
        qn_t = st.enter_context(nc.sbuf_tensor("qn_t", [2, 128], F32))
        kn_t = st.enter_context(nc.sbuf_tensor("kn_t", [2, 128], F32))
        taps_t = st.enter_context(nc.sbuf_tensor("taps_t", [128, 96], F32))
        convb_t = st.enter_context(nc.sbuf_tensor("convb_t", [128, 32], F32))

        PSQ = st.enter_context(nc.psum_tensor("psq", [128, 2048], F32))
        PS4 = st.enter_context(nc.psum_tensor("ps4", [128, 512], F32))
        PS5 = st.enter_context(nc.psum_tensor("ps5", [128, 512], F32))
        PSB = st.enter_context(nc.psum_tensor("psb", [128, 1024], F32))
        PS = [PSQ[:, 512 * i:512 * (i + 1)] for i in range(4)] + [PS4[:], PS5[:]]

        def psq(i, p0, p1, c0, c1):
            return PSQ[p0:p1, 512 * i + c0:512 * i + c1]

        sems = {n: st.enter_context(nc.semaphore(n))
                for n in ("sld", "gld", "tsp", "ct", "cv", "cs", "cg",
                          "ts0", "ts1", "ts2", "xb0", "xb1", "ab0", "ab1",
                          "ws0", "ws1", "sb0", "sb1", "sb2", "sb3",
                          "wo", "wq", "dac", "sg")}

        def dma(eng, out, in_, waits=(), sem=None):
            sem = sem or ("gld" if eng == "gpsimd" else "sld")
            E(eng, lambda e, o=out, i=in_: e.dma_start(out=o, in_=i), waits, sem, 16)
            return sem

        def tap(name, src_ap):
            if name not in DEBUG_TAPS:
                return
            dma("sync", dbg[name], src_ap,
                waits=[("cv", A("cv")), ("cs", A("cs")), ("ct", A("ct")),
                       ("gld", A("gld")), ("tsp", A("tsp"))])
            bar = A("sld")
            for eng, s in (("tensor", "ct"), ("vector", "cv"), ("scalar", "cs")):
                E(eng, lambda e: e.nop(), [("sld", bar)], s, 1)

        for o, i in [(cos_t, dcos), (sin_t, dsin), (tri_t[:], dtri), (id_t[:], dident),
                     (onesf_t[:], donesf), (ones2_t[:], dones2), (sel_t[:], dsel01),
                     (sm_t[:], dsmalls), (qn_t[:], dqn), (kn_t[:], dkn),
                     (taps_t[:], dtaps), (convb_t[:], dconvb)]:
            dma("sync", o, i)
        consts_ld = A("sld")
        for eng in ("tensor", "vector", "scalar"):
            E(eng, lambda e: e.nop(), [("sld", consts_ld)],
              {"tensor": "ct", "vector": "cv", "scalar": "cs"}[eng], 1)
        for c in range(NCC):
            dma("gpsimd", ab(R_B + 1024 * c, 1024), dwvT[128 * c:128 * (c + 1), :])
        wv_ld = A("gld")

        xnf = ab(R_A, 16384)
        for i in range(NTC):
            par = i % 2
            xbuf = af(F_S1 + 1024 * par, 1024)
            nbuf = af(F_S4 + 1024 * par, 1024)
            sqs = af(F_MSB, 1024)
            nb16 = ab(R_B + 8192 + 1024 * par, 1024)
            qb16 = ab(R_B + 10240 + 1024 * par, 1024)
            dma("sync", xbuf, dx[128 * i:128 * (i + 1), :],
                waits=free(("xbuf", par)))
            prod(("xbuf", par), "sld")
            E("vector", lambda e, xb=xbuf: e.reduce_sum(
                out=af(S_STATS + 0, 1), in_=xb, axis=AX.X),
              need(("xbuf", par)), "cv", 1)
            E("scalar", lambda e, xb=xbuf, sqs=sqs: e.activation(
                out=sqs, in_=xb, func=AF.Square, accum_out=af(S_STATS + 1, 1)),
              need(("xbuf", par)), "cs", 1)
            cons(("xbuf", par), "cs")
            sq_ev = ("cs", A("cs"))
            E("vector", lambda e: e.tensor_scalar(
                out=af(S_STATS + 2, 1), in0=af(S_STATS + 0, 1), scalar1=1.0 / C,
                scalar2=None, op0=ALU.mult), [("cv", A("cv"))], "cv", 1)
            E("vector", lambda e: e.scalar_tensor_tensor(
                out=af(S_STATS + 3, 1), in0=af(S_STATS + 2, 1), scalar=-1.0,
                in1=af(S_STATS + 2, 1), op0=ALU.mult, op1=ALU.mult),
              [("cv", A("cv"))], "cv", 1)
            E("vector", lambda e: e.scalar_tensor_tensor(
                out=af(S_STATS + 3, 1), in0=af(S_STATS + 1, 1), scalar=1.0 / C,
                in1=af(S_STATS + 3, 1), op0=ALU.mult, op1=ALU.add),
              [("cv", A("cv")), sq_ev], "cv", 1)
            E("scalar", lambda e: e.activation(
                out=af(S_STATS + 3, 1), in_=af(S_STATS + 3, 1), func=AF.Sqrt,
                bias=sm_t[:, 5:6]), [("cv", A("cv"))], "cs", 1)
            E("vector", lambda e: e.reciprocal(
                out=af(S_STATS + 3, 1), in_=af(S_STATS + 3, 1)),
              [("cs", A("cs"))], "cv", 1)
            E("vector", lambda e: e.scalar_tensor_tensor(
                out=af(S_STATS + 2, 1), in0=af(S_STATS + 2, 1), scalar=-1.0,
                in1=af(S_STATS + 3, 1), op0=ALU.mult, op1=ALU.mult),
              [("cv", A("cv"))], "cv", 1)
            E("vector", lambda e, xb=xbuf, nb=nbuf: e.tensor_scalar(
                out=nb, in0=xb, scalar1=af(S_STATS + 3, 1),
                scalar2=af(S_STATS + 2, 1), op0=ALU.mult, op1=ALU.add),
              [("cv", A("cv"))] + free(("nbuf", par)), "cv", 1)
            cons(("xbuf", par), "cv")
            prod(("nbuf", par), "cv")
            E("vector", lambda e, nb=nbuf, o=nb16: e.tensor_copy(out=o, in_=nb),
              free(("nb16", par)), "cv", 1)
            cons(("nbuf", par), "cv")
            prod(("nb16", par), "cv")
            E("sync", lambda e, i=i, src=nb16: e.dma_start_transpose(
                out=xnf.rearrange("p (c t) -> p c t", c=NCC)
                    [:, :, 128 * i:128 * (i + 1)],
                in_=src), need(("nb16", par)) + [("tsp", A("tsp"))], "tsp", 16)
            cons(("nb16", par), "tsp")
            E("vector", lambda e, nb=nbuf, i=i: e.reduce_max(
                out=af(S_S1COL + i, 1), in_=nb, axis=AX.X, apply_absolute_value=True),
              [], "cv", 1)
            E("vector", lambda e, i=i: e.tensor_scalar(
                out=af(S_R1COL + i, 1), in0=af(S_S1COL + i, 1), scalar1=1e-5,
                scalar2=1.0 / 127.0, op0=ALU.max, op1=ALU.mult),
              [("cv", A("cv"))], "cv", 1)
            E("vector", lambda e, i=i: e.reciprocal(
                out=af(S_S1COL + i, 1), in_=af(S_R1COL + i, 1)),
              [("cv", A("cv"))], "cv", 1)
            for hh in (0, 1):
                E("vector", lambda e, nb=nbuf, i=i, hh=hh: e.tensor_scalar(
                    out=i32b[:], in0=nb[:, 512 * hh:512 * (hh + 1)],
                    scalar1=af(S_S1COL + i, 1), scalar2=None,
                    op0=ALU.mult), [("cv", A("cv"))], "cv", 1)
                E("vector", lambda e, i=i, o=qb16, hh=hh: e.tensor_scalar(
                    out=o[:, 512 * hh:512 * (hh + 1)], in0=i32b[:],
                    scalar1=af(S_R1COL + i, 1), scalar2=None,
                    op0=ALU.mult),
                  [("cv", A("cv"))] + (free(("qb16", par)) if hh == 0 else []),
                  "cv", 1)
            cons(("nbuf", par), "cv")
            prod(("qb16", par), "cv")
            dma("gpsimd", dxqs[128 * i:128 * (i + 1), :], qb16,
                waits=need(("qb16", par)))
            cons(("qb16", par), "gld")
        p0_tsp = A("tsp")
        p0_gld = A("gld")
        tap("xnf", ab(R_A, 16384))

        wjobs = []
        for c in range(NCC):
            for b4 in range(8):
                wjobs.append(dw1T[128 * c:128 * (c + 1), 512 * b4:512 * (b4 + 1)])
        for c in range(NHC):
            for b2 in range(2):
                wjobs.append(dw2T[128 * c:128 * (c + 1), 512 * b2:512 * (b2 + 1)])
        WBUFS = [F_WABS, F_WRS1, F_WRS2]
        wj_state = {"next": 0}

        def emit_wjobs(n):
            for _ in range(n):
                j = wj_state["next"]
                if j >= len(wjobs) + 3:
                    return
                r = j - 3
                if 0 <= r < len(wjobs):
                    E("vector", lambda e, b=af(WBUFS[r % 3], 512), r=r:
                        e.reduce_sum(out=af(S_WSUMS + r, 1), in_=b, axis=AX.X,
                                     apply_absolute_value=True),
                      need(("wbuf", r % 3)), "cv", 1)
                    cons(("wbuf", r % 3), "cv")
                if j < len(wjobs):
                    dma("gpsimd", af(WBUFS[j % 3], 512), wjobs[j],
                        waits=free(("wbuf", j % 3)))
                    prod(("wbuf", j % 3), "gld")
                wj_state["next"] = j + 1

        vt = ab(R_D, 16640)
        E("vector", lambda e: e.memset(
            vt.rearrange("p (j h o) -> p j h o", j=NTC, h=H)[:, :, :, 64:65], 1.0),
          [], "cv", 1)
        vones_cv = A("cv")
        for j in range(NTC):
            for c in range(NCC):
                for sp in range(2):
                    w0 = ([("gld", wv_ld), ("tsp", p0_tsp)] + free(("psb",))
                          if (c, sp) == (0, 0) else [])
                    E("tensor", lambda e, j=j, c=c, sp=sp: e.matmul(
                        PSB[:, 512 * sp:512 * (sp + 1)],
                        xnf[:, T * c + 128 * j:T * c + 128 * (j + 1)],
                        ab(R_B + 1024 * c + 512 * sp, 512),
                        start=(c == 0), stop=(c == NCC - 1)), w0, "ct", 1)
            prod(("psb",), "ct")
            E("vector", lambda e, j=j: e.tensor_copy(
                out=ab(R_D + 1040 * j, 1040).rearrange(
                    "p (h o) -> p h o", o=65)[:, :, 0:64],
                in_=PSB[:].rearrange("p (h o) -> p h o", h=H)),
              need(("psb",)), "cv", 1)
            cons(("psb",), "cv")
        p1v_cv = A("cv")
        tap("vt", ab(R_D, 16640))

        qhat = ab(R_B, 16384)
        khat = ab(R_C, 16384)
        s1 = af(F_S1, 2048)
        s2b = af(F_S2, 2048)
        s4 = af(F_S4, 2048)

        P1 = [dict() for _ in range(16)]

        def p1_qkv(oc):
            sec = ab(R_E + 1024 * (oc % 2), 1024)
            dma("gpsimd", sec.rearrange("p (c n) -> p c n", c=NCC),
                dwqkT[:, 128 * oc:128 * (oc + 1)].rearrange("(c p) n -> p c n",
                                                            c=NCC),
                waits=free(("sec", oc % 2)))
            prod(("sec", oc % 2), "gld")
            for c in range(NCC):
                for sp in range(4):
                    w0 = (need(("sec", oc % 2)) + [("cv", p1v_cv)]
                          + free(("qk",))
                          if (c, sp) == (0, 0) else [])
                    E("tensor", lambda e, c=c, sp=sp, sec=sec: e.matmul(
                        PS[sp], sec[:, 128 * c:128 * (c + 1)],
                        xnf[:, T * c + 512 * sp:T * c + 512 * (sp + 1)],
                        start=(c == 0), stop=(c == NCC - 1)), w0, "ct", 1)
            prod(("qk",), "ct")
            cons(("sec", oc % 2), "ct")

        def p1_nonpe(oc):
            isq = oc < 8
            ch = oc % 8
            dst = qhat if isq else khat
            mxoff = S_MAXQ if isq else S_MAXK
            st_ = P1[oc]
            st_["c_ms"] = [Cell() for _ in range(4)]
            st_["c_wrs"] = [Cell() for _ in range(4)]
            st_["c_nsq"] = [Cell() for _ in range(4)]
            E("scalar", lambda e: e.copy(out=s1, in_=PSQ[:]),
              need(("qk",)) + free(("s1",)), "cs", 1)
            cons(("qk",), "cs")
            prod(("s1",), "cs")
            zfree = (P1[oc - 1]["c_nsq"][3:] if oc >= 1 else []) + free(("s4",))
            E("scalar", lambda e: e.activation(out=s4, in_=PSQ[:], func=AF.Square),
              need(("qk",)) + zfree, "cs", 1)
            cons(("qk",), "cs")
            prod(("s4",), "cs")
            st_["sq_ev"] = ("cs", A("cs"))
            for (a, b) in ((0, 32), (32, 0), (64, 96), (96, 64)):
                dma("sync", AF_[a:a + 32, F_S2:F_S2 + 2048],
                    AF_[b:b + 32, F_S1:F_S1 + 2048],
                    waits=need(("s1",)) + free(("s2b",)) +
                    ([("tsp", p0_tsp)] if oc == 0 else []))
            cons(("s1",), "sld")
            prod(("s2b",), "sld")
            E("gpsimd", lambda e: e.tensor_mul(out=s2b, in0=s2b, in1=sin_t),
              need(("s2b",)) + ([("sld", consts_ld)] if oc == 0 else []), "cg", 1)
            prod(("s2b",), "cg")
            for sp in range(4):
                msfree = (free(("msb",)) +
                          ([P1[oc - 1]["c_wrs"][3]] if oc >= 1 else [])
                          if sp == 0 else [])
                E("vector", lambda e, sp=sp: e.tensor_copy(
                    out=AF_[0:2, F_MSB + 512 * sp:F_MSB + 512 * (sp + 1)],
                    in_=PS4[0:2, :]),
                  [st_["c_ms"][sp]] + msfree, "cv", 1)
                st_["msevac_ev_%d" % sp] = ("cv", A("cv"))
            prod(("msb",), "cv")
            st_["msevac3"] = ("cv", A("cv"))
            E("scalar", lambda e: e.activation(
                out=AF_[0:2, F_MSB:F_MSB + 2048], in_=AF_[0:2, F_MSB:F_MSB + 2048],
                func=AF.Ln, scale=1.0 / D, bias=sm_t[0:2, 2:3]),
              need(("msb",)), "cs", 1)
            E("scalar", lambda e: e.activation(
                out=AF_[0:2, F_MSB:F_MSB + 2048], in_=AF_[0:2, F_MSB:F_MSB + 2048],
                func=AF.Exp, scale=-0.5), [], "cs", 1)
            prod(("msb",), "cs")
            st_["rs_ev"] = ("cs", A("cs"))
            E("vector", lambda e: e.tensor_mul(out=s4, in0=s1, in1=cos_t),
              need(("s1",)) + [st_["c_ms"][3]] + free(("s4",)), "cv", 1)
            cons(("s1",), "cv")
            prod(("s4",), "cv")
            E("vector", lambda e: e.tensor_add(out=s2b, in0=s4, in1=s2b),
              need(("s2b",), ("s4",)), "cv", 1)
            cons(("s4",), "cv")
            prod(("s2b",), "cv")
            for sp in range(4):
                E("vector", lambda e, sp=sp, dst=dst, ch=ch: e.tensor_mul(
                    out=dst[:, T * ch + 512 * sp:T * ch + 512 * (sp + 1)],
                    in0=s2b[:, 512 * sp:512 * (sp + 1)],
                    in1=PSB[:, 512 * (sp % 2):512 * (sp % 2) + 512]),
                  [st_["c_wrs"][sp]] + need(("s2b",)), "cv", 1)
                st_["tt_ev_%d" % sp] = ("cv", A("cv"))
                cons(("s2b",), "cv")
            st_["qh_ev"] = ("cv", A("cv"))
            E("scalar", lambda e, dst=dst, ch=ch: e.activation(
                out=s4, in_=dst[:, T * ch:T * (ch + 1)], func=AF.Square),
              [st_["qh_ev"]] + free(("s4",)), "cs", 1)
            prod(("s4",), "cs")
            st_["nsqsq_ev"] = ("cs", A("cs"))
            for sp in range(4):
                E("vector", lambda e, sp=sp: e.reduce_max(
                    out=AF_[0:2, S_TMX + sp:S_TMX + sp + 1],
                    in_=PS5[0:2, :], axis=AX.X), [st_["c_nsq"][sp]], "cv", 1)
                st_["nsqred_ev_%d" % sp] = ("cv", A("cv"))
            E("vector", lambda e, mxoff=mxoff, ch=ch: e.reduce_max(
                out=AF_[0:2, mxoff + ch:mxoff + ch + 1],
                in_=AF_[0:2, S_TMX:S_TMX + 4], axis=AX.X),
              [("cv", A("cv"))], "cv", 1)
            st_["done_cv"] = ("cv", A("cv"))

        def p1_pesmalls(oc):
            isq = oc < 8
            nstat = qn_t if isq else kn_t
            st_ = P1[oc]
            pst = P1[oc - 1] if oc >= 1 else None
            for sp in range(4):
                w0 = [st_["sq_ev"]]
                if sp > 0:
                    w0.append(st_["msevac_ev_%d" % (sp - 1)])
                elif pst is not None:
                    w0.append(pst["msevac_ev_3"])
                E("tensor", lambda e, sp=sp: e.matmul(
                    PS4[0:2, :], ones2_t[:], s4[:, 512 * sp:512 * (sp + 1)],
                    start=True, stop=True), w0, "ct", 1)
                em.fill(st_["c_ms"][sp], "ct")
            for sp in range(4):
                w0 = [st_["rs_ev"]]
                if sp >= 2:
                    w0.append(st_["tt_ev_%d" % (sp - 2)])
                elif pst is not None:
                    w0.append(pst["tt_ev_%d" % (sp + 2)])
                else:
                    w0 += free(("psb",))
                E("tensor", lambda e, sp=sp, nstat=nstat: e.matmul(
                    PSB[:, 512 * (sp % 2):512 * (sp % 2) + 512], nstat[:],
                    AF_[0:2, F_MSB + 512 * sp:F_MSB + 512 * (sp + 1)],
                    start=True, stop=True), w0, "ct", 1)
                em.fill(st_["c_wrs"][sp], "ct")
            for sp in range(4):
                w0 = [st_["nsqsq_ev"]]
                if sp > 0:
                    w0.append(st_["nsqred_ev_%d" % (sp - 1)])
                elif pst is not None:
                    w0.append(pst["nsqred_ev_3"])
                E("tensor", lambda e, sp=sp: e.matmul(
                    PS5[0:2, :], ones2_t[:], s4[:, 512 * sp:512 * (sp + 1)],
                    start=True, stop=True), w0, "ct", 1)
                em.fill(st_["c_nsq"][sp], "ct")

        for oc in range(16):
            p1_qkv(oc)
            if oc >= 1:
                p1_pesmalls(oc - 1)
            p1_nonpe(oc)
        p1_pesmalls(15)
        p1a_cv = A("cv")


        E("scalar", lambda e: e.activation(
            out=AF_[0:2, S_MAXQ:S_MAXQ + 8], in_=AF_[0:2, S_MAXQ:S_MAXQ + 8],
            func=AF.Sqrt), [("cv", p1a_cv)], "cs", 1)
        E("scalar", lambda e: e.activation(
            out=AF_[0:2, S_MAXK:S_MAXK + 8], in_=AF_[0:2, S_MAXK:S_MAXK + 8],
            func=AF.Sqrt), [], "cs", 1)
        E("vector", lambda e: e.tensor_mul(
            out=AF_[0:2, S_MAXQ:S_MAXQ + 8], in0=AF_[0:2, S_MAXQ:S_MAXQ + 8],
            in1=AF_[0:2, S_MAXK:S_MAXK + 8]), [("cs", A("cs"))], "cv", 1)
        E("vector", lambda e: e.tensor_scalar(
            out=AF_[0:2, S_MAXQ:S_MAXQ + 8], in0=AF_[0:2, S_MAXQ:S_MAXQ + 8],
            scalar1=-0.125, scalar2=None, op0=ALU.mult),
          [("cv", A("cv"))], "cv", 1)
        negM_cv = A("cv")
        for h in range(H):
            r, c = h % 2, h // 2
            E("tensor", lambda e, r=r, c=c, h=h: e.matmul(
                PS4[:, h:h + 1], sel_t[:, 128 * r:128 * (r + 1)],
                AF_[0:2, S_MAXQ + c:S_MAXQ + c + 1], start=True, stop=True),
              [("cv", negM_cv)] + (free(("ps4",)) if h == 0 else []), "ct", 1)
        prod(("ps4",), "ct")
        E("vector", lambda e: e.tensor_copy(
            out=af(S_BIASM, 16), in_=PS4[:, 0:16]), need(("ps4",)), "cv", 1)
        cons(("ps4",), "cv")
        biasM_cv = A("cv")
        tap("qhat", ab(R_B, 16384))
        tap("khat", ab(R_C, 16384))

        pt_off = []
        off = 0
        for j in range(8):
            pt_off.append((R_A, off))
            off += T - 128 * j
        off = 0
        for j in range(8, 16):
            pt_off.append((R_E, off))
            off += T - 128 * j

        def pt_ap(j, col0, width):
            base, o = pt_off[j]
            return ab(base + o + col0, width)

        for pair in range(8):
            for parx in range(2):
                h = 2 * pair + parx
                base = 64 * parx
                ptev = {}
                for j in range(NTC):
                    qt0 = 128 * j
                    nsp = (T - qt0 + 511) // 512
                    for spi in range(nsp):
                        c0 = qt0 + 512 * spi
                        w = min(512, T - c0)
                        bnk = ("sb", (j + spi) % 4)
                        bank = PS[(j + spi) % 4]
                        w0 = free(bnk) + ([("cv", biasM_cv)] if h == 0 else [])
                        E("tensor", lambda e, j=j, c0=c0, w=w, bank=bank, base=base,
                            pair=pair: e.matmul(
                            bank[:, 0:w],
                            khat[base:base + 64,
                                 T * pair + 128 * j:T * pair + 128 * (j + 1)],
                            qhat[base:base + 64, T * pair + c0:T * pair + c0 + w],
                            start=True, stop=True), w0, "ct", 1)
                        prod(bnk, "ct")
                        E("scalar", lambda e, j=j, c0=c0, w=w, bank=bank, qt0=qt0,
                            h=h: e.activation(
                            out=pt_ap(j, c0 - qt0, w), in_=bank[:, 0:w],
                            func=AF.Exp, scale=0.125, bias=af(S_BIASM + h, 1)),
                          need(bnk) +
                          (free(("pt", j)) + [("cv", biasM_cv)] if spi == 0 else []),
                          "cs", 1)
                        cons(bnk, "cs")
                    exp_ev = ("cs", A("cs"))
                    E("vector", lambda e, j=j: e.tensor_mul(
                        out=pt_ap(j, 0, 128), in0=pt_ap(j, 0, 128), in1=tri_t[:]),
                      [exp_ev], "cv", 1)
                    ptev[j] = [exp_ev, ("cv", A("cv"))]
                    prod(("pt", j), "cv")
                for s in range(4):
                    avb = ("avb", s % 2)
                    avbank = PS[4 + (s % 2)]
                    njs = 4 * s + 4
                    for j in range(njs):
                        coff = 512 * s - 128 * j
                        c0 = max(0, coff)
                        pcol = max(0, -coff)
                        w = 512 - pcol
                        w0 = list(ptev[j]) + (free(avb) if j == 0 else [])
                        E("tensor", lambda e, j=j, c0=c0, pcol=pcol, w=w,
                            avbank=avbank, h=h, njs=njs: e.matmul(
                            avbank[0:65, pcol:pcol + w],
                            vt[:, 1040 * j + 65 * h:1040 * j + 65 * h + 65],
                            pt_ap(j, c0, w),
                            start=(j == 0), stop=(j == njs - 1)),
                          w0 + ([("cv", vones_cv)] if h == 0 and s == 0 else []),
                          "ct", 1)
                        cons(("pt", j), "ct")
                    prod(avb, "ct")
                    E("scalar", lambda e, s=s, parx=parx, avbank=avbank: e.copy(
                        out=AF_[0:65, F_S1 + 2048 * parx + 512 * s:
                                F_S1 + 2048 * parx + 512 * (s + 1)],
                        in_=avbank[0:65, :]),
                      need(avb) + (free(("ascr", parx)) if s == 0 else []), "cs", 1)
                    cons(avb, "cs")
                prod(("ascr", parx), "cs")
            for parx in range(2):
                h = 2 * pair + parx
                E("scalar", lambda e, parx=parx: e.activation(
                    out=AF_[64:65, F_MSB:F_MSB + 2048],
                    in_=AF_[64:65, F_S1 + 2048 * parx:F_S1 + 2048 * parx + 2048],
                    func=AF.Ln),
                  need(("ascr", parx)) + free(("msb",)), "cs", 1)
                E("scalar", lambda e: e.activation(
                    out=AF_[64:65, F_MSB:F_MSB + 2048],
                    in_=AF_[64:65, F_MSB:F_MSB + 2048], func=AF.Exp, scale=-1.0),
                  [], "cs", 1)
                prod(("msb",), "cs")
                for s in range(4):
                    bnk = ("nb", s % 2)
                    bank = PSB[0:64, 512 * (s % 2):512 * (s % 2) + 512]
                    E("tensor", lambda e, s=s, bank=bank: e.matmul(
                        bank, onesf_t[64:65, 0:64],
                        AF_[64:65, F_MSB + 512 * s:F_MSB + 512 * (s + 1)],
                        start=True, stop=True),
                      need(("msb",)) + free(bnk), "ct", 1)
                    prod(bnk, "ct")
                    E("scalar", lambda e, s=s, bank=bank: e.copy(
                        out=AF_[0:64, F_MSB + 512 * s:F_MSB + 512 * (s + 1)],
                        in_=bank),
                      need(bnk) + (free(("rb",)) if s == 0 else []), "cs", 1)
                    cons(bnk, "cs")
                prod(("rb",), "cs")
                if parx == 0:
                    E("vector", lambda e, pair=pair: e.tensor_mul(
                        out=AB[0:64, R_B + T * pair:R_B + T * (pair + 1)],
                        in0=AF_[0:64, F_S1:F_S1 + 2048],
                        in1=AF_[0:64, F_MSB:F_MSB + 2048]),
                      need(("rb",), ("ascr", 0)), "cv", 1)
                    cons(("rb",), "cv")
                    cons(("ascr", 0), "cv")
                else:
                    E("vector", lambda e: e.tensor_mul(
                        out=AB[0:64, R_A + 12800:R_A + 12800 + 2048],
                        in0=AF_[0:64, F_S1 + 2048:F_S1 + 4096],
                        in1=AF_[0:64, F_MSB:F_MSB + 2048]),
                      need(("rb",), ("ascr", 1)) + free(("oscr",)), "cv", 1)
                    cons(("rb",), "cv")
                    cons(("ascr", 1), "cv")
                    prod(("oscr",), "cv")
                    E("sync", lambda e, pair=pair: e.dma_start(
                        out=AB[64:128, R_B + T * pair:R_B + T * (pair + 1)],
                        in_=AB[0:64, R_A + 12800:R_A + 12800 + 2048]),
                      need(("oscr",)), "sld", 16)
                    cons(("oscr",), "sld")
            emit_wjobs(17)
        emit_wjobs(200)
        tap("attnout", ab(R_B, 16384))
        attn_cv = A("cv")
        attn_cs = A("cs")
        attn_sld = A("sld")
        attn_ct = A("ct")
        attnout = qhat

        XQW = 2052
        xqvf = ab(R_D, NCC * XQW)
        E("vector", lambda e: e.memset(
            xqvf.rearrange("p (c t) -> p c t", c=NCC)[:, :, 0:1], 0.0),
          [("cv", attn_cv), ("cs", attn_cs), ("ct", attn_ct), ("sld", attn_sld)],
          "cv", 1)
        E("vector", lambda e: e.memset(
            xqvf.rearrange("p (c t) -> p c t", c=NCC)[:, :, 2049:2052], 0.0),
          [], "cv", 1)
        guards_cv = A("cv")
        for c in range(NCC):
            dma("gpsimd", ab(R_C + 1024 * c, 1024), dwoutT[128 * c:128 * (c + 1), :],
                waits=[("ct", attn_ct), ("cs", attn_cs)], sem="wo")
        wout_ld = A("wo")

        E("vector", lambda e: e.reduce_sum(
            out=af(S_WSUMS + 128, 1), in_=af(S_WSUMS, 64), axis=AX.X),
          [("cv", A("cv"))], "cv", 1)
        E("vector", lambda e: e.reduce_sum(
            out=af(S_WSUMS + 129, 1), in_=af(S_WSUMS + 64, 64), axis=AX.X),
          [("cv", A("cv"))], "cv", 1)
        wred_cv = A("cv")
        for k in range(2):
            E("tensor", lambda e, k=k: e.matmul(
                PS4[0:1, k:k + 1], onesf_t[:, 0:1], af(S_WSUMS + 128 + k, 1),
                start=True, stop=True),
              [("cv", wred_cv), ("cs", attn_cs)] +
              (free(("ps4",), ("ps45",)) if k == 0 else []), "ct", 1)
        prod(("ps4",), "ct")
        E("vector", lambda e: e.tensor_copy(
            out=AF_[0:1, S_WSUMS + 130:S_WSUMS + 132], in_=PS4[0:1, 0:2]),
          need(("ps4",)), "cv", 1)
        cons(("ps4",), "cv")
        E("vector", lambda e: e.tensor_scalar(
            out=AF_[0:1, S_WSUMS + 130:S_WSUMS + 132],
            in0=AF_[0:1, S_WSUMS + 130:S_WSUMS + 132],
            scalar1=1.0 / (C * HID), scalar2=1e-5, op0=ALU.mult, op1=ALU.max),
          [("cv", A("cv"))], "cv", 1)
        E("vector", lambda e: e.reciprocal(
            out=AF_[0:1, S_WSUMS + 132:S_WSUMS + 134],
            in_=AF_[0:1, S_WSUMS + 130:S_WSUMS + 132]),
          [("cv", A("cv"))], "cv", 1)
        mw_cv = A("cv")
        E("tensor", lambda e: e.matmul(
            PS4[:, 0:4], onesf_t[0:1, 0:128],
            AF_[0:1, S_WSUMS + 130:S_WSUMS + 134], start=True, stop=True),
          [("cv", mw_cv)] + free(("ps4",)), "ct", 1)
        prod(("ps4",), "ct")
        E("vector", lambda e: e.tensor_copy(out=af(S_WSC, 4), in_=PS4[:, 0:4]),
          need(("ps4",)), "cv", 1)
        cons(("ps4",), "cv")
        wsc_cv = A("cv")


        def p25_tp(i):
            spar = i % 3
            src = ab(R_E + 1024 * spar, 1024)
            dma("sync", src, dxqs[128 * i:128 * (i + 1), :],
                waits=free(("tsrc", spar)) +
                ([("gld", p0_gld), ("sld", attn_sld), ("ct", attn_ct),
                  ("cs", attn_cs), ("cv", attn_cv)] if i == 0 else []),
                sem="ts%d" % spar)
            prod(("tsrc", spar), "ts%d" % spar)
            w32 = af(F_S4 + 1024 * (i % 2), 1024)
            E("scalar", lambda e, src=src, w32=w32: e.activation(
                out=w32, in_=src, func=AF.Copy),
              need(("tsrc", spar)) + free(("w32", i % 2)), "cs", 1)
            cons(("tsrc", spar), "cs")
            prod(("w32", i % 2), "cs")
            for c in range(NCC):
                tb = ("tpb", c % 2)
                w0 = (need(("w32", i % 2)) if c == 0 else []) + free(tb) + \
                    ([("cs", attn_cs), ("ct", attn_ct)] if i == 0 and c < 2 else [])
                E("tensor", lambda e, c=c, w32=w32: e.transpose(
                    psq(c % 2, 0, 128, 0, 128), w32[:, 128 * c:128 * (c + 1)],
                    id_t[:]), w0, "ct", 1)
                prod(tb, "ct")
                if c == NCC - 1:
                    cons(("w32", i % 2), "ct")
                dst = xqvf[:, XQW * c + 1 + 128 * i:XQW * c + 1 + 128 * (i + 1)]
                if c % 2 == 0:
                    E("scalar", lambda e, c=c, dst=dst: e.activation(
                        out=dst, in_=psq(c % 2, 0, 128, 0, 128), func=AF.Copy),
                      need(tb) + ([("cv", guards_cv)] if i == 0 else []), "cs", 1)
                    cons(tb, "cs")
                else:
                    E("vector", lambda e, c=c, dst=dst: e.tensor_copy(
                        out=dst, in_=psq(c % 2, 0, 128, 0, 128)),
                      need(tb) + ([("cv", guards_cv)] if i == 0 else []), "cv", 1)
                    cons(tb, "cv")

        def p25_proj(i):
            par = i % 2
            xbuf = af(F_MSB + 1024 * par, 1024)
            obuf = af(F_S2 + 1024 * par, 1024)
            dma("sync", xbuf, dx[128 * i:128 * (i + 1), :],
                waits=free(("xre", par)) +
                ([("cv", attn_cv), ("cs", attn_cs)] if i < 2 else []),
                sem="xb%d" % par)
            prod(("xre", par), "xb%d" % par)
            pjr = ("psb",) if par == 0 else (("ps4",), ("ps45",))
            for c in range(NCC):
                for sp in range(2):
                    w0 = ([("wo", wout_ld), ("sld", attn_sld),
                           ("cv", attn_cv)] +
                          (free(("psb",)) if par == 0
                           else free(("ps4",), ("ps45",)))
                          if (c, sp) == (0, 0) else [])

                    def pjmm(e, i=i, c=c, sp=sp, par=par):
                        outap = (PSB[:, 512 * sp:512 * (sp + 1)] if par == 0
                                 else (PS4[:] if sp == 0 else PS5[:]))
                        return e.matmul(
                            outap,
                            attnout[:, T * c + 128 * i:T * c + 128 * (i + 1)],
                            ab(R_C + 1024 * c + 512 * sp, 512),
                            start=(c == 0), stop=(c == NCC - 1))
                    E("tensor", pjmm, w0, "ct", 1)
            if par == 0:
                prod(("psb",), "ct")
            else:
                prod(("ps4",), "ct")
                prod(("ps45",), "ct")

            def pjfin(e, xbuf=xbuf, obuf=obuf, par=par):
                if par == 0:
                    return e.scalar_tensor_tensor(
                        out=obuf, in0=PSB[:], scalar=1.0, in1=xbuf,
                        op0=ALU.mult, op1=ALU.add)
                e.scalar_tensor_tensor(
                    out=obuf[:, 0:512], in0=PS4[:], scalar=1.0,
                    in1=xbuf[:, 0:512], op0=ALU.mult, op1=ALU.add)
                return e.scalar_tensor_tensor(
                    out=obuf[:, 512:1024], in0=PS5[:], scalar=1.0,
                    in1=xbuf[:, 512:1024], op0=ALU.mult, op1=ALU.add)
            pw = (need(("psb",)) if par == 0 else need(("ps4",), ("ps45",))) + \
                need(("xre", par)) + free(("obuf", par))
            E("vector", pjfin, pw, "cv", 1)
            if par == 0:
                cons(("psb",), "cv")
            else:
                cons(("ps4",), "cv")
                cons(("ps45",), "cv")
            cons(("xre", par), "cv")
            prod(("obuf", par), "cv")
            dma("sync", daccs[128 * i:128 * (i + 1), :], obuf,
                waits=need(("obuf", par)), sem="dac")
            cons(("obuf", par), "dac")

        wsign2 = [(R_A + 1024 * hc) if hc < 16 else (R_B + 1024 * (hc - 16))
                  for hc in range(NHC)]
        w1q_g = {}
        fcnt = [0]
        cg_hw = A("cg")

        STRB = (14016, 14528, 15040, 15552)

        def item_w2(hc, h):
            k5 = fcnt[0] % 4
            first = fcnt[0] < 4
            fcnt[0] += 1
            buf = af(STRB[k5], 512)
            dma("gpsimd", buf, dw2T[128 * hc:128 * (hc + 1), 512 * h:512 * (h + 1)],
                waits=free(("fcos", k5)) +
                ([("cv", attn_cv), ("cs", attn_cs), ("sld", attn_sld),
                  ("cg", cg_hw)] if first else []), sem="sb%d" % k5)
            prod(("fcos", k5), "sb%d" % k5)
            E("vector", lambda e, buf=buf: e.tensor_scalar(
                out=i32b[:], in0=buf, scalar1=AF_[:, S_WSC + 3:S_WSC + 4],
                scalar2=None, op0=ALU.mult),
              need(("fcos", k5)) + ([("cv", wsc_cv)] if hc == 0 and h == 0
                                    else []), "cv", 1)
            cons(("fcos", k5), "cv")
            E("vector", lambda e, hc=hc, h=h: e.tensor_scalar(
                out=ab(wsign2[hc] + 512 * h, 512), in0=i32b[:],
                scalar1=-1.0, scalar2=1.0, op0=ALU.max, op1=ALU.min),
              [("sld", attn_sld), ("ct", attn_ct)] if hc == 0 and h == 0 else
              ([("ct", proj_ct), ("sld", proj_sld)] if hc == 16 and h == 0
               else []), "cv", 1)

        def item_w1(jj, h):
            b4, c = divmod(jj, 8)
            k5 = fcnt[0] % 4
            first = fcnt[0] < 4
            fcnt[0] += 1
            buf = af(STRB[k5], 512)
            dma("gpsimd", buf,
                dw1T[128 * c:128 * (c + 1),
                     1024 * b4 + 512 * h:1024 * b4 + 512 * (h + 1)],
                waits=free(("fcos", k5)) +
                ([("cv", attn_cv), ("cs", attn_cs), ("sld", attn_sld),
                  ("cg", cg_hw)] if first else []), sem="sb%d" % k5)
            prod(("fcos", k5), "sb%d" % k5)
            E("vector", lambda e, buf=buf: e.tensor_scalar(
                out=i32b[:], in0=buf, scalar1=AF_[:, S_WSC + 2:S_WSC + 3],
                scalar2=None, op0=ALU.mult), need(("fcos", k5)), "cv", 1)
            cons(("fcos", k5), "cv")
            bk = (2 * jj + h) % 8
            bout = ab(R_C + 8192 + 512 * bk, 512)
            E("vector", lambda e, bout=bout: e.tensor_scalar(
                out=bout, in0=i32b[:], scalar1=-1.0, scalar2=1.0,
                op0=ALU.max, op1=ALU.min),
              free(("bout", bk)) + ([("ct", attn_ct)] if 2 * jj + h < 8
                                    else []), "cv", 1)
            prod(("bout", bk), "cv")
            dma("sync",
                dw1q[128 * c:128 * (c + 1),
                     1024 * b4 + 512 * h:1024 * b4 + 512 * (h + 1)],
                bout, waits=need(("bout", bk)), sem="wq")
            cons(("bout", bk), "wq")
            if c == 7 and h == 1:
                w1q_g[b4] = A("wq")

        items = [(lambda hc=hc, h=h: item_w2(hc, h))
                 for hc in range(16) for h in range(2)] + \
                [(lambda jj=jj, h=h: item_w1(jj, h))
                 for jj in range(32) for h in range(2)]
        xqvf_cv = xqvf_cs = None
        for i in range(NTC):
            p25_tp(i)
            if i == NTC - 1:
                xqvf_cs = A("cs")
                xqvf_cv = A("cv")
            if i >= 2:
                p25_proj(i - 2)
            for fn in items[6 * i:6 * i + 6]:
                fn()
        tap("xqvf", xqvf)
        p25_proj(NTC - 2)
        p25_proj(NTC - 1)
        proj_sld = A("sld")
        proj_dac = A("dac")
        proj_ct = A("ct")
        for hc in range(16, 32):
            for h in range(2):
                item_w2(hc, h)
        w2q_cv = A("cv")

        yt2 = af(F_YT, 8192)

        def xq2b(ti):
            return ab(R_C + 8192 * (ti % 2), 8192)
        pend_silu = []
        pend_T = []
        round_cells = [Cell() for _ in range(NFT)]
        yt2ev = {}

        def flush_T():
            while pend_T:
                qti, tw, cT = pend_T.pop(0)
                for k in range(2):
                    E("tensor", lambda e, k=k: e.transpose(
                        psq(2, 0, 128, 128 * k, 128 * (k + 1)),
                        af(F_FOLD + 128 * k, 128), id_t[:]),
                      (need(("fold",)) + tw) if k == 0 else [], "ct", 1)
                    em.fill(cT[k], "ct")
                cons(("fold",), "ct")

        def flush_silu():
            while pend_silu:
                fn, waits, key = pend_silu.pop(0)
                E("scalar", fn, waits, "cs", 1)
                yt2ev[key] = ("cs", A("cs"))

        def p3_h(ti):
            a = FT * ti
            hspan = FT + 2
            for hc in range(NHC):
                par = hc % 2
                p4 = hc % 4
                hbr = ("hb", p4)
                hb = PS[p4]
                sec = ab(R_E + 1024 * par, 1024)
                dma("sync", sec.rearrange("p (c n) -> p c n", c=NCC),
                    dw1q[:, 128 * hc:128 * (hc + 1)].rearrange(
                        "(c p) n -> p c n", c=NCC),
                    waits=[("wq", w1q_g[hc // 8]), ("cv", xqvf_cv),
                           ("cs", xqvf_cs)] + free(("w1sec", par)),
                    sem="ws%d" % par)
                prod(("w1sec", par), "ws%d" % par)
                for c in range(NCC):
                    w0 = (need(("w1sec", par)) + free(hbr) +
                          ([("cs", xqvf_cs), ("cv", xqvf_cv), ("ct", attn_ct)]
                           if ti == 0 and hc < 4 else [])
                          if c == 0 else [])
                    E("tensor", lambda e, a=a, hspan=hspan, c=c, hb=hb, sec=sec:
                        e.matmul(
                        hb[:, 0:hspan], sec[:, 128 * c:128 * (c + 1)],
                        xqvf[:, XQW * c + a:XQW * c + a + hspan],
                        start=(c == 0), stop=(c == NCC - 1)), w0, "ct", 1)
                prod(hbr, "ct")
                cons(("w1sec", par), "ct")
                csc = af(F_CONV0 + 256 * par, FT)
                cscr = ("csc", par)
                E("scalar", lambda e, hb=hb, hc=hc, csc=csc: e.activation(
                    out=csc, in_=hb[:, 2:2 + FT], func=AF.Copy,
                    scale=taps_t[:, 64 + hc:65 + hc]),
                  need(hbr) + free(cscr), "cs", 1)
                cons(hbr, "cs")
                prod(cscr, "cs")
                E("vector", lambda e, hb=hb, hc=hc, csc=csc: e.scalar_tensor_tensor(
                    out=csc, in0=hb[:, 0:FT], scalar=taps_t[:, hc:hc + 1],
                    in1=csc, op0=ALU.mult, op1=ALU.add),
                  need(cscr, hbr), "cv", 1)
                prod(cscr, "cv")
                E("vector", lambda e, hb=hb, hc=hc, csc=csc: e.scalar_tensor_tensor(
                    out=csc, in0=hb[:, 1:1 + FT], scalar=taps_t[:, 32 + hc:33 + hc],
                    in1=csc, op0=ALU.mult, op1=ALU.add),
                  need(cscr), "cv", 1)
                cons(hbr, "cv")
                prod(cscr, "cv")
                silu_waits = (need(cscr) + [("cv", wsc_cv)] +
                              ([round_cells[ti - 1]] if ti >= 1 else []))
                pend_silu.append((
                    (lambda e, hc=hc, csc=csc: e.activation(
                        out=yt2[:, FT * hc:FT * (hc + 1)], in_=csc, func=AF.Silu,
                        scale=AF_[:, S_WSC:S_WSC + 1], bias=convb_t[:, hc:hc + 1])),
                    silu_waits, ti))
                cons(cscr, "cs")
                if len(pend_silu) > 1:
                    fn, waits, key = pend_silu.pop(0)
                    E("scalar", fn, waits, "cs", 1)
                    yt2ev[key] = ("cs", A("cs"))
                if hc == 1:
                    flush_T()

        def p3_quant(ti):
            flush_silu()
            xq2 = xq2b(ti)
            sev = yt2ev[ti]
            E("vector", lambda e: e.tensor_reduce(
                out=af(F_FOLD, FT), in_=yt2.rearrange("p (c t) -> p t c", c=NHC),
                axis=AX.X, op=ALU.max, apply_absolute_value=True),
              [sev] + free(("fold",)), "cv", 1)
            prod(("fold",), "cv")
            cT = [Cell(), Cell()]
            pend_T.append((ti, free(("hb", 2)), cT))
            for k in range(2):
                gcol = 2 * ti + k
                E("vector", lambda e, k=k: e.reduce_max(
                    out=af(S_TMX + k, 1),
                    in_=psq(2, 0, 128, 128 * k, 128 * (k + 1)), axis=AX.X),
                  [cT[k]], "cv", 1)
                cons(("hb", 2), "cv")
                E("vector", lambda e, k=k, gcol=gcol: e.tensor_scalar(
                    out=af(S_FSC + gcol, 1), in0=af(S_TMX + k, 1), scalar1=1e-5,
                    scalar2=1.0 / 127.0, op0=ALU.max, op1=ALU.mult),
                  [("cv", A("cv"))], "cv", 1)
                E("vector", lambda e, gcol=gcol: e.reciprocal(
                    out=af(S_S2SC + gcol, 1), in_=af(S_FSC + gcol, 1)),
                  [("cv", A("cv"))], "cv", 1)
                E("vector", lambda e, gcol=gcol: e.tensor_scalar(
                    out=af(S_FSC + gcol, 1), in0=af(S_FSC + gcol, 1),
                    scalar1=AF_[:, S_WSC + 1:S_WSC + 2], scalar2=None,
                    op0=ALU.mult), [("cv", A("cv"))], "cv", 1)
            s2_cv = A("cv")
            for k in range(2):
                E("gpsimd", lambda e, k=k, ti=ti: e.dma_start(
                    out=ds2[ti, 128 * k:128 * (k + 1)],
                    in_=AF_[:, S_S2SC + 2 * ti + k:S_S2SC + 2 * ti + k + 1]),
                  ([("cv", s2_cv)] + free(("s2bc",))) if k == 0 else [],
                  "sg", 16)
            E("gpsimd", lambda e, ti=ti: e.dma_start(
                out=AF_[:, S_S2ROW:S_S2ROW + FT],
                in_=ds2[ti:ti + 1, :].broadcast_to([128, FT])),
              [("sg", A("sg"))], "sg", 16)
            prod(("s2bc",), "sg")
            for hc in range(NHC):
                E("vector", lambda e, hc=hc: e.tensor_tensor(
                    out=i32b[:, 0:FT], in0=yt2[:, FT * hc:FT * (hc + 1)],
                    in1=AF_[:, S_S2ROW:S_S2ROW + FT], op=ALU.mult),
                  (need(("s2bc",)) if hc == 0 else []) + [sev], "cv", 1)
                E("vector", lambda e, hc=hc, xq2=xq2: e.tensor_copy(
                    out=xq2[:, FT * hc:FT * (hc + 1)], in_=i32b[:, 0:FT]),
                  [("cv", A("cv"))] + free(("xq2", ti % 2, hc)), "cv", 1)
                prod(("xq2", ti % 2, hc), "cv")
            cons(("s2bc",), "cv")
            em.fill(round_cells[ti], "cv")

        def p3_w2(ti):
            a = FT * ti
            xq2 = xq2b(ti)
            for hc in range(NHC):
                for sp in range(2):
                    w0 = (need(("xq2", ti % 2, hc)) +
                          ([("cv", w2q_cv)] if ti == 0 and hc == 0 else []) +
                          (free(("w2a",), ("psb",)) if hc == 0 and sp == 0
                           else []))
                    E("tensor", lambda e, hc=hc, sp=sp: e.matmul(
                        PSB[:, 512 * sp:512 * (sp + 1)],
                        xq2[:, FT * hc:FT * hc + 128],
                        ab(wsign2[hc] + 512 * sp, 512),
                        start=(hc == 0), stop=(hc == NHC - 1)), w0, "ct", 1)
                    w1_ = (free(("w2b",), ("ps4",), ("ps45",))
                           if hc == 0 and sp == 0 else [])
                    E("tensor", lambda e, hc=hc, sp=sp: e.matmul(
                        PS4[:] if sp == 0 else PS5[:],
                        xq2[:, FT * hc + 128:FT * hc + 256],
                        ab(wsign2[hc] + 512 * sp, 512),
                        start=(hc == 0), stop=(hc == NHC - 1)), w1_, "ct", 1)
                cons(("xq2", ti % 2, hc), "ct")
            prod(("w2a",), "ct")
            prod(("w2b",), "ct")
            for k in range(2):
                gcol = 2 * ti + k
                t0 = a + 128 * k
                abuf = af(F_XNC3 + 1024 * k, 1024)
                ob = af(F_OUTC + 1024 * k, 1024)
                dma("sync", abuf, daccs[t0:t0 + 128, :],
                    waits=free(("abuf", k)) + [("dac", proj_dac)],
                    sem="ab%d" % k)
                prod(("abuf", k), "ab%d" % k)
                wres = ("w2a",) if k == 0 else ("w2b",)
                for half in range(2):
                    def fin(e, k=k, gcol=gcol, abuf=abuf, ob=ob, half=half):
                        src = (PSB[:, 512 * half:512 * (half + 1)] if k == 0
                               else (PS4[:] if half == 0 else PS5[:]))
                        return e.scalar_tensor_tensor(
                            out=ob[:, 512 * half:512 * (half + 1)], in0=src,
                            scalar=af(S_FSC + gcol, 1),
                            in1=abuf[:, 512 * half:512 * (half + 1)],
                            op0=ALU.mult, op1=ALU.add)
                    E("vector", fin,
                      need(wres, ("abuf", k)) + free(("outc", k)), "cv", 1)
                cons(wres, "cv")
                cons(("abuf", k), "cv")
                prod(("outc", k), "cv")
                dma("sync", dout[t0:t0 + 128, :], ob, waits=need(("outc", k)))
                cons(("outc", k), "sld")

        for ti in range(NFT):
            p3_h(ti)
            if ti >= 1:
                p3_w2(ti - 1)
            p3_quant(ti)
        flush_T()
        p3_w2(NFT - 1)
        if DEBUG_TAPS:
            srcs = {"biasM": af(S_BIASM, 16),
                    "wsc": af(S_WSC, 4), "y0": yt2, "xq2t0": xq2b(NFT - 1),
                    "s2": af(S_S2SC, 16),
                    "abuf1": af(F_XNC3 + 1024, 1024),
                    "ob1": af(F_OUTC + 1024, 1024)}
            for name in DEBUG_TAPS:
                if name in srcs:
                    dma("sync", dbg[name], srcs[name],
                        waits=[("cv", A("cv")), ("cs", A("cs")), ("ct", A("ct")),
                               ("gld", A("gld"))])
        E("sync", lambda e: e.nop(),
          [(s, A(s)) for s in ("sld", "gld", "tsp", "ts0", "ts1", "ts2",
                               "xb0", "xb1", "ab0", "ab1", "ws0", "ws1",
                               "sb0", "sb1", "sb2", "sb3", "wo", "wq",
                               "dac", "sg")], None)

        block = st.enter_context(nc.Block())
        em.finalize(block, sems)
    return nc


def kernel(**inputs):
    in_maps = _host_inputs(**inputs)
    nc = build_nc()
    res = run_bass_kernel_spmd(nc, in_maps, core_ids=list(range(8)))
    out = np.stack([np.asarray(r["out"]) for r in res.results], axis=0)
    return out.astype(np.float32, copy=False)


# version 09
# speedup vs baseline: 1.0274x; 1.0274x over previous
import numpy as np
import ml_dtypes
from contextlib import ExitStack

import concourse.bass as bass
import concourse.mybir as mybir
from concourse.bass_utils import run_bass_kernel_spmd

F32 = mybir.dt.float32
BF16 = mybir.dt.bfloat16
I32 = mybir.dt.int32
AF = mybir.ActivationFunctionType
ALU = mybir.AluOpType
AX = mybir.AxisListType
BF = ml_dtypes.bfloat16

T = 2048
C = 1024
H = 16
D = 64
HID = 4096
NTC = T // 128
NCC = C // 128
NHC = HID // 128
EPS_LN = 1e-5
EPS_RMS = 1e-6
FT = 256
NFT = T // FT

DEBUG_TAPS = []


class Cell:
    __slots__ = ("sem", "v")

    def __init__(self):
        self.sem = None
        self.v = None


class Em:

    def __init__(self):
        self.q = {e: [] for e in ("tensor", "vector", "scalar", "gpsimd", "sync")}
        self.cnt = {}
        self.resp = {}
        self.resc = {}

    def fill(self, cell, sem):
        cell.sem = sem
        cell.v = self.at(sem)

    def at(self, sem):
        return self.cnt.get(sem, 0)

    def emit(self, eng, fn, waits=(), inc=None, incn=1):
        self.q[eng].append((fn, tuple(waits), inc, incn))
        if inc is not None:
            self.cnt[inc] = self.cnt.get(inc, 0) + incn

    def prod(self, res, sem):
        self.resp[res] = (sem, self.at(sem))
        self.resc[res] = {}

    def cons(self, res, sem):
        d = self.resc.setdefault(res, {})
        d[sem] = self.at(sem)

    def need(self, *ress):
        out = []
        for r in ress:
            p = self.resp.get(r)
            if p:
                out.append(p)
        return out

    def free(self, *ress):
        out = []
        for r in ress:
            out.extend(self.resc.get(r, {}).items())
        return out

    def finalize(self, block, sems):
        def mk(name):
            ops = self.q[name]

            def run(e):
                hw = {}
                for fn, waits, inc, incn in ops:
                    merged = {}
                    for w in waits:
                        if isinstance(w, Cell):
                            assert w.v is not None, "unfilled Cell wait"
                            s, v = w.sem, w.v
                        else:
                            s, v = w
                        if v > 0 and v > merged.get(s, 0):
                            merged[s] = v
                    for s, v in merged.items():
                        if v > hw.get(s, 0):
                            e.wait_ge(sems[s], v)
                            hw[s] = v
                    inst = fn(e)
                    if inc is not None:
                        inst.then_inc(sems[inc], incn)
            return run
        block.tensor(mk("tensor"))
        block.vector(mk("vector"))
        block.scalar(mk("scalar"))
        block.gpsimd(mk("gpsimd"))
        block.sync(mk("sync"))


def _host_consts():
    half = D // 2
    inv_freq = 1.0 / (10000.0 ** (np.arange(0, half, dtype=np.float64) * 2.0 / D))
    ang = np.arange(T, dtype=np.float64)[:, None] * inv_freq[None, :]
    cosv = np.cos(ang).astype(np.float32)
    sinv = np.sin(ang).astype(np.float32)
    cosrep = np.zeros((128, T), np.float32)
    sinpat = np.zeros((128, T), np.float32)
    for p in range(128):
        j = p % half
        cosrep[p] = cosv[:, j]
        sinpat[p] = (-sinv[:, j]) if (p % D) < half else sinv[:, j]
    trimask = np.zeros((128, 128), np.float32)
    for kt in range(128):
        trimask[kt, kt:] = 1.0
    ident = np.eye(128, dtype=np.float32)
    onesf = np.ones((128, 128), np.float32)
    ones2 = np.zeros((128, 2), np.float32)
    ones2[0:64, 0] = 1.0
    ones2[64:128, 1] = 1.0
    sel01 = np.zeros((2, 256), np.float32)
    sel01[0, 0:128] = 1.0
    sel01[1, 128:256] = 1.0
    smalls = np.zeros((128, 8), np.float32)
    smalls[:, 0] = 127.0
    smalls[:, 1] = 1e-5
    smalls[:, 2] = EPS_RMS
    smalls[:, 3] = 1.0
    smalls[:, 5] = EPS_LN
    return dict(cosrep=cosrep, sinpat=sinpat,
                trimask=np.ascontiguousarray(trimask.astype(BF)),
                ident=ident, onesf=onesf, ones2=ones2, sel01=sel01, smalls=smalls)


def _host_inputs(L0_attn_x, L0_attn_w_qkv, L0_attn_w_out, L0_attn_q_norm_w,
                 L0_attn_k_norm_w, L0_ffn_w1, L0_ffn_conv_w, L0_ffn_conv_b,
                 L0_ffn_w2):
    consts = _host_consts()
    wqkv = np.asarray(L0_attn_w_qkv)
    wqkT = np.ascontiguousarray(wqkv[0:2 * C].T)
    wvT = np.ascontiguousarray(wqkv[2 * C:3 * C].T)
    woutT = np.ascontiguousarray(np.asarray(L0_attn_w_out).T)
    w1T = np.ascontiguousarray(np.asarray(L0_ffn_w1).T)
    w2T = np.ascontiguousarray(np.asarray(L0_ffn_w2).T)
    qn = np.asarray(L0_attn_q_norm_w).astype(np.float32, copy=False)
    kn = np.asarray(L0_attn_k_norm_w).astype(np.float32, copy=False)
    qn_stat = np.zeros((2, 128), np.float32)
    qn_stat[0, 0:64] = qn
    qn_stat[1, 64:128] = qn
    kn_stat = np.zeros((2, 128), np.float32)
    kn_stat[0, 0:64] = kn
    kn_stat[1, 64:128] = kn
    cw = np.asarray(L0_ffn_conv_w).reshape(HID, 3)
    taps = np.zeros((128, 96), np.float32)
    convb = np.zeros((128, 32), np.float32)
    cb = np.asarray(L0_ffn_conv_b)
    for c in range(NHC):
        taps[:, c] = cw[128 * c:128 * (c + 1), 0]
        taps[:, 32 + c] = cw[128 * c:128 * (c + 1), 1]
        taps[:, 64 + c] = cw[128 * c:128 * (c + 1), 2]
        convb[:, c] = cb[128 * c:128 * (c + 1)]
    base = dict(wqkT=wqkT, wvT=wvT, woutT=woutT, w1T=w1T, w2T=w2T,
                qn_stat=qn_stat, kn_stat=kn_stat, taps=taps, convb=convb, **consts)
    xs = np.asarray(L0_attn_x)
    return [dict(base, x=np.ascontiguousarray(xs[b], dtype=np.float32))
            for b in range(8)]


def build_nc():
    nc = bass.Bass("TRN2", target_bir_lowering=False)

    dx = nc.dram_tensor("x", [T, C], F32, kind="ExternalInput").ap()
    dwqkT = nc.dram_tensor("wqkT", [C, 2 * C], F32, kind="ExternalInput").ap()
    dwvT = nc.dram_tensor("wvT", [C, C], F32, kind="ExternalInput").ap()
    dwoutT = nc.dram_tensor("woutT", [C, C], F32, kind="ExternalInput").ap()
    dw1T = nc.dram_tensor("w1T", [C, HID], F32, kind="ExternalInput").ap()
    dw2T = nc.dram_tensor("w2T", [HID, C], F32, kind="ExternalInput").ap()
    dcos = nc.dram_tensor("cosrep", [128, T], F32, kind="ExternalInput").ap()
    dsin = nc.dram_tensor("sinpat", [128, T], F32, kind="ExternalInput").ap()
    dtri = nc.dram_tensor("trimask", [128, 128], BF16, kind="ExternalInput").ap()
    dident = nc.dram_tensor("ident", [128, 128], F32, kind="ExternalInput").ap()
    donesf = nc.dram_tensor("onesf", [128, 128], F32, kind="ExternalInput").ap()
    dones2 = nc.dram_tensor("ones2", [128, 2], F32, kind="ExternalInput").ap()
    dsel01 = nc.dram_tensor("sel01", [2, 256], F32, kind="ExternalInput").ap()
    dsmalls = nc.dram_tensor("smalls", [128, 8], F32, kind="ExternalInput").ap()
    dqn = nc.dram_tensor("qn_stat", [2, 128], F32, kind="ExternalInput").ap()
    dkn = nc.dram_tensor("kn_stat", [2, 128], F32, kind="ExternalInput").ap()
    dtaps = nc.dram_tensor("taps", [128, 96], F32, kind="ExternalInput").ap()
    dconvb = nc.dram_tensor("convb", [128, 32], F32, kind="ExternalInput").ap()

    dxqs = nc.dram_tensor("xqs", [T, C], BF16, kind="Internal").ap()
    daccs = nc.dram_tensor("accs", [T, C], F32, kind="Internal").ap()
    dw1q = nc.dram_tensor("w1q", [C, HID], BF16, kind="Internal").ap()
    ds2 = nc.dram_tensor("ds2", [16, 256], F32, kind="Internal").ap()
    dout = nc.dram_tensor("out", [T, C], F32, kind="ExternalOutput").ap()

    dbg = {}
    if DEBUG_TAPS:
        shapes = {"xnf": ([128, NCC * T], BF16), "qhat": ([128, NCC * T], BF16),
                  "khat": ([128, NCC * T], BF16), "vt": ([128, NTC * H * 65], BF16),
                  "attnout": ([128, NCC * T], BF16), "biasM": ([128, 16], F32),
                  "wsc": ([128, 4], F32), "y0": ([128, NHC * FT], F32),
                  "xq2t0": ([128, NHC * FT], BF16), "s2": ([128, 16], F32),
                  "xqvf": ([128, NCC * 2052], BF16),
                  "abuf1": ([128, 1024], F32), "ob1": ([128, 1024], F32)}
        for name in DEBUG_TAPS:
            shp, dt = shapes[name]
            dbg[name] = nc.dram_tensor("dbg_" + name, shp, dt,
                                       kind="ExternalOutput").ap()

    em = Em()
    E, A = em.emit, em.at
    need, free, prod, cons = em.need, em.free, em.prod, em.cons

    st = ExitStack()
    with st:
        AB = st.enter_context(nc.sbuf_tensor("AB", [128, 70400], BF16))
        AF_ = st.enter_context(nc.sbuf_tensor("AFA", [128, 16064], F32))
        i32b = st.enter_context(nc.sbuf_tensor("i32b", [128, 512], I32))

        R_A, R_B, R_C, R_D, R_E = 0, 16384, 32768, 49152, 65792

        def ab(off, n):
            return AB[:, off:off + n]

        F_COS, F_SIN = 0, 2048
        F_S1, F_S2, F_S4 = 4096, 6144, 8192
        F_MSB = 10240
        F_WFS = 12288
        S_STATS = 13312
        S_S1COL = 13376
        S_R1COL = 13392
        S_MAXQ = 13408
        S_MAXK = 13424
        S_BIASM = 13440
        S_WSUMS = 13456
        S_WSC = 13590
        S_S2SC = 13594
        S_FSC = 13610
        S_TMX = 13626
        S_S2ROW = 13632
        F_WABS = 13632
        F_WRS1 = 14848
        F_WRS2 = 15360
        F_YT = 0
        F_CONV0 = 8192
        F_CONV1 = 8448
        F_FOLD = 8704
        F_XNC3 = 8960
        F_OUTC = 11008

        def af(off, n):
            return AF_[:, off:off + n]

        cos_t = af(F_COS, 2048)
        sin_t = af(F_SIN, 2048)
        tri_t = st.enter_context(nc.sbuf_tensor("tri_t", [128, 128], BF16))
        id_t = st.enter_context(nc.sbuf_tensor("id_t", [128, 128], F32))
        onesf_t = st.enter_context(nc.sbuf_tensor("onesf_t", [128, 128], F32))
        ones2_t = st.enter_context(nc.sbuf_tensor("ones2_t", [128, 2], F32))
        sel_t = st.enter_context(nc.sbuf_tensor("sel_t", [2, 256], F32))
        sm_t = st.enter_context(nc.sbuf_tensor("sm_t", [128, 8], F32))
        qn_t = st.enter_context(nc.sbuf_tensor("qn_t", [2, 128], F32))
        kn_t = st.enter_context(nc.sbuf_tensor("kn_t", [2, 128], F32))
        taps_t = st.enter_context(nc.sbuf_tensor("taps_t", [128, 96], F32))
        convb_t = st.enter_context(nc.sbuf_tensor("convb_t", [128, 32], F32))

        PSQ = st.enter_context(nc.psum_tensor("psq", [128, 2048], F32))
        PS4 = st.enter_context(nc.psum_tensor("ps4", [128, 512], F32))
        PS5 = st.enter_context(nc.psum_tensor("ps5", [128, 512], F32))
        PSB = st.enter_context(nc.psum_tensor("psb", [128, 1024], F32))
        PS = [PSQ[:, 512 * i:512 * (i + 1)] for i in range(4)] + [PS4[:], PS5[:]]

        def psq(i, p0, p1, c0, c1):
            return PSQ[p0:p1, 512 * i + c0:512 * i + c1]

        sems = {n: st.enter_context(nc.semaphore(n))
                for n in ("sld", "gld", "tsp", "ct", "cv", "cs", "cg",
                          "ts0", "ts1", "ts2", "xb0", "xb1", "ab0", "ab1",
                          "ws0", "ws1", "sb0", "sb1", "sb2", "sb3",
                          "wo", "wq", "dac", "sg")}

        def dma(eng, out, in_, waits=(), sem=None):
            sem = sem or ("gld" if eng == "gpsimd" else "sld")
            E(eng, lambda e, o=out, i=in_: e.dma_start(out=o, in_=i), waits, sem, 16)
            return sem

        def tap(name, src_ap):
            if name not in DEBUG_TAPS:
                return
            dma("sync", dbg[name], src_ap,
                waits=[("cv", A("cv")), ("cs", A("cs")), ("ct", A("ct")),
                       ("gld", A("gld")), ("tsp", A("tsp"))])
            bar = A("sld")
            for eng, s in (("tensor", "ct"), ("vector", "cv"), ("scalar", "cs")):
                E(eng, lambda e: e.nop(), [("sld", bar)], s, 1)

        for o, i in [(cos_t, dcos), (sin_t, dsin), (tri_t[:], dtri), (id_t[:], dident),
                     (onesf_t[:], donesf), (ones2_t[:], dones2), (sel_t[:], dsel01),
                     (sm_t[:], dsmalls), (qn_t[:], dqn), (kn_t[:], dkn),
                     (taps_t[:], dtaps), (convb_t[:], dconvb)]:
            dma("sync", o, i)
        consts_ld = A("sld")
        for eng in ("tensor", "vector", "scalar"):
            E(eng, lambda e: e.nop(), [("sld", consts_ld)],
              {"tensor": "ct", "vector": "cv", "scalar": "cs"}[eng], 1)
        for c in range(NCC):
            dma("gpsimd", ab(R_B + 1024 * c, 1024), dwvT[128 * c:128 * (c + 1), :])
        wv_ld = A("gld")

        xnf = ab(R_A, 16384)
        for i in range(NTC):
            par = i % 2
            xbuf = af(F_S1 + 1024 * par, 1024)
            nbuf = af(F_S4 + 1024 * par, 1024)
            sqs = af(F_MSB, 1024)
            nb16 = ab(R_B + 8192 + 1024 * par, 1024)
            qb16 = ab(R_B + 10240 + 1024 * par, 1024)
            dma("sync", xbuf, dx[128 * i:128 * (i + 1), :],
                waits=free(("xbuf", par)))
            prod(("xbuf", par), "sld")
            E("vector", lambda e, xb=xbuf: e.reduce_sum(
                out=af(S_STATS + 0, 1), in_=xb, axis=AX.X),
              need(("xbuf", par)), "cv", 1)
            E("scalar", lambda e, xb=xbuf, sqs=sqs: e.activation(
                out=sqs, in_=xb, func=AF.Square, accum_out=af(S_STATS + 1, 1)),
              need(("xbuf", par)), "cs", 1)
            cons(("xbuf", par), "cs")
            sq_ev = ("cs", A("cs"))
            E("vector", lambda e: e.tensor_scalar(
                out=af(S_STATS + 2, 1), in0=af(S_STATS + 0, 1), scalar1=1.0 / C,
                scalar2=None, op0=ALU.mult), [("cv", A("cv"))], "cv", 1)
            E("vector", lambda e: e.scalar_tensor_tensor(
                out=af(S_STATS + 3, 1), in0=af(S_STATS + 2, 1), scalar=-1.0,
                in1=af(S_STATS + 2, 1), op0=ALU.mult, op1=ALU.mult),
              [("cv", A("cv"))], "cv", 1)
            E("vector", lambda e: e.scalar_tensor_tensor(
                out=af(S_STATS + 3, 1), in0=af(S_STATS + 1, 1), scalar=1.0 / C,
                in1=af(S_STATS + 3, 1), op0=ALU.mult, op1=ALU.add),
              [("cv", A("cv")), sq_ev], "cv", 1)
            E("scalar", lambda e: e.activation(
                out=af(S_STATS + 3, 1), in_=af(S_STATS + 3, 1), func=AF.Sqrt,
                bias=sm_t[:, 5:6]), [("cv", A("cv"))], "cs", 1)
            E("vector", lambda e: e.reciprocal(
                out=af(S_STATS + 3, 1), in_=af(S_STATS + 3, 1)),
              [("cs", A("cs"))], "cv", 1)
            E("vector", lambda e: e.scalar_tensor_tensor(
                out=af(S_STATS + 2, 1), in0=af(S_STATS + 2, 1), scalar=-1.0,
                in1=af(S_STATS + 3, 1), op0=ALU.mult, op1=ALU.mult),
              [("cv", A("cv"))], "cv", 1)
            E("vector", lambda e, xb=xbuf, nb=nbuf: e.tensor_scalar(
                out=nb, in0=xb, scalar1=af(S_STATS + 3, 1),
                scalar2=af(S_STATS + 2, 1), op0=ALU.mult, op1=ALU.add),
              [("cv", A("cv"))] + free(("nbuf", par)), "cv", 1)
            cons(("xbuf", par), "cv")
            prod(("nbuf", par), "cv")
            E("vector", lambda e, nb=nbuf, o=nb16: e.tensor_copy(out=o, in_=nb),
              free(("nb16", par)), "cv", 1)
            cons(("nbuf", par), "cv")
            prod(("nb16", par), "cv")
            E("sync", lambda e, i=i, src=nb16: e.dma_start_transpose(
                out=xnf.rearrange("p (c t) -> p c t", c=NCC)
                    [:, :, 128 * i:128 * (i + 1)],
                in_=src), need(("nb16", par)) + [("tsp", A("tsp"))], "tsp", 16)
            cons(("nb16", par), "tsp")
            E("vector", lambda e, nb=nbuf, i=i: e.reduce_max(
                out=af(S_S1COL + i, 1), in_=nb, axis=AX.X, apply_absolute_value=True),
              [], "cv", 1)
            E("vector", lambda e, i=i: e.tensor_scalar(
                out=af(S_R1COL + i, 1), in0=af(S_S1COL + i, 1), scalar1=1e-5,
                scalar2=1.0 / 127.0, op0=ALU.max, op1=ALU.mult),
              [("cv", A("cv"))], "cv", 1)
            E("vector", lambda e, i=i: e.reciprocal(
                out=af(S_S1COL + i, 1), in_=af(S_R1COL + i, 1)),
              [("cv", A("cv"))], "cv", 1)
            for hh in (0, 1):
                E("vector", lambda e, nb=nbuf, i=i, hh=hh: e.tensor_scalar(
                    out=i32b[:], in0=nb[:, 512 * hh:512 * (hh + 1)],
                    scalar1=af(S_S1COL + i, 1), scalar2=None,
                    op0=ALU.mult), [("cv", A("cv"))], "cv", 1)
                E("vector", lambda e, i=i, o=qb16, hh=hh: e.tensor_scalar(
                    out=o[:, 512 * hh:512 * (hh + 1)], in0=i32b[:],
                    scalar1=af(S_R1COL + i, 1), scalar2=None,
                    op0=ALU.mult),
                  [("cv", A("cv"))] + (free(("qb16", par)) if hh == 0 else []),
                  "cv", 1)
            cons(("nbuf", par), "cv")
            prod(("qb16", par), "cv")
            dma("gpsimd", dxqs[128 * i:128 * (i + 1), :], qb16,
                waits=need(("qb16", par)))
            cons(("qb16", par), "gld")
        p0_tsp = A("tsp")
        p0_gld = A("gld")
        tap("xnf", ab(R_A, 16384))

        wjobs = []
        for c in range(NCC):
            for b4 in range(8):
                wjobs.append(dw1T[128 * c:128 * (c + 1), 512 * b4:512 * (b4 + 1)])
        for c in range(NHC):
            for b2 in range(2):
                wjobs.append(dw2T[128 * c:128 * (c + 1), 512 * b2:512 * (b2 + 1)])
        WBUFS = [F_WABS, F_WRS1, F_WRS2]
        wj_state = {"next": 0}

        def emit_wjobs(n):
            for _ in range(n):
                j = wj_state["next"]
                if j >= len(wjobs) + 3:
                    return
                r = j - 3
                if 0 <= r < len(wjobs):
                    E("vector", lambda e, b=af(WBUFS[r % 3], 512), r=r:
                        e.reduce_sum(out=af(S_WSUMS + r, 1), in_=b, axis=AX.X,
                                     apply_absolute_value=True),
                      need(("wbuf", r % 3)), "cv", 1)
                    cons(("wbuf", r % 3), "cv")
                if j < len(wjobs):
                    dma("gpsimd", af(WBUFS[j % 3], 512), wjobs[j],
                        waits=free(("wbuf", j % 3)))
                    prod(("wbuf", j % 3), "gld")
                wj_state["next"] = j + 1

        vt = ab(R_D, 16640)
        E("vector", lambda e: e.memset(
            vt.rearrange("p (j h o) -> p j h o", j=NTC, h=H)[:, :, :, 64:65], 1.0),
          [], "cv", 1)
        vones_cv = A("cv")
        for j in range(NTC):
            for c in range(NCC):
                for sp in range(2):
                    w0 = ([("gld", wv_ld), ("tsp", p0_tsp)] + free(("psb",))
                          if (c, sp) == (0, 0) else [])
                    E("tensor", lambda e, j=j, c=c, sp=sp: e.matmul(
                        PSB[:, 512 * sp:512 * (sp + 1)],
                        xnf[:, T * c + 128 * j:T * c + 128 * (j + 1)],
                        ab(R_B + 1024 * c + 512 * sp, 512),
                        start=(c == 0), stop=(c == NCC - 1)), w0, "ct", 1)
            prod(("psb",), "ct")
            E("vector", lambda e, j=j: e.tensor_copy(
                out=ab(R_D + 1040 * j, 1040).rearrange(
                    "p (h o) -> p h o", o=65)[:, :, 0:64],
                in_=PSB[:].rearrange("p (h o) -> p h o", h=H)),
              need(("psb",)), "cv", 1)
            cons(("psb",), "cv")
        p1v_cv = A("cv")
        tap("vt", ab(R_D, 16640))

        qhat = ab(R_B, 16384)
        khat = ab(R_C, 16384)
        s1 = af(F_S1, 2048)
        s2b = af(F_S2, 2048)
        s4 = af(F_S4, 2048)

        P1 = [dict() for _ in range(16)]

        def p1_qkv(oc):
            sec = ab(R_E + 1024 * (oc % 2), 1024)
            dma("gpsimd", sec.rearrange("p (c n) -> p c n", c=NCC),
                dwqkT[:, 128 * oc:128 * (oc + 1)].rearrange("(c p) n -> p c n",
                                                            c=NCC),
                waits=free(("sec", oc % 2)))
            prod(("sec", oc % 2), "gld")
            for c in range(NCC):
                for sp in range(4):
                    w0 = (need(("sec", oc % 2)) + [("cv", p1v_cv)]
                          + free(("qk",))
                          if (c, sp) == (0, 0) else [])
                    E("tensor", lambda e, c=c, sp=sp, sec=sec: e.matmul(
                        PS[sp], sec[:, 128 * c:128 * (c + 1)],
                        xnf[:, T * c + 512 * sp:T * c + 512 * (sp + 1)],
                        start=(c == 0), stop=(c == NCC - 1)), w0, "ct", 1)
            prod(("qk",), "ct")
            cons(("sec", oc % 2), "ct")

        def p1_nonpe(oc):
            isq = oc < 8
            ch = oc % 8
            dst = qhat if isq else khat
            mxoff = S_MAXQ if isq else S_MAXK
            st_ = P1[oc]
            st_["c_ms"] = [Cell() for _ in range(4)]
            st_["c_wrs"] = [Cell() for _ in range(4)]
            st_["c_nsq"] = [Cell() for _ in range(4)]
            E("scalar", lambda e: e.copy(out=s1, in_=PSQ[:]),
              need(("qk",)) + free(("s1",)), "cs", 1)
            cons(("qk",), "cs")
            prod(("s1",), "cs")
            zfree = (P1[oc - 1]["c_nsq"][3:] if oc >= 1 else []) + free(("s4",))
            E("scalar", lambda e: e.activation(out=s4, in_=PSQ[:], func=AF.Square),
              need(("qk",)) + zfree, "cs", 1)
            cons(("qk",), "cs")
            prod(("s4",), "cs")
            st_["sq_ev"] = ("cs", A("cs"))
            for (a, b) in ((0, 32), (32, 0), (64, 96), (96, 64)):
                dma("sync", AF_[a:a + 32, F_S2:F_S2 + 2048],
                    AF_[b:b + 32, F_S1:F_S1 + 2048],
                    waits=need(("s1",)) + free(("s2b",)) +
                    ([("tsp", p0_tsp)] if oc == 0 else []))
            cons(("s1",), "sld")
            prod(("s2b",), "sld")
            E("gpsimd", lambda e: e.tensor_mul(out=s2b, in0=s2b, in1=sin_t),
              need(("s2b",)) + ([("sld", consts_ld)] if oc == 0 else []), "cg", 1)
            prod(("s2b",), "cg")
            for sp in range(4):
                msfree = (free(("msb",)) +
                          ([P1[oc - 1]["c_wrs"][3]] if oc >= 1 else [])
                          if sp == 0 else [])
                E("vector", lambda e, sp=sp: e.tensor_copy(
                    out=AF_[0:2, F_MSB + 512 * sp:F_MSB + 512 * (sp + 1)],
                    in_=PS4[0:2, :]),
                  [st_["c_ms"][sp]] + msfree, "cv", 1)
                st_["msevac_ev_%d" % sp] = ("cv", A("cv"))
            prod(("msb",), "cv")
            st_["msevac3"] = ("cv", A("cv"))
            E("scalar", lambda e: e.activation(
                out=AF_[0:2, F_MSB:F_MSB + 2048], in_=AF_[0:2, F_MSB:F_MSB + 2048],
                func=AF.Ln, scale=1.0 / D, bias=sm_t[0:2, 2:3]),
              need(("msb",)), "cs", 1)
            E("scalar", lambda e: e.activation(
                out=AF_[0:2, F_MSB:F_MSB + 2048], in_=AF_[0:2, F_MSB:F_MSB + 2048],
                func=AF.Exp, scale=-0.5), [], "cs", 1)
            prod(("msb",), "cs")
            st_["rs_ev"] = ("cs", A("cs"))
            E("vector", lambda e: e.tensor_mul(out=s4, in0=s1, in1=cos_t),
              need(("s1",)) + [st_["c_ms"][3]] + free(("s4",)), "cv", 1)
            cons(("s1",), "cv")
            prod(("s4",), "cv")
            E("vector", lambda e: e.tensor_add(out=s2b, in0=s4, in1=s2b),
              need(("s2b",), ("s4",)), "cv", 1)
            cons(("s4",), "cv")
            prod(("s2b",), "cv")
            for sp in range(4):
                E("vector", lambda e, sp=sp, dst=dst, ch=ch: e.tensor_mul(
                    out=dst[:, T * ch + 512 * sp:T * ch + 512 * (sp + 1)],
                    in0=s2b[:, 512 * sp:512 * (sp + 1)],
                    in1=PSB[:, 512 * (sp % 2):512 * (sp % 2) + 512]),
                  [st_["c_wrs"][sp]] + need(("s2b",)), "cv", 1)
                st_["tt_ev_%d" % sp] = ("cv", A("cv"))
                cons(("s2b",), "cv")
            st_["qh_ev"] = ("cv", A("cv"))
            E("scalar", lambda e, dst=dst, ch=ch: e.activation(
                out=s4, in_=dst[:, T * ch:T * (ch + 1)], func=AF.Square),
              [st_["qh_ev"]] + free(("s4",)), "cs", 1)
            prod(("s4",), "cs")
            st_["nsqsq_ev"] = ("cs", A("cs"))
            for sp in range(4):
                E("vector", lambda e, sp=sp: e.reduce_max(
                    out=AF_[0:2, S_TMX + sp:S_TMX + sp + 1],
                    in_=PS5[0:2, :], axis=AX.X), [st_["c_nsq"][sp]], "cv", 1)
                st_["nsqred_ev_%d" % sp] = ("cv", A("cv"))
            E("vector", lambda e, mxoff=mxoff, ch=ch: e.reduce_max(
                out=AF_[0:2, mxoff + ch:mxoff + ch + 1],
                in_=AF_[0:2, S_TMX:S_TMX + 4], axis=AX.X),
              [("cv", A("cv"))], "cv", 1)
            st_["done_cv"] = ("cv", A("cv"))

        def p1_pesmalls(oc):
            isq = oc < 8
            nstat = qn_t if isq else kn_t
            st_ = P1[oc]
            pst = P1[oc - 1] if oc >= 1 else None
            for sp in range(4):
                w0 = [st_["sq_ev"]]
                if sp > 0:
                    w0.append(st_["msevac_ev_%d" % (sp - 1)])
                elif pst is not None:
                    w0.append(pst["msevac_ev_3"])
                E("tensor", lambda e, sp=sp: e.matmul(
                    PS4[0:2, :], ones2_t[:], s4[:, 512 * sp:512 * (sp + 1)],
                    start=True, stop=True), w0, "ct", 1)
                em.fill(st_["c_ms"][sp], "ct")
            for sp in range(4):
                w0 = [st_["rs_ev"]]
                if sp >= 2:
                    w0.append(st_["tt_ev_%d" % (sp - 2)])
                elif pst is not None:
                    w0.append(pst["tt_ev_%d" % (sp + 2)])
                else:
                    w0 += free(("psb",))
                E("tensor", lambda e, sp=sp, nstat=nstat: e.matmul(
                    PSB[:, 512 * (sp % 2):512 * (sp % 2) + 512], nstat[:],
                    AF_[0:2, F_MSB + 512 * sp:F_MSB + 512 * (sp + 1)],
                    start=True, stop=True), w0, "ct", 1)
                em.fill(st_["c_wrs"][sp], "ct")
            for sp in range(4):
                w0 = [st_["nsqsq_ev"]]
                if sp > 0:
                    w0.append(st_["nsqred_ev_%d" % (sp - 1)])
                elif pst is not None:
                    w0.append(pst["nsqred_ev_3"])
                E("tensor", lambda e, sp=sp: e.matmul(
                    PS5[0:2, :], ones2_t[:], s4[:, 512 * sp:512 * (sp + 1)],
                    start=True, stop=True), w0, "ct", 1)
                em.fill(st_["c_nsq"][sp], "ct")

        for oc in range(16):
            p1_qkv(oc)
            if oc >= 1:
                p1_pesmalls(oc - 1)
            p1_nonpe(oc)
        p1_pesmalls(15)
        p1a_cv = A("cv")


        E("scalar", lambda e: e.activation(
            out=AF_[0:2, S_MAXQ:S_MAXQ + 8], in_=AF_[0:2, S_MAXQ:S_MAXQ + 8],
            func=AF.Sqrt), [("cv", p1a_cv)], "cs", 1)
        E("scalar", lambda e: e.activation(
            out=AF_[0:2, S_MAXK:S_MAXK + 8], in_=AF_[0:2, S_MAXK:S_MAXK + 8],
            func=AF.Sqrt), [], "cs", 1)
        E("vector", lambda e: e.tensor_mul(
            out=AF_[0:2, S_MAXQ:S_MAXQ + 8], in0=AF_[0:2, S_MAXQ:S_MAXQ + 8],
            in1=AF_[0:2, S_MAXK:S_MAXK + 8]), [("cs", A("cs"))], "cv", 1)
        E("vector", lambda e: e.tensor_scalar(
            out=AF_[0:2, S_MAXQ:S_MAXQ + 8], in0=AF_[0:2, S_MAXQ:S_MAXQ + 8],
            scalar1=-0.125, scalar2=None, op0=ALU.mult),
          [("cv", A("cv"))], "cv", 1)
        negM_cv = A("cv")
        for h in range(H):
            r, c = h % 2, h // 2
            E("tensor", lambda e, r=r, c=c, h=h: e.matmul(
                PS4[:, h:h + 1], sel_t[:, 128 * r:128 * (r + 1)],
                AF_[0:2, S_MAXQ + c:S_MAXQ + c + 1], start=True, stop=True),
              [("cv", negM_cv)] + (free(("ps4",)) if h == 0 else []), "ct", 1)
        prod(("ps4",), "ct")
        E("vector", lambda e: e.tensor_copy(
            out=af(S_BIASM, 16), in_=PS4[:, 0:16]), need(("ps4",)), "cv", 1)
        cons(("ps4",), "cv")
        biasM_cv = A("cv")
        tap("qhat", ab(R_B, 16384))
        tap("khat", ab(R_C, 16384))

        pt_off = []
        off = 0
        for j in range(8):
            pt_off.append((R_A, off))
            off += T - 128 * j
        off = 0
        for j in range(8, 16):
            pt_off.append((R_E, off))
            off += T - 128 * j

        def pt_ap(j, col0, width):
            base, o = pt_off[j]
            return ab(base + o + col0, width)

        for pair in range(8):
            for parx in range(2):
                h = 2 * pair + parx
                base = 64 * parx
                ptev = {}
                for j in range(NTC):
                    qt0 = 128 * j
                    nsp = (T - qt0 + 511) // 512
                    for spi in range(nsp):
                        c0 = qt0 + 512 * spi
                        w = min(512, T - c0)
                        bnk = ("sb", (j + spi) % 4)
                        bank = PS[(j + spi) % 4]
                        w0 = free(bnk) + ([("cv", biasM_cv)] if h == 0 else [])
                        E("tensor", lambda e, j=j, c0=c0, w=w, bank=bank, base=base,
                            pair=pair: e.matmul(
                            bank[:, 0:w],
                            khat[base:base + 64,
                                 T * pair + 128 * j:T * pair + 128 * (j + 1)],
                            qhat[base:base + 64, T * pair + c0:T * pair + c0 + w],
                            start=True, stop=True), w0, "ct", 1)
                        prod(bnk, "ct")
                        E("scalar", lambda e, j=j, c0=c0, w=w, bank=bank, qt0=qt0,
                            h=h: e.activation(
                            out=pt_ap(j, c0 - qt0, w), in_=bank[:, 0:w],
                            func=AF.Exp, scale=0.125, bias=af(S_BIASM + h, 1)),
                          need(bnk) +
                          (free(("pt", j)) + [("cv", biasM_cv)] if spi == 0 else []),
                          "cs", 1)
                        cons(bnk, "cs")
                    exp_ev = ("cs", A("cs"))
                    E("vector", lambda e, j=j: e.tensor_mul(
                        out=pt_ap(j, 0, 128), in0=pt_ap(j, 0, 128), in1=tri_t[:]),
                      [exp_ev], "cv", 1)
                    ptev[j] = [exp_ev, ("cv", A("cv"))]
                    prod(("pt", j), "cv")
                for s in range(4):
                    avb = ("avb", s % 2)
                    avbank = PS[4 + (s % 2)]
                    njs = 4 * s + 4
                    for j in range(njs):
                        coff = 512 * s - 128 * j
                        c0 = max(0, coff)
                        pcol = max(0, -coff)
                        w = 512 - pcol
                        w0 = list(ptev[j]) + (free(avb) if j == 0 else [])
                        E("tensor", lambda e, j=j, c0=c0, pcol=pcol, w=w,
                            avbank=avbank, h=h, njs=njs: e.matmul(
                            avbank[0:65, pcol:pcol + w],
                            vt[:, 1040 * j + 65 * h:1040 * j + 65 * h + 65],
                            pt_ap(j, c0, w),
                            start=(j == 0), stop=(j == njs - 1)),
                          w0 + ([("cv", vones_cv)] if h == 0 and s == 0 else []),
                          "ct", 1)
                        cons(("pt", j), "ct")
                    prod(avb, "ct")
                    E("scalar", lambda e, s=s, parx=parx, avbank=avbank: e.copy(
                        out=AF_[0:65, F_S1 + 2048 * parx + 512 * s:
                                F_S1 + 2048 * parx + 512 * (s + 1)],
                        in_=avbank[0:65, :]),
                      need(avb) + (free(("ascr", parx)) if s == 0 else []), "cs", 1)
                    cons(avb, "cs")
                prod(("ascr", parx), "cs")
            for parx in range(2):
                h = 2 * pair + parx
                E("scalar", lambda e, parx=parx: e.activation(
                    out=AF_[64:65, F_MSB:F_MSB + 2048],
                    in_=AF_[64:65, F_S1 + 2048 * parx:F_S1 + 2048 * parx + 2048],
                    func=AF.Ln),
                  need(("ascr", parx)) + free(("msb",)), "cs", 1)
                E("scalar", lambda e: e.activation(
                    out=AF_[64:65, F_MSB:F_MSB + 2048],
                    in_=AF_[64:65, F_MSB:F_MSB + 2048], func=AF.Exp, scale=-1.0),
                  [], "cs", 1)
                prod(("msb",), "cs")
                for s in range(4):
                    bnk = ("nb", s % 2)
                    bank = PSB[0:64, 512 * (s % 2):512 * (s % 2) + 512]
                    E("tensor", lambda e, s=s, bank=bank: e.matmul(
                        bank, onesf_t[64:65, 0:64],
                        AF_[64:65, F_MSB + 512 * s:F_MSB + 512 * (s + 1)],
                        start=True, stop=True),
                      need(("msb",)) + free(bnk), "ct", 1)
                    prod(bnk, "ct")
                    E("scalar", lambda e, s=s, bank=bank: e.copy(
                        out=AF_[0:64, F_MSB + 512 * s:F_MSB + 512 * (s + 1)],
                        in_=bank),
                      need(bnk) + (free(("rb",)) if s == 0 else []), "cs", 1)
                    cons(bnk, "cs")
                prod(("rb",), "cs")
                if parx == 0:
                    E("vector", lambda e, pair=pair: e.tensor_mul(
                        out=AB[0:64, R_B + T * pair:R_B + T * (pair + 1)],
                        in0=AF_[0:64, F_S1:F_S1 + 2048],
                        in1=AF_[0:64, F_MSB:F_MSB + 2048]),
                      need(("rb",), ("ascr", 0)), "cv", 1)
                    cons(("rb",), "cv")
                    cons(("ascr", 0), "cv")
                else:
                    E("vector", lambda e: e.tensor_mul(
                        out=AB[0:64, R_A + 12800:R_A + 12800 + 2048],
                        in0=AF_[0:64, F_S1 + 2048:F_S1 + 4096],
                        in1=AF_[0:64, F_MSB:F_MSB + 2048]),
                      need(("rb",), ("ascr", 1)) + free(("oscr",)), "cv", 1)
                    cons(("rb",), "cv")
                    cons(("ascr", 1), "cv")
                    prod(("oscr",), "cv")
                    E("sync", lambda e, pair=pair: e.dma_start(
                        out=AB[64:128, R_B + T * pair:R_B + T * (pair + 1)],
                        in_=AB[0:64, R_A + 12800:R_A + 12800 + 2048]),
                      need(("oscr",)), "sld", 16)
                    cons(("oscr",), "sld")
            emit_wjobs(17)
        emit_wjobs(200)
        tap("attnout", ab(R_B, 16384))
        attn_cv = A("cv")
        attn_cs = A("cs")
        attn_sld = A("sld")
        attn_ct = A("ct")
        attnout = qhat

        XQW = 2052
        xqvf = ab(R_D, NCC * XQW)
        E("vector", lambda e: e.memset(
            xqvf.rearrange("p (c t) -> p c t", c=NCC)[:, :, 0:1], 0.0),
          [("cv", attn_cv), ("cs", attn_cs), ("ct", attn_ct), ("sld", attn_sld)],
          "cv", 1)
        E("vector", lambda e: e.memset(
            xqvf.rearrange("p (c t) -> p c t", c=NCC)[:, :, 2049:2052], 0.0),
          [], "cv", 1)
        guards_cv = A("cv")
        for c in range(NCC):
            dma("gpsimd", ab(R_C + 1024 * c, 1024), dwoutT[128 * c:128 * (c + 1), :],
                waits=[("ct", attn_ct), ("cs", attn_cs)], sem="wo")
        wout_ld = A("wo")

        E("vector", lambda e: e.reduce_sum(
            out=af(S_WSUMS + 128, 1), in_=af(S_WSUMS, 64), axis=AX.X),
          [("cv", A("cv"))], "cv", 1)
        E("vector", lambda e: e.reduce_sum(
            out=af(S_WSUMS + 129, 1), in_=af(S_WSUMS + 64, 64), axis=AX.X),
          [("cv", A("cv"))], "cv", 1)
        wred_cv = A("cv")
        for k in range(2):
            E("tensor", lambda e, k=k: e.matmul(
                PS4[0:1, k:k + 1], onesf_t[:, 0:1], af(S_WSUMS + 128 + k, 1),
                start=True, stop=True),
              [("cv", wred_cv), ("cs", attn_cs)] +
              (free(("ps4",), ("ps45",)) if k == 0 else []), "ct", 1)
        prod(("ps4",), "ct")
        E("vector", lambda e: e.tensor_copy(
            out=AF_[0:1, S_WSUMS + 130:S_WSUMS + 132], in_=PS4[0:1, 0:2]),
          need(("ps4",)), "cv", 1)
        cons(("ps4",), "cv")
        E("vector", lambda e: e.tensor_scalar(
            out=AF_[0:1, S_WSUMS + 130:S_WSUMS + 132],
            in0=AF_[0:1, S_WSUMS + 130:S_WSUMS + 132],
            scalar1=1.0 / (C * HID), scalar2=1e-5, op0=ALU.mult, op1=ALU.max),
          [("cv", A("cv"))], "cv", 1)
        E("vector", lambda e: e.reciprocal(
            out=AF_[0:1, S_WSUMS + 132:S_WSUMS + 134],
            in_=AF_[0:1, S_WSUMS + 130:S_WSUMS + 132]),
          [("cv", A("cv"))], "cv", 1)
        mw_cv = A("cv")
        E("tensor", lambda e: e.matmul(
            PS4[:, 0:4], onesf_t[0:1, 0:128],
            AF_[0:1, S_WSUMS + 130:S_WSUMS + 134], start=True, stop=True),
          [("cv", mw_cv)] + free(("ps4",)), "ct", 1)
        prod(("ps4",), "ct")
        E("vector", lambda e: e.tensor_copy(out=af(S_WSC, 4), in_=PS4[:, 0:4]),
          need(("ps4",)), "cv", 1)
        cons(("ps4",), "cv")
        wsc_cv = A("cv")


        def p25_tp(i):
            spar = i % 3
            src = ab(R_E + 1024 * spar, 1024)
            dma("sync", src, dxqs[128 * i:128 * (i + 1), :],
                waits=free(("tsrc", spar)) +
                ([("gld", p0_gld), ("sld", attn_sld), ("ct", attn_ct),
                  ("cs", attn_cs), ("cv", attn_cv)] if i == 0 else []),
                sem="ts%d" % spar)
            prod(("tsrc", spar), "ts%d" % spar)
            w32 = af(F_S4 + 1024 * (i % 2), 1024)
            E("scalar", lambda e, src=src, w32=w32: e.activation(
                out=w32, in_=src, func=AF.Copy),
              need(("tsrc", spar)) + free(("w32", i % 2)), "cs", 1)
            cons(("tsrc", spar), "cs")
            prod(("w32", i % 2), "cs")
            for c in range(NCC):
                tb = ("tpb", c % 2)
                w0 = (need(("w32", i % 2)) if c == 0 else []) + free(tb) + \
                    ([("cs", attn_cs), ("ct", attn_ct)] if i == 0 and c < 2 else [])
                E("tensor", lambda e, c=c, w32=w32: e.transpose(
                    psq(c % 2, 0, 128, 0, 128), w32[:, 128 * c:128 * (c + 1)],
                    id_t[:]), w0, "ct", 1)
                prod(tb, "ct")
                if c == NCC - 1:
                    cons(("w32", i % 2), "ct")
                dst = xqvf[:, XQW * c + 1 + 128 * i:XQW * c + 1 + 128 * (i + 1)]
                if c % 2 == 0:
                    E("scalar", lambda e, c=c, dst=dst: e.activation(
                        out=dst, in_=psq(c % 2, 0, 128, 0, 128), func=AF.Copy),
                      need(tb) + ([("cv", guards_cv)] if i == 0 else []), "cs", 1)
                    cons(tb, "cs")
                else:
                    E("vector", lambda e, c=c, dst=dst: e.tensor_copy(
                        out=dst, in_=psq(c % 2, 0, 128, 0, 128)),
                      need(tb) + ([("cv", guards_cv)] if i == 0 else []), "cv", 1)
                    cons(tb, "cv")

        def p25_proj(i):
            par = i % 2
            xbuf = af(F_MSB + 1024 * par, 1024)
            obuf = af(F_S2 + 1024 * par, 1024)
            dma("sync", xbuf, dx[128 * i:128 * (i + 1), :],
                waits=free(("xre", par)) +
                ([("cv", attn_cv), ("cs", attn_cs)] if i < 2 else []),
                sem="xb%d" % par)
            prod(("xre", par), "xb%d" % par)
            pjr = ("psb",) if par == 0 else (("ps4",), ("ps45",))
            for c in range(NCC):
                for sp in range(2):
                    w0 = ([("wo", wout_ld), ("sld", attn_sld),
                           ("cv", attn_cv)] +
                          (free(("psb",)) if par == 0
                           else free(("ps4",), ("ps45",)))
                          if (c, sp) == (0, 0) else [])

                    def pjmm(e, i=i, c=c, sp=sp, par=par):
                        outap = (PSB[:, 512 * sp:512 * (sp + 1)] if par == 0
                                 else (PS4[:] if sp == 0 else PS5[:]))
                        return e.matmul(
                            outap,
                            attnout[:, T * c + 128 * i:T * c + 128 * (i + 1)],
                            ab(R_C + 1024 * c + 512 * sp, 512),
                            start=(c == 0), stop=(c == NCC - 1))
                    E("tensor", pjmm, w0, "ct", 1)
            if par == 0:
                prod(("psb",), "ct")
            else:
                prod(("ps4",), "ct")
                prod(("ps45",), "ct")

            def pjfin(e, xbuf=xbuf, obuf=obuf, par=par):
                if par == 0:
                    return e.scalar_tensor_tensor(
                        out=obuf, in0=PSB[:], scalar=1.0, in1=xbuf,
                        op0=ALU.mult, op1=ALU.add)
                e.scalar_tensor_tensor(
                    out=obuf[:, 0:512], in0=PS4[:], scalar=1.0,
                    in1=xbuf[:, 0:512], op0=ALU.mult, op1=ALU.add)
                return e.scalar_tensor_tensor(
                    out=obuf[:, 512:1024], in0=PS5[:], scalar=1.0,
                    in1=xbuf[:, 512:1024], op0=ALU.mult, op1=ALU.add)
            pw = (need(("psb",)) if par == 0 else need(("ps4",), ("ps45",))) + \
                need(("xre", par)) + free(("obuf", par))
            E("vector", pjfin, pw, "cv", 1)
            if par == 0:
                cons(("psb",), "cv")
            else:
                cons(("ps4",), "cv")
                cons(("ps45",), "cv")
            cons(("xre", par), "cv")
            prod(("obuf", par), "cv")
            dma("sync", daccs[128 * i:128 * (i + 1), :], obuf,
                waits=need(("obuf", par)), sem="dac")
            cons(("obuf", par), "dac")

        wsign2 = [(R_A + 1024 * hc) if hc < 16 else (R_B + 1024 * (hc - 16))
                  for hc in range(NHC)]
        w1q_g = {}
        fcnt = [0]
        cg_hw = A("cg")

        STRB = (14016, 14528, 15040, 15552)

        def item_w2(hc, h):
            k5 = fcnt[0] % 4
            first = fcnt[0] < 4
            fcnt[0] += 1
            buf = af(STRB[k5], 512)
            dma("gpsimd", buf, dw2T[128 * hc:128 * (hc + 1), 512 * h:512 * (h + 1)],
                waits=free(("fcos", k5)) +
                ([("cv", attn_cv), ("cs", attn_cs), ("sld", attn_sld),
                  ("cg", cg_hw)] if first else []), sem="sb%d" % k5)
            prod(("fcos", k5), "sb%d" % k5)
            E("vector", lambda e, buf=buf: e.tensor_scalar(
                out=i32b[:], in0=buf, scalar1=AF_[:, S_WSC + 3:S_WSC + 4],
                scalar2=None, op0=ALU.mult),
              need(("fcos", k5)) + ([("cv", wsc_cv)] if hc == 0 and h == 0
                                    else []), "cv", 1)
            cons(("fcos", k5), "cv")
            E("vector", lambda e, hc=hc, h=h: e.tensor_scalar(
                out=ab(wsign2[hc] + 512 * h, 512), in0=i32b[:],
                scalar1=-1.0, scalar2=1.0, op0=ALU.max, op1=ALU.min),
              [("sld", attn_sld), ("ct", attn_ct)] if hc == 0 and h == 0 else
              ([("ct", proj_ct), ("sld", proj_sld)] if hc == 16 and h == 0
               else []), "cv", 1)

        def item_w1(jj, h):
            b4, c = divmod(jj, 8)
            k5 = fcnt[0] % 4
            first = fcnt[0] < 4
            fcnt[0] += 1
            buf = af(STRB[k5], 512)
            dma("gpsimd", buf,
                dw1T[128 * c:128 * (c + 1),
                     1024 * b4 + 512 * h:1024 * b4 + 512 * (h + 1)],
                waits=free(("fcos", k5)) +
                ([("cv", attn_cv), ("cs", attn_cs), ("sld", attn_sld),
                  ("cg", cg_hw)] if first else []), sem="sb%d" % k5)
            prod(("fcos", k5), "sb%d" % k5)
            E("vector", lambda e, buf=buf: e.tensor_scalar(
                out=i32b[:], in0=buf, scalar1=AF_[:, S_WSC + 2:S_WSC + 3],
                scalar2=None, op0=ALU.mult), need(("fcos", k5)), "cv", 1)
            cons(("fcos", k5), "cv")
            bk = (2 * jj + h) % 8
            bout = ab(R_C + 8192 + 512 * bk, 512)
            E("vector", lambda e, bout=bout: e.tensor_scalar(
                out=bout, in0=i32b[:], scalar1=-1.0, scalar2=1.0,
                op0=ALU.max, op1=ALU.min),
              free(("bout", bk)) + ([("ct", attn_ct)] if 2 * jj + h < 8
                                    else []), "cv", 1)
            prod(("bout", bk), "cv")
            dma("sync",
                dw1q[128 * c:128 * (c + 1),
                     1024 * b4 + 512 * h:1024 * b4 + 512 * (h + 1)],
                bout, waits=need(("bout", bk)), sem="wq")
            cons(("bout", bk), "wq")
            if c == 7 and h == 1:
                w1q_g[b4] = A("wq")

        items = [(lambda hc=hc, h=h: item_w2(hc, h))
                 for hc in range(16) for h in range(2)] + \
                [(lambda jj=jj, h=h: item_w1(jj, h))
                 for jj in range(32) for h in range(2)]
        xqvf_cv = xqvf_cs = None
        for i in range(NTC):
            p25_tp(i)
            if i == NTC - 1:
                xqvf_cs = A("cs")
                xqvf_cv = A("cv")
            if i >= 2:
                p25_proj(i - 2)
            for fn in items[6 * i:6 * i + 6]:
                fn()
        tap("xqvf", xqvf)
        p25_proj(NTC - 2)
        p25_proj(NTC - 1)
        proj_sld = A("sld")
        proj_dac = A("dac")
        proj_ct = A("ct")
        for hc in range(16, 32):
            for h in range(2):
                item_w2(hc, h)
        w2q_cv = A("cv")

        yt2 = af(F_YT, 8192)

        def xq2b(ti):
            return ab(R_C + 8192 * (ti % 2), 8192)
        pend_silu = []
        pend_T = []
        pend_sT = []
        pend_BC = []
        round_cells = [Cell() for _ in range(NFT)]
        yt2ev = {}

        def flush_T():
            while pend_T:
                qti, tw, cT = pend_T.pop(0)
                for k in range(2):
                    E("tensor", lambda e, k=k: e.transpose(
                        psq(2, 0, 128, 128 * k, 128 * (k + 1)),
                        af(F_FOLD + 128 * k, 128), id_t[:]),
                      (need(("fold",)) + tw) if k == 0 else [], "ct", 1)
                    em.fill(cT[k], "ct")
                cons(("fold",), "ct")

        def flush_sT():
            while pend_sT:
                qti, s2cv, fw, cT_s = pend_sT.pop(0)
                E("tensor", lambda e, qti=qti: e.transpose(
                    PS5[0:2, 0:128],
                    AF_[:, S_S2SC + 2 * qti:S_S2SC + 2 * qti + 2], id_t[:]),
                  [("cv", s2cv)] + fw, "ct", 1)
                em.fill(cT_s, "ct")

        def flush_BC():
            while pend_BC:
                qti, gat_sg, cBC = pend_BC.pop(0)
                E("tensor", lambda e: e.matmul(
                    PS4[:, 0:FT], onesf_t[64:65, 0:128],
                    AF_[64:65, S_S2ROW + 128:S_S2ROW + 128 + FT],
                    start=True, stop=True), [("sg", gat_sg)], "ct", 1)
                cons(("s2rg",), "ct")
                em.fill(cBC, "ct")

        def flush_silu():
            while pend_silu:
                fn, waits, key = pend_silu.pop(0)
                E("scalar", fn, waits, "cs", 1)
                yt2ev[key] = ("cs", A("cs"))

        def p3_h(ti):
            a = FT * ti
            hspan = FT + 2
            for hc in range(NHC):
                par = hc % 2
                p4 = hc % 4
                hbr = ("hb", p4)
                hb = PS[p4]
                sec = ab(R_E + 1024 * par, 1024)
                dma("sync", sec.rearrange("p (c n) -> p c n", c=NCC),
                    dw1q[:, 128 * hc:128 * (hc + 1)].rearrange(
                        "(c p) n -> p c n", c=NCC),
                    waits=[("wq", w1q_g[hc // 8]), ("cv", xqvf_cv),
                           ("cs", xqvf_cs)] + free(("w1sec", par)),
                    sem="ws%d" % par)
                prod(("w1sec", par), "ws%d" % par)
                for c in range(NCC):
                    w0 = (need(("w1sec", par)) + free(hbr) +
                          ([("cs", xqvf_cs), ("cv", xqvf_cv), ("ct", attn_ct)]
                           if ti == 0 and hc < 4 else [])
                          if c == 0 else [])
                    E("tensor", lambda e, a=a, hspan=hspan, c=c, hb=hb, sec=sec:
                        e.matmul(
                        hb[:, 0:hspan], sec[:, 128 * c:128 * (c + 1)],
                        xqvf[:, XQW * c + a:XQW * c + a + hspan],
                        start=(c == 0), stop=(c == NCC - 1)), w0, "ct", 1)
                prod(hbr, "ct")
                cons(("w1sec", par), "ct")
                csc = af(F_CONV0 + 256 * par, FT)
                cscr = ("csc", par)
                E("scalar", lambda e, hb=hb, hc=hc, csc=csc: e.activation(
                    out=csc, in_=hb[:, 2:2 + FT], func=AF.Copy,
                    scale=taps_t[:, 64 + hc:65 + hc]),
                  need(hbr) + free(cscr), "cs", 1)
                cons(hbr, "cs")
                prod(cscr, "cs")
                E("vector", lambda e, hb=hb, hc=hc, csc=csc: e.scalar_tensor_tensor(
                    out=csc, in0=hb[:, 0:FT], scalar=taps_t[:, hc:hc + 1],
                    in1=csc, op0=ALU.mult, op1=ALU.add),
                  need(cscr, hbr), "cv", 1)
                prod(cscr, "cv")
                E("vector", lambda e, hb=hb, hc=hc, csc=csc: e.scalar_tensor_tensor(
                    out=csc, in0=hb[:, 1:1 + FT], scalar=taps_t[:, 32 + hc:33 + hc],
                    in1=csc, op0=ALU.mult, op1=ALU.add),
                  need(cscr), "cv", 1)
                cons(hbr, "cv")
                prod(cscr, "cv")
                silu_waits = (need(cscr) + [("cv", wsc_cv)] +
                              ([round_cells[ti - 1]] if ti >= 1 else []))
                pend_silu.append((
                    (lambda e, hc=hc, csc=csc: e.activation(
                        out=yt2[:, FT * hc:FT * (hc + 1)], in_=csc, func=AF.Silu,
                        scale=AF_[:, S_WSC:S_WSC + 1], bias=convb_t[:, hc:hc + 1])),
                    silu_waits, ti))
                cons(cscr, "cs")
                if len(pend_silu) > 1:
                    fn, waits, key = pend_silu.pop(0)
                    E("scalar", fn, waits, "cs", 1)
                    yt2ev[key] = ("cs", A("cs"))
                if hc == 1:
                    flush_T()
                elif hc == 2:
                    flush_sT()
                elif hc == 3:
                    flush_BC()

        def p3_quant(ti):
            flush_silu()
            xq2 = xq2b(ti)
            sev = yt2ev[ti]
            E("vector", lambda e: e.tensor_reduce(
                out=af(F_FOLD, FT), in_=yt2.rearrange("p (c t) -> p t c", c=NHC),
                axis=AX.X, op=ALU.max, apply_absolute_value=True),
              [sev] + free(("fold",)), "cv", 1)
            prod(("fold",), "cv")
            cT = [Cell(), Cell()]
            pend_T.append((ti, free(("hb", 2)), cT))
            for k in range(2):
                gcol = 2 * ti + k
                E("vector", lambda e, k=k: e.reduce_max(
                    out=af(S_TMX + k, 1),
                    in_=psq(2, 0, 128, 128 * k, 128 * (k + 1)), axis=AX.X),
                  [cT[k]], "cv", 1)
                cons(("hb", 2), "cv")
                E("vector", lambda e, k=k, gcol=gcol: e.tensor_scalar(
                    out=af(S_FSC + gcol, 1), in0=af(S_TMX + k, 1), scalar1=1e-5,
                    scalar2=1.0 / 127.0, op0=ALU.max, op1=ALU.mult),
                  [("cv", A("cv"))], "cv", 1)
                E("vector", lambda e, gcol=gcol: e.reciprocal(
                    out=af(S_S2SC + gcol, 1), in_=af(S_FSC + gcol, 1)),
                  [("cv", A("cv"))], "cv", 1)
                E("vector", lambda e, gcol=gcol: e.tensor_scalar(
                    out=af(S_FSC + gcol, 1), in0=af(S_FSC + gcol, 1),
                    scalar1=AF_[:, S_WSC + 1:S_WSC + 2], scalar2=None,
                    op0=ALU.mult), [("cv", A("cv"))], "cv", 1)
            s2_cv = A("cv")
            cT_s = Cell()
            cBC = Cell()
            fw = free(("w2b",))
            pend_sT.append((ti, s2_cv, fw, cT_s))
            E("vector", lambda e: e.tensor_copy(
                out=AF_[0:2, S_S2ROW:S_S2ROW + 128], in_=PS5[0:2, 0:128]),
              [cT_s] + free(("s2row",)), "cv", 1)
            prod(("s2row",), "cv")
            for k in range(2):
                E("gpsimd", lambda e, k=k: e.dma_start(
                    out=AF_[64:65, S_S2ROW + 128 + 128 * k:
                            S_S2ROW + 256 + 128 * k],
                    in_=AF_[k:k + 1, S_S2ROW:S_S2ROW + 128]),
                  (need(("s2row",)) + free(("s2rg",))) if k == 0 else [],
                  "sg", 16)
            cons(("s2row",), "sg")
            prod(("s2rg",), "sg")
            pend_BC.append((ti, A("sg"), cBC))
            for hc in range(NHC):
                E("vector", lambda e, hc=hc: e.tensor_tensor(
                    out=i32b[:, 0:FT], in0=yt2[:, FT * hc:FT * (hc + 1)],
                    in1=PS4[:, 0:FT], op=ALU.mult),
                  ([cBC] if hc == 0 else []) + [sev], "cv", 1)
                E("vector", lambda e, hc=hc, xq2=xq2: e.tensor_copy(
                    out=xq2[:, FT * hc:FT * (hc + 1)], in_=i32b[:, 0:FT]),
                  [("cv", A("cv"))] + free(("xq2", ti % 2, hc)), "cv", 1)
                prod(("xq2", ti % 2, hc), "cv")
            em.fill(round_cells[ti], "cv")

        def p3_w2(ti):
            a = FT * ti
            xq2 = xq2b(ti)
            for hc in range(NHC):
                for sp in range(2):
                    w0 = (need(("xq2", ti % 2, hc)) +
                          ([("cv", w2q_cv)] if ti == 0 and hc == 0 else []) +
                          (free(("w2a",), ("psb",)) if hc == 0 and sp == 0
                           else []))
                    E("tensor", lambda e, hc=hc, sp=sp: e.matmul(
                        PSB[:, 512 * sp:512 * (sp + 1)],
                        xq2[:, FT * hc:FT * hc + 128],
                        ab(wsign2[hc] + 512 * sp, 512),
                        start=(hc == 0), stop=(hc == NHC - 1)), w0, "ct", 1)
                    w1_ = (free(("w2b",), ("ps4",), ("ps45",)) +
                           [round_cells[ti]]
                           if hc == 0 and sp == 0 else [])
                    E("tensor", lambda e, hc=hc, sp=sp: e.matmul(
                        PS4[:] if sp == 0 else PS5[:],
                        xq2[:, FT * hc + 128:FT * hc + 256],
                        ab(wsign2[hc] + 512 * sp, 512),
                        start=(hc == 0), stop=(hc == NHC - 1)), w1_, "ct", 1)
                cons(("xq2", ti % 2, hc), "ct")
            prod(("w2a",), "ct")
            prod(("w2b",), "ct")
            for k in range(2):
                gcol = 2 * ti + k
                t0 = a + 128 * k
                abuf = af(F_XNC3 + 1024 * k, 1024)
                ob = af(F_OUTC + 1024 * k, 1024)
                dma("sync", abuf, daccs[t0:t0 + 128, :],
                    waits=free(("abuf", k)) + [("dac", proj_dac)],
                    sem="ab%d" % k)
                prod(("abuf", k), "ab%d" % k)
                wres = ("w2a",) if k == 0 else ("w2b",)
                for half in range(2):
                    def fin(e, k=k, gcol=gcol, abuf=abuf, ob=ob, half=half):
                        src = (PSB[:, 512 * half:512 * (half + 1)] if k == 0
                               else (PS4[:] if half == 0 else PS5[:]))
                        return e.scalar_tensor_tensor(
                            out=ob[:, 512 * half:512 * (half + 1)], in0=src,
                            scalar=af(S_FSC + gcol, 1),
                            in1=abuf[:, 512 * half:512 * (half + 1)],
                            op0=ALU.mult, op1=ALU.add)
                    E("vector", fin,
                      need(wres, ("abuf", k)) + free(("outc", k)), "cv", 1)
                cons(wres, "cv")
                cons(("abuf", k), "cv")
                prod(("outc", k), "cv")
                dma("sync", dout[t0:t0 + 128, :], ob, waits=need(("outc", k)))
                cons(("outc", k), "sld")

        for ti in range(NFT):
            p3_h(ti)
            if ti >= 1:
                p3_w2(ti - 1)
            p3_quant(ti)
        flush_T()
        flush_sT()
        flush_BC()
        p3_w2(NFT - 1)
        if DEBUG_TAPS:
            srcs = {"biasM": af(S_BIASM, 16),
                    "wsc": af(S_WSC, 4), "y0": yt2, "xq2t0": xq2b(NFT - 1),
                    "s2": af(S_S2SC, 16),
                    "abuf1": af(F_XNC3 + 1024, 1024),
                    "ob1": af(F_OUTC + 1024, 1024)}
            for name in DEBUG_TAPS:
                if name in srcs:
                    dma("sync", dbg[name], srcs[name],
                        waits=[("cv", A("cv")), ("cs", A("cs")), ("ct", A("ct")),
                               ("gld", A("gld"))])
        E("sync", lambda e: e.nop(),
          [(s, A(s)) for s in ("sld", "gld", "tsp", "ts0", "ts1", "ts2",
                               "xb0", "xb1", "ab0", "ab1", "ws0", "ws1",
                               "sb0", "sb1", "sb2", "sb3", "wo", "wq",
                               "dac", "sg")], None)

        block = st.enter_context(nc.Block())
        em.finalize(block, sems)
    return nc


def kernel(**inputs):
    in_maps = _host_inputs(**inputs)
    nc = build_nc()
    res = run_bass_kernel_spmd(nc, in_maps, core_ids=list(range(8)))
    out = np.stack([np.asarray(r["out"]) for r in res.results], axis=0)
    return out.astype(np.float32, copy=False)


# version 10
# speedup vs baseline: 1.0757x; 1.0470x over previous
import numpy as np
import ml_dtypes
from contextlib import ExitStack

import concourse.bass as bass
import concourse.mybir as mybir
from concourse.bass_utils import run_bass_kernel_spmd

F32 = mybir.dt.float32
BF16 = mybir.dt.bfloat16
I32 = mybir.dt.int32
AF = mybir.ActivationFunctionType
ALU = mybir.AluOpType
AX = mybir.AxisListType
BF = ml_dtypes.bfloat16

T = 2048
C = 1024
H = 16
D = 64
HID = 4096
NTC = T // 128
NCC = C // 128
NHC = HID // 128
EPS_LN = 1e-5
EPS_RMS = 1e-6
FT = 256
NFT = T // FT

DEBUG_TAPS = []


class Cell:
    __slots__ = ("sem", "v")

    def __init__(self):
        self.sem = None
        self.v = None


class Em:

    def __init__(self):
        self.q = {e: [] for e in ("tensor", "vector", "scalar", "gpsimd", "sync")}
        self.cnt = {}
        self.resp = {}
        self.resc = {}

    def fill(self, cell, sem):
        cell.sem = sem
        cell.v = self.at(sem)

    def at(self, sem):
        return self.cnt.get(sem, 0)

    def emit(self, eng, fn, waits=(), inc=None, incn=1):
        self.q[eng].append((fn, tuple(waits), inc, incn))
        if inc is not None:
            self.cnt[inc] = self.cnt.get(inc, 0) + incn

    def prod(self, res, sem):
        self.resp[res] = (sem, self.at(sem))
        self.resc[res] = {}

    def cons(self, res, sem):
        d = self.resc.setdefault(res, {})
        d[sem] = self.at(sem)

    def need(self, *ress):
        out = []
        for r in ress:
            p = self.resp.get(r)
            if p:
                out.append(p)
        return out

    def free(self, *ress):
        out = []
        for r in ress:
            out.extend(self.resc.get(r, {}).items())
        return out

    def finalize(self, block, sems):
        def mk(name):
            ops = self.q[name]

            def run(e):
                hw = {}
                for fn, waits, inc, incn in ops:
                    merged = {}
                    for w in waits:
                        if isinstance(w, Cell):
                            assert w.v is not None, "unfilled Cell wait"
                            s, v = w.sem, w.v
                        else:
                            s, v = w
                        if v > 0 and v > merged.get(s, 0):
                            merged[s] = v
                    for s, v in merged.items():
                        if v > hw.get(s, 0):
                            e.wait_ge(sems[s], v)
                            hw[s] = v
                    inst = fn(e)
                    if inc is not None:
                        inst.then_inc(sems[inc], incn)
            return run
        block.tensor(mk("tensor"))
        block.vector(mk("vector"))
        block.scalar(mk("scalar"))
        block.gpsimd(mk("gpsimd"))
        block.sync(mk("sync"))


def _host_consts():
    half = D // 2
    inv_freq = 1.0 / (10000.0 ** (np.arange(0, half, dtype=np.float64) * 2.0 / D))
    ang = np.arange(T, dtype=np.float64)[:, None] * inv_freq[None, :]
    cosv = np.cos(ang).astype(np.float32)
    sinv = np.sin(ang).astype(np.float32)
    cosrep = np.zeros((128, T), np.float32)
    sinpat = np.zeros((128, T), np.float32)
    for p in range(128):
        j = p % half
        cosrep[p] = cosv[:, j]
        sinpat[p] = (-sinv[:, j]) if (p % D) < half else sinv[:, j]
    trimask = np.zeros((128, 128), np.float32)
    for kt in range(128):
        trimask[kt, kt:] = 1.0
    ident = np.eye(128, dtype=np.float32)
    onesf = np.ones((128, 128), np.float32)
    ones2 = np.zeros((128, 2), np.float32)
    ones2[0:64, 0] = 1.0
    ones2[64:128, 1] = 1.0
    sel01 = np.zeros((2, 256), np.float32)
    sel01[0, 0:128] = 1.0
    sel01[1, 128:256] = 1.0
    smalls = np.zeros((128, 8), np.float32)
    smalls[:, 0] = 127.0
    smalls[:, 1] = 1e-5
    smalls[:, 2] = EPS_RMS
    smalls[:, 3] = 1.0
    smalls[:, 5] = EPS_LN
    return dict(cosrep=cosrep, sinpat=sinpat,
                trimask=np.ascontiguousarray(trimask.astype(BF)),
                ident=ident, onesf=onesf, ones2=ones2, sel01=sel01, smalls=smalls)


def _host_inputs(L0_attn_x, L0_attn_w_qkv, L0_attn_w_out, L0_attn_q_norm_w,
                 L0_attn_k_norm_w, L0_ffn_w1, L0_ffn_conv_w, L0_ffn_conv_b,
                 L0_ffn_w2):
    consts = _host_consts()
    wqkv = np.asarray(L0_attn_w_qkv)
    wqkT = np.ascontiguousarray(wqkv[0:2 * C].T)
    wvT = np.ascontiguousarray(wqkv[2 * C:3 * C].T)
    woutT = np.ascontiguousarray(np.asarray(L0_attn_w_out).T)
    w1T = np.ascontiguousarray(np.asarray(L0_ffn_w1).T)
    w2T = np.ascontiguousarray(np.asarray(L0_ffn_w2).T)
    qn = np.asarray(L0_attn_q_norm_w).astype(np.float32, copy=False)
    kn = np.asarray(L0_attn_k_norm_w).astype(np.float32, copy=False)
    qn_stat = np.zeros((2, 128), np.float32)
    qn_stat[0, 0:64] = qn
    qn_stat[1, 64:128] = qn
    kn_stat = np.zeros((2, 128), np.float32)
    kn_stat[0, 0:64] = kn
    kn_stat[1, 64:128] = kn
    cw = np.asarray(L0_ffn_conv_w).reshape(HID, 3)
    taps = np.zeros((128, 96), np.float32)
    convb = np.zeros((128, 32), np.float32)
    cb = np.asarray(L0_ffn_conv_b)
    for c in range(NHC):
        taps[:, c] = cw[128 * c:128 * (c + 1), 0]
        taps[:, 32 + c] = cw[128 * c:128 * (c + 1), 1]
        taps[:, 64 + c] = cw[128 * c:128 * (c + 1), 2]
        convb[:, c] = cb[128 * c:128 * (c + 1)]
    base = dict(wqkT=wqkT, wvT=wvT, woutT=woutT, w1T=w1T, w2T=w2T,
                qn_stat=qn_stat, kn_stat=kn_stat, taps=taps, convb=convb, **consts)
    xs = np.asarray(L0_attn_x)
    return [dict(base, x=np.ascontiguousarray(xs[b], dtype=np.float32))
            for b in range(8)]


def build_nc():
    nc = bass.Bass("TRN2", target_bir_lowering=False)

    dx = nc.dram_tensor("x", [T, C], F32, kind="ExternalInput").ap()
    dwqkT = nc.dram_tensor("wqkT", [C, 2 * C], F32, kind="ExternalInput").ap()
    dwvT = nc.dram_tensor("wvT", [C, C], F32, kind="ExternalInput").ap()
    dwoutT = nc.dram_tensor("woutT", [C, C], F32, kind="ExternalInput").ap()
    dw1T = nc.dram_tensor("w1T", [C, HID], F32, kind="ExternalInput").ap()
    dw2T = nc.dram_tensor("w2T", [HID, C], F32, kind="ExternalInput").ap()
    dcos = nc.dram_tensor("cosrep", [128, T], F32, kind="ExternalInput").ap()
    dsin = nc.dram_tensor("sinpat", [128, T], F32, kind="ExternalInput").ap()
    dtri = nc.dram_tensor("trimask", [128, 128], BF16, kind="ExternalInput").ap()
    dident = nc.dram_tensor("ident", [128, 128], F32, kind="ExternalInput").ap()
    donesf = nc.dram_tensor("onesf", [128, 128], F32, kind="ExternalInput").ap()
    dones2 = nc.dram_tensor("ones2", [128, 2], F32, kind="ExternalInput").ap()
    dsel01 = nc.dram_tensor("sel01", [2, 256], F32, kind="ExternalInput").ap()
    dsmalls = nc.dram_tensor("smalls", [128, 8], F32, kind="ExternalInput").ap()
    dqn = nc.dram_tensor("qn_stat", [2, 128], F32, kind="ExternalInput").ap()
    dkn = nc.dram_tensor("kn_stat", [2, 128], F32, kind="ExternalInput").ap()
    dtaps = nc.dram_tensor("taps", [128, 96], F32, kind="ExternalInput").ap()
    dconvb = nc.dram_tensor("convb", [128, 32], F32, kind="ExternalInput").ap()

    dxqs = nc.dram_tensor("xqs", [T, C], BF16, kind="Internal").ap()
    daccs = nc.dram_tensor("accs", [T, C], F32, kind="Internal").ap()
    dw1q = nc.dram_tensor("w1q", [C, HID], BF16, kind="Internal").ap()
    ds2 = nc.dram_tensor("ds2", [16, 256], F32, kind="Internal").ap()
    dout = nc.dram_tensor("out", [T, C], F32, kind="ExternalOutput").ap()

    dbg = {}
    if DEBUG_TAPS:
        shapes = {"xnf": ([128, NCC * T], BF16), "qhat": ([128, NCC * T], BF16),
                  "khat": ([128, NCC * T], BF16), "vt": ([128, NTC * H * 65], BF16),
                  "attnout": ([128, NCC * T], BF16), "biasM": ([128, 16], F32),
                  "wsc": ([128, 4], F32), "y0": ([128, NHC * FT], F32),
                  "xq2t0": ([128, NHC * FT], BF16), "s2": ([128, 16], F32),
                  "xqvf": ([128, NCC * 2052], BF16),
                  "abuf1": ([128, 1024], F32), "ob1": ([128, 1024], F32)}
        for name in DEBUG_TAPS:
            shp, dt = shapes[name]
            dbg[name] = nc.dram_tensor("dbg_" + name, shp, dt,
                                       kind="ExternalOutput").ap()

    em = Em()
    E, A = em.emit, em.at
    need, free, prod, cons = em.need, em.free, em.prod, em.cons

    st = ExitStack()
    with st:
        AB = st.enter_context(nc.sbuf_tensor("AB", [128, 70400], BF16))
        AF_ = st.enter_context(nc.sbuf_tensor("AFA", [128, 16064], F32))
        i32b = st.enter_context(nc.sbuf_tensor("i32b", [128, 512], I32))

        R_A, R_B, R_C, R_D, R_E = 0, 16384, 32768, 49152, 65792

        def ab(off, n):
            return AB[:, off:off + n]

        F_COS, F_SIN = 0, 2048
        F_S1, F_S2, F_S4 = 4096, 6144, 8192
        F_MSB = 10240
        F_WFS = 12288
        S_STATS = 13312
        S_S1COL = 13376
        S_R1COL = 13392
        S_MAXQ = 13408
        S_MAXK = 13424
        S_BIASM = 13440
        S_WSUMS = 13456
        S_WSC = 13590
        S_S2SC = 13594
        S_FSC = 13610
        S_TMX = 13626
        S_S2ROW = 13632
        F_WABS = 13632
        F_WRS1 = 14848
        F_WRS2 = 15360
        F_YT = 0
        F_CONV0 = 8192
        F_CONV1 = 8448
        F_FOLD = 8704
        F_XNC3 = 8960
        F_OUTC = 11008

        def af(off, n):
            return AF_[:, off:off + n]

        cos_t = af(F_COS, 2048)
        sin_t = af(F_SIN, 2048)
        tri_t = st.enter_context(nc.sbuf_tensor("tri_t", [128, 128], BF16))
        id_t = st.enter_context(nc.sbuf_tensor("id_t", [128, 128], F32))
        onesf_t = st.enter_context(nc.sbuf_tensor("onesf_t", [128, 128], F32))
        ones2_t = st.enter_context(nc.sbuf_tensor("ones2_t", [128, 2], F32))
        sel_t = st.enter_context(nc.sbuf_tensor("sel_t", [2, 256], F32))
        sm_t = st.enter_context(nc.sbuf_tensor("sm_t", [128, 8], F32))
        qn_t = st.enter_context(nc.sbuf_tensor("qn_t", [2, 128], F32))
        kn_t = st.enter_context(nc.sbuf_tensor("kn_t", [2, 128], F32))
        taps_t = st.enter_context(nc.sbuf_tensor("taps_t", [128, 96], F32))
        convb_t = st.enter_context(nc.sbuf_tensor("convb_t", [128, 32], F32))

        PSQ = st.enter_context(nc.psum_tensor("psq", [128, 2048], F32))
        PS4 = st.enter_context(nc.psum_tensor("ps4", [128, 512], F32))
        PS5 = st.enter_context(nc.psum_tensor("ps5", [128, 512], F32))
        PSB = st.enter_context(nc.psum_tensor("psb", [128, 1024], F32))
        PS = [PSQ[:, 512 * i:512 * (i + 1)] for i in range(4)] + [PS4[:], PS5[:]]

        def psq(i, p0, p1, c0, c1):
            return PSQ[p0:p1, 512 * i + c0:512 * i + c1]

        sems = {n: st.enter_context(nc.semaphore(n))
                for n in ("sld", "gld", "tsp", "ct", "cv", "cs", "cg",
                          "ts0", "ts1", "ts2", "xb0", "xb1", "ab0", "ab1",
                          "ws0", "ws1", "sb0", "sb1", "sb2", "sb3",
                          "wo", "wq", "dac", "sg")}

        def dma(eng, out, in_, waits=(), sem=None):
            sem = sem or ("gld" if eng == "gpsimd" else "sld")
            E(eng, lambda e, o=out, i=in_: e.dma_start(out=o, in_=i), waits, sem, 16)
            return sem

        def tap(name, src_ap):
            if name not in DEBUG_TAPS:
                return
            dma("sync", dbg[name], src_ap,
                waits=[("cv", A("cv")), ("cs", A("cs")), ("ct", A("ct")),
                       ("gld", A("gld")), ("tsp", A("tsp"))])
            bar = A("sld")
            for eng, s in (("tensor", "ct"), ("vector", "cv"), ("scalar", "cs")):
                E(eng, lambda e: e.nop(), [("sld", bar)], s, 1)

        for o, i in [(cos_t, dcos), (sin_t, dsin), (tri_t[:], dtri), (id_t[:], dident),
                     (onesf_t[:], donesf), (ones2_t[:], dones2), (sel_t[:], dsel01),
                     (sm_t[:], dsmalls), (qn_t[:], dqn), (kn_t[:], dkn),
                     (taps_t[:], dtaps), (convb_t[:], dconvb)]:
            dma("sync", o, i)
        consts_ld = A("sld")
        for eng in ("tensor", "vector", "scalar"):
            E(eng, lambda e: e.nop(), [("sld", consts_ld)],
              {"tensor": "ct", "vector": "cv", "scalar": "cs"}[eng], 1)
        for c in range(NCC):
            dma("gpsimd", ab(R_B + 1024 * c, 1024), dwvT[128 * c:128 * (c + 1), :])
        wv_ld = A("gld")

        xnf = ab(R_A, 16384)
        for i in range(NTC):
            par = i % 2
            xbuf = af(F_S1 + 1024 * par, 1024)
            nbuf = af(F_S4 + 1024 * par, 1024)
            sqs = af(F_MSB, 1024)
            nb16 = ab(R_B + 8192 + 1024 * par, 1024)
            qb16 = ab(R_B + 10240 + 1024 * par, 1024)
            dma("sync", xbuf, dx[128 * i:128 * (i + 1), :],
                waits=free(("xbuf", par)))
            prod(("xbuf", par), "sld")
            E("vector", lambda e, xb=xbuf: e.reduce_sum(
                out=af(S_STATS + 0, 1), in_=xb, axis=AX.X),
              need(("xbuf", par)), "cv", 1)
            E("scalar", lambda e, xb=xbuf, sqs=sqs: e.activation(
                out=sqs, in_=xb, func=AF.Square, accum_out=af(S_STATS + 1, 1)),
              need(("xbuf", par)), "cs", 1)
            cons(("xbuf", par), "cs")
            sq_ev = ("cs", A("cs"))
            E("vector", lambda e: e.tensor_scalar(
                out=af(S_STATS + 2, 1), in0=af(S_STATS + 0, 1), scalar1=1.0 / C,
                scalar2=None, op0=ALU.mult), [("cv", A("cv"))], "cv", 1)
            E("vector", lambda e: e.scalar_tensor_tensor(
                out=af(S_STATS + 3, 1), in0=af(S_STATS + 2, 1), scalar=-1.0,
                in1=af(S_STATS + 2, 1), op0=ALU.mult, op1=ALU.mult),
              [("cv", A("cv"))], "cv", 1)
            E("vector", lambda e: e.scalar_tensor_tensor(
                out=af(S_STATS + 3, 1), in0=af(S_STATS + 1, 1), scalar=1.0 / C,
                in1=af(S_STATS + 3, 1), op0=ALU.mult, op1=ALU.add),
              [("cv", A("cv")), sq_ev], "cv", 1)
            E("scalar", lambda e: e.activation(
                out=af(S_STATS + 3, 1), in_=af(S_STATS + 3, 1), func=AF.Sqrt,
                bias=sm_t[:, 5:6]), [("cv", A("cv"))], "cs", 1)
            E("vector", lambda e: e.reciprocal(
                out=af(S_STATS + 3, 1), in_=af(S_STATS + 3, 1)),
              [("cs", A("cs"))], "cv", 1)
            E("vector", lambda e: e.scalar_tensor_tensor(
                out=af(S_STATS + 2, 1), in0=af(S_STATS + 2, 1), scalar=-1.0,
                in1=af(S_STATS + 3, 1), op0=ALU.mult, op1=ALU.mult),
              [("cv", A("cv"))], "cv", 1)
            E("vector", lambda e, xb=xbuf, nb=nbuf: e.tensor_scalar(
                out=nb, in0=xb, scalar1=af(S_STATS + 3, 1),
                scalar2=af(S_STATS + 2, 1), op0=ALU.mult, op1=ALU.add),
              [("cv", A("cv"))] + free(("nbuf", par)), "cv", 1)
            cons(("xbuf", par), "cv")
            prod(("nbuf", par), "cv")
            E("vector", lambda e, nb=nbuf, o=nb16: e.tensor_copy(out=o, in_=nb),
              free(("nb16", par)), "cv", 1)
            cons(("nbuf", par), "cv")
            prod(("nb16", par), "cv")
            E("sync", lambda e, i=i, src=nb16: e.dma_start_transpose(
                out=xnf.rearrange("p (c t) -> p c t", c=NCC)
                    [:, :, 128 * i:128 * (i + 1)],
                in_=src), need(("nb16", par)) + [("tsp", A("tsp"))], "tsp", 16)
            cons(("nb16", par), "tsp")
            E("vector", lambda e, nb=nbuf, i=i: e.reduce_max(
                out=af(S_S1COL + i, 1), in_=nb, axis=AX.X, apply_absolute_value=True),
              [], "cv", 1)
            E("vector", lambda e, i=i: e.tensor_scalar(
                out=af(S_R1COL + i, 1), in0=af(S_S1COL + i, 1), scalar1=1e-5,
                scalar2=1.0 / 127.0, op0=ALU.max, op1=ALU.mult),
              [("cv", A("cv"))], "cv", 1)
            E("vector", lambda e, i=i: e.reciprocal(
                out=af(S_S1COL + i, 1), in_=af(S_R1COL + i, 1)),
              [("cv", A("cv"))], "cv", 1)
            for hh in (0, 1):
                E("vector", lambda e, nb=nbuf, i=i, hh=hh: e.tensor_scalar(
                    out=i32b[:], in0=nb[:, 512 * hh:512 * (hh + 1)],
                    scalar1=af(S_S1COL + i, 1), scalar2=None,
                    op0=ALU.mult), [("cv", A("cv"))], "cv", 1)
                E("vector", lambda e, i=i, o=qb16, hh=hh: e.tensor_scalar(
                    out=o[:, 512 * hh:512 * (hh + 1)], in0=i32b[:],
                    scalar1=af(S_R1COL + i, 1), scalar2=None,
                    op0=ALU.mult),
                  [("cv", A("cv"))] + (free(("qb16", par)) if hh == 0 else []),
                  "cv", 1)
            cons(("nbuf", par), "cv")
            prod(("qb16", par), "cv")
            dma("gpsimd", dxqs[128 * i:128 * (i + 1), :], qb16,
                waits=need(("qb16", par)))
            cons(("qb16", par), "gld")
        p0_tsp = A("tsp")
        p0_gld = A("gld")
        tap("xnf", ab(R_A, 16384))

        wjobs = []
        for c in range(NCC):
            for b4 in range(8):
                wjobs.append(dw1T[128 * c:128 * (c + 1), 512 * b4:512 * (b4 + 1)])
        for c in range(NHC):
            for b2 in range(2):
                wjobs.append(dw2T[128 * c:128 * (c + 1), 512 * b2:512 * (b2 + 1)])
        WBUFS = [F_WABS, F_WRS1, F_WRS2]
        wj_state = {"next": 0}

        def emit_wjobs(n):
            for _ in range(n):
                j = wj_state["next"]
                if j >= len(wjobs) + 3:
                    return
                r = j - 3
                if 0 <= r < len(wjobs):
                    E("vector", lambda e, b=af(WBUFS[r % 3], 512), r=r:
                        e.reduce_sum(out=af(S_WSUMS + r, 1), in_=b, axis=AX.X,
                                     apply_absolute_value=True),
                      need(("wbuf", r % 3)), "cv", 1)
                    cons(("wbuf", r % 3), "cv")
                if j < len(wjobs):
                    dma("gpsimd", af(WBUFS[j % 3], 512), wjobs[j],
                        waits=free(("wbuf", j % 3)))
                    prod(("wbuf", j % 3), "gld")
                wj_state["next"] = j + 1

        vt = ab(R_D, 16640)
        E("vector", lambda e: e.memset(
            vt.rearrange("p (j h o) -> p j h o", j=NTC, h=H)[:, :, :, 64:65], 1.0),
          [], "cv", 1)
        vones_cv = A("cv")
        for j in range(NTC):
            for c in range(NCC):
                for sp in range(2):
                    w0 = ([("gld", wv_ld), ("tsp", p0_tsp)] + free(("psb",))
                          if (c, sp) == (0, 0) else [])
                    E("tensor", lambda e, j=j, c=c, sp=sp: e.matmul(
                        PSB[:, 512 * sp:512 * (sp + 1)],
                        xnf[:, T * c + 128 * j:T * c + 128 * (j + 1)],
                        ab(R_B + 1024 * c + 512 * sp, 512),
                        start=(c == 0), stop=(c == NCC - 1)), w0, "ct", 1)
            prod(("psb",), "ct")
            E("vector", lambda e, j=j: e.tensor_copy(
                out=ab(R_D + 1040 * j, 1040).rearrange(
                    "p (h o) -> p h o", o=65)[:, :, 0:64],
                in_=PSB[:].rearrange("p (h o) -> p h o", h=H)),
              need(("psb",)), "cv", 1)
            cons(("psb",), "cv")
        p1v_cv = A("cv")
        tap("vt", ab(R_D, 16640))

        qhat = ab(R_B, 16384)
        khat = ab(R_C, 16384)
        s1 = af(F_S1, 2048)
        s2b = af(F_S2, 2048)
        s4 = af(F_S4, 2048)

        P1 = [dict() for _ in range(16)]

        def p1_qkv(oc):
            sec = ab(R_E + 1024 * (oc % 2), 1024)
            dma("gpsimd", sec.rearrange("p (c n) -> p c n", c=NCC),
                dwqkT[:, 128 * oc:128 * (oc + 1)].rearrange("(c p) n -> p c n",
                                                            c=NCC),
                waits=free(("sec", oc % 2)))
            prod(("sec", oc % 2), "gld")
            for c in range(NCC):
                for sp in range(4):
                    w0 = (need(("sec", oc % 2)) + [("cv", p1v_cv)]
                          + free(("qk",))
                          if (c, sp) == (0, 0) else [])
                    E("tensor", lambda e, c=c, sp=sp, sec=sec: e.matmul(
                        PS[sp], sec[:, 128 * c:128 * (c + 1)],
                        xnf[:, T * c + 512 * sp:T * c + 512 * (sp + 1)],
                        start=(c == 0), stop=(c == NCC - 1)), w0, "ct", 1)
            prod(("qk",), "ct")
            cons(("sec", oc % 2), "ct")

        def p1_nonpe(oc):
            isq = oc < 8
            ch = oc % 8
            dst = qhat if isq else khat
            mxoff = S_MAXQ if isq else S_MAXK
            st_ = P1[oc]
            st_["c_ms"] = [Cell() for _ in range(4)]
            st_["c_wrs"] = [Cell() for _ in range(4)]
            st_["c_nsq"] = [Cell() for _ in range(4)]
            E("scalar", lambda e: e.copy(out=s1, in_=PSQ[:]),
              need(("qk",)) + free(("s1",)), "cs", 1)
            cons(("qk",), "cs")
            prod(("s1",), "cs")
            zfree = (P1[oc - 1]["c_nsq"][3:] if oc >= 1 else []) + free(("s4",))
            E("scalar", lambda e: e.activation(out=s4, in_=PSQ[:], func=AF.Square),
              need(("qk",)) + zfree, "cs", 1)
            cons(("qk",), "cs")
            prod(("s4",), "cs")
            st_["sq_ev"] = ("cs", A("cs"))
            for (a, b) in ((0, 32), (32, 0), (64, 96), (96, 64)):
                dma("sync", AF_[a:a + 32, F_S2:F_S2 + 2048],
                    AF_[b:b + 32, F_S1:F_S1 + 2048],
                    waits=need(("s1",)) + free(("s2b",)) +
                    ([("tsp", p0_tsp)] if oc == 0 else []))
            cons(("s1",), "sld")
            prod(("s2b",), "sld")
            E("gpsimd", lambda e: e.tensor_mul(out=s2b, in0=s2b, in1=sin_t),
              need(("s2b",)) + ([("sld", consts_ld)] if oc == 0 else []), "cg", 1)
            prod(("s2b",), "cg")
            for sp in range(4):
                msfree = (free(("msb",)) +
                          ([P1[oc - 1]["c_wrs"][3]] if oc >= 1 else [])
                          if sp == 0 else [])
                E("vector", lambda e, sp=sp: e.tensor_copy(
                    out=AF_[0:2, F_MSB + 512 * sp:F_MSB + 512 * (sp + 1)],
                    in_=PS4[0:2, :]),
                  [st_["c_ms"][sp]] + msfree, "cv", 1)
                st_["msevac_ev_%d" % sp] = ("cv", A("cv"))
            prod(("msb",), "cv")
            st_["msevac3"] = ("cv", A("cv"))
            E("scalar", lambda e: e.activation(
                out=AF_[0:2, F_MSB:F_MSB + 2048], in_=AF_[0:2, F_MSB:F_MSB + 2048],
                func=AF.Ln, scale=1.0 / D, bias=sm_t[0:2, 2:3]),
              need(("msb",)), "cs", 1)
            E("scalar", lambda e: e.activation(
                out=AF_[0:2, F_MSB:F_MSB + 2048], in_=AF_[0:2, F_MSB:F_MSB + 2048],
                func=AF.Exp, scale=-0.5), [], "cs", 1)
            prod(("msb",), "cs")
            st_["rs_ev"] = ("cs", A("cs"))
            E("vector", lambda e: e.tensor_mul(out=s4, in0=s1, in1=cos_t),
              need(("s1",)) + [st_["c_ms"][3]] + free(("s4",)), "cv", 1)
            cons(("s1",), "cv")
            prod(("s4",), "cv")
            E("vector", lambda e: e.tensor_add(out=s2b, in0=s4, in1=s2b),
              need(("s2b",), ("s4",)), "cv", 1)
            cons(("s4",), "cv")
            prod(("s2b",), "cv")
            for sp in range(4):
                E("vector", lambda e, sp=sp, dst=dst, ch=ch: e.tensor_mul(
                    out=dst[:, T * ch + 512 * sp:T * ch + 512 * (sp + 1)],
                    in0=s2b[:, 512 * sp:512 * (sp + 1)],
                    in1=PSB[:, 512 * (sp % 2):512 * (sp % 2) + 512]),
                  [st_["c_wrs"][sp]] + need(("s2b",)), "cv", 1)
                st_["tt_ev_%d" % sp] = ("cv", A("cv"))
                cons(("s2b",), "cv")
            st_["qh_ev"] = ("cv", A("cv"))
            E("scalar", lambda e, dst=dst, ch=ch: e.activation(
                out=s4, in_=dst[:, T * ch:T * (ch + 1)], func=AF.Square),
              [st_["qh_ev"]] + free(("s4",)), "cs", 1)
            prod(("s4",), "cs")
            st_["nsqsq_ev"] = ("cs", A("cs"))
            for sp in range(4):
                E("vector", lambda e, sp=sp: e.reduce_max(
                    out=AF_[0:2, S_TMX + sp:S_TMX + sp + 1],
                    in_=PS5[0:2, :], axis=AX.X), [st_["c_nsq"][sp]], "cv", 1)
                st_["nsqred_ev_%d" % sp] = ("cv", A("cv"))
            E("vector", lambda e, mxoff=mxoff, ch=ch: e.reduce_max(
                out=AF_[0:2, mxoff + ch:mxoff + ch + 1],
                in_=AF_[0:2, S_TMX:S_TMX + 4], axis=AX.X),
              [("cv", A("cv"))], "cv", 1)
            st_["done_cv"] = ("cv", A("cv"))

        def p1_pesmalls(oc):
            isq = oc < 8
            nstat = qn_t if isq else kn_t
            st_ = P1[oc]
            pst = P1[oc - 1] if oc >= 1 else None
            for sp in range(4):
                w0 = [st_["sq_ev"]]
                if sp > 0:
                    w0.append(st_["msevac_ev_%d" % (sp - 1)])
                elif pst is not None:
                    w0.append(pst["msevac_ev_3"])
                E("tensor", lambda e, sp=sp: e.matmul(
                    PS4[0:2, :], ones2_t[:], s4[:, 512 * sp:512 * (sp + 1)],
                    start=True, stop=True), w0, "ct", 1)
                em.fill(st_["c_ms"][sp], "ct")
            for sp in range(4):
                w0 = [st_["rs_ev"]]
                if sp >= 2:
                    w0.append(st_["tt_ev_%d" % (sp - 2)])
                elif pst is not None:
                    w0.append(pst["tt_ev_%d" % (sp + 2)])
                else:
                    w0 += free(("psb",))
                E("tensor", lambda e, sp=sp, nstat=nstat: e.matmul(
                    PSB[:, 512 * (sp % 2):512 * (sp % 2) + 512], nstat[:],
                    AF_[0:2, F_MSB + 512 * sp:F_MSB + 512 * (sp + 1)],
                    start=True, stop=True), w0, "ct", 1)
                em.fill(st_["c_wrs"][sp], "ct")
            for sp in range(4):
                w0 = [st_["nsqsq_ev"]]
                if sp > 0:
                    w0.append(st_["nsqred_ev_%d" % (sp - 1)])
                elif pst is not None:
                    w0.append(pst["nsqred_ev_3"])
                E("tensor", lambda e, sp=sp: e.matmul(
                    PS5[0:2, :], ones2_t[:], s4[:, 512 * sp:512 * (sp + 1)],
                    start=True, stop=True), w0, "ct", 1)
                em.fill(st_["c_nsq"][sp], "ct")

        for oc in range(16):
            p1_qkv(oc)
            if oc >= 1:
                p1_pesmalls(oc - 1)
            p1_nonpe(oc)
        p1_pesmalls(15)
        p1a_cv = A("cv")


        E("scalar", lambda e: e.activation(
            out=AF_[0:2, S_MAXQ:S_MAXQ + 8], in_=AF_[0:2, S_MAXQ:S_MAXQ + 8],
            func=AF.Sqrt), [("cv", p1a_cv)], "cs", 1)
        E("scalar", lambda e: e.activation(
            out=AF_[0:2, S_MAXK:S_MAXK + 8], in_=AF_[0:2, S_MAXK:S_MAXK + 8],
            func=AF.Sqrt), [], "cs", 1)
        E("vector", lambda e: e.tensor_mul(
            out=AF_[0:2, S_MAXQ:S_MAXQ + 8], in0=AF_[0:2, S_MAXQ:S_MAXQ + 8],
            in1=AF_[0:2, S_MAXK:S_MAXK + 8]), [("cs", A("cs"))], "cv", 1)
        E("vector", lambda e: e.tensor_scalar(
            out=AF_[0:2, S_MAXQ:S_MAXQ + 8], in0=AF_[0:2, S_MAXQ:S_MAXQ + 8],
            scalar1=-0.125, scalar2=None, op0=ALU.mult),
          [("cv", A("cv"))], "cv", 1)
        negM_cv = A("cv")
        for h in range(H):
            r, c = h % 2, h // 2
            E("tensor", lambda e, r=r, c=c, h=h: e.matmul(
                PS4[:, h:h + 1], sel_t[:, 128 * r:128 * (r + 1)],
                AF_[0:2, S_MAXQ + c:S_MAXQ + c + 1], start=True, stop=True),
              [("cv", negM_cv)] + (free(("ps4",)) if h == 0 else []), "ct", 1)
        prod(("ps4",), "ct")
        E("vector", lambda e: e.tensor_copy(
            out=af(S_BIASM, 16), in_=PS4[:, 0:16]), need(("ps4",)), "cv", 1)
        cons(("ps4",), "cv")
        biasM_cv = A("cv")
        tap("qhat", ab(R_B, 16384))
        tap("khat", ab(R_C, 16384))

        pt_off = []
        off = 0
        for j in range(8):
            pt_off.append((R_A, off))
            off += T - 128 * j
        off = 0
        for j in range(8, 16):
            pt_off.append((R_E, off))
            off += T - 128 * j

        def pt_ap(j, col0, width):
            base, o = pt_off[j]
            return ab(base + o + col0, width)

        for pair in range(8):
            for parx in range(2):
                h = 2 * pair + parx
                base = 64 * parx
                ptev = {}
                for j in range(NTC):
                    qt0 = 128 * j
                    nsp = (T - qt0 + 511) // 512
                    for spi in range(nsp):
                        c0 = qt0 + 512 * spi
                        w = min(512, T - c0)
                        bnk = ("sb", (j + spi) % 4)
                        bank = PS[(j + spi) % 4]
                        w0 = free(bnk) + ([("cv", biasM_cv)] if h == 0 else [])
                        E("tensor", lambda e, j=j, c0=c0, w=w, bank=bank, base=base,
                            pair=pair: e.matmul(
                            bank[:, 0:w],
                            khat[base:base + 64,
                                 T * pair + 128 * j:T * pair + 128 * (j + 1)],
                            qhat[base:base + 64, T * pair + c0:T * pair + c0 + w],
                            start=True, stop=True), w0, "ct", 1)
                        prod(bnk, "ct")
                        E("scalar", lambda e, j=j, c0=c0, w=w, bank=bank, qt0=qt0,
                            h=h: e.activation(
                            out=pt_ap(j, c0 - qt0, w), in_=bank[:, 0:w],
                            func=AF.Exp, scale=0.125, bias=af(S_BIASM + h, 1)),
                          need(bnk) +
                          (free(("pt", j)) + [("cv", biasM_cv)] if spi == 0 else []),
                          "cs", 1)
                        cons(bnk, "cs")
                    exp_ev = ("cs", A("cs"))
                    E("vector", lambda e, j=j: e.tensor_mul(
                        out=pt_ap(j, 0, 128), in0=pt_ap(j, 0, 128), in1=tri_t[:]),
                      [exp_ev], "cv", 1)
                    ptev[j] = [exp_ev, ("cv", A("cv"))]
                    prod(("pt", j), "cv")
                for s in range(4):
                    avb = ("avb", s % 2)
                    avbank = PS[4 + (s % 2)]
                    njs = 4 * s + 4
                    for j in range(njs):
                        coff = 512 * s - 128 * j
                        c0 = max(0, coff)
                        pcol = max(0, -coff)
                        w = 512 - pcol
                        w0 = list(ptev[j]) + (free(avb) if j == 0 else [])
                        E("tensor", lambda e, j=j, c0=c0, pcol=pcol, w=w,
                            avbank=avbank, h=h, njs=njs: e.matmul(
                            avbank[0:65, pcol:pcol + w],
                            vt[:, 1040 * j + 65 * h:1040 * j + 65 * h + 65],
                            pt_ap(j, c0, w),
                            start=(j == 0), stop=(j == njs - 1)),
                          w0 + ([("cv", vones_cv)] if h == 0 and s == 0 else []),
                          "ct", 1)
                        cons(("pt", j), "ct")
                    prod(avb, "ct")
                    E("scalar", lambda e, s=s, parx=parx, avbank=avbank: e.copy(
                        out=AF_[0:65, F_S1 + 2048 * parx + 512 * s:
                                F_S1 + 2048 * parx + 512 * (s + 1)],
                        in_=avbank[0:65, :]),
                      need(avb) + (free(("ascr", parx)) if s == 0 else []), "cs", 1)
                    cons(avb, "cs")
                prod(("ascr", parx), "cs")
            for parx in range(2):
                h = 2 * pair + parx
                E("scalar", lambda e, parx=parx: e.activation(
                    out=AF_[64:65, F_MSB:F_MSB + 2048],
                    in_=AF_[64:65, F_S1 + 2048 * parx:F_S1 + 2048 * parx + 2048],
                    func=AF.Ln),
                  need(("ascr", parx)) + free(("msb",)), "cs", 1)
                E("scalar", lambda e: e.activation(
                    out=AF_[64:65, F_MSB:F_MSB + 2048],
                    in_=AF_[64:65, F_MSB:F_MSB + 2048], func=AF.Exp, scale=-1.0),
                  [], "cs", 1)
                prod(("msb",), "cs")
                for s in range(4):
                    bnk = ("nb", s % 2)
                    bank = PSB[0:64, 512 * (s % 2):512 * (s % 2) + 512]
                    E("tensor", lambda e, s=s, bank=bank: e.matmul(
                        bank, onesf_t[64:65, 0:64],
                        AF_[64:65, F_MSB + 512 * s:F_MSB + 512 * (s + 1)],
                        start=True, stop=True),
                      need(("msb",)) + free(bnk), "ct", 1)
                    prod(bnk, "ct")
                    E("scalar", lambda e, s=s, bank=bank: e.copy(
                        out=AF_[0:64, F_MSB + 512 * s:F_MSB + 512 * (s + 1)],
                        in_=bank),
                      need(bnk) + (free(("rb",)) if s == 0 else []), "cs", 1)
                    cons(bnk, "cs")
                prod(("rb",), "cs")
                if parx == 0:
                    E("vector", lambda e, pair=pair: e.tensor_mul(
                        out=AB[0:64, R_B + T * pair:R_B + T * (pair + 1)],
                        in0=AF_[0:64, F_S1:F_S1 + 2048],
                        in1=AF_[0:64, F_MSB:F_MSB + 2048]),
                      need(("rb",), ("ascr", 0)), "cv", 1)
                    cons(("rb",), "cv")
                    cons(("ascr", 0), "cv")
                else:
                    E("vector", lambda e: e.tensor_mul(
                        out=AB[0:64, R_A + 12800:R_A + 12800 + 2048],
                        in0=AF_[0:64, F_S1 + 2048:F_S1 + 4096],
                        in1=AF_[0:64, F_MSB:F_MSB + 2048]),
                      need(("rb",), ("ascr", 1)) + free(("oscr",)), "cv", 1)
                    cons(("rb",), "cv")
                    cons(("ascr", 1), "cv")
                    prod(("oscr",), "cv")
                    E("sync", lambda e, pair=pair: e.dma_start(
                        out=AB[64:128, R_B + T * pair:R_B + T * (pair + 1)],
                        in_=AB[0:64, R_A + 12800:R_A + 12800 + 2048]),
                      need(("oscr",)), "sld", 16)
                    cons(("oscr",), "sld")
            emit_wjobs(17)
        emit_wjobs(200)
        tap("attnout", ab(R_B, 16384))
        attn_cv = A("cv")
        attn_cs = A("cs")
        attn_sld = A("sld")
        attn_ct = A("ct")
        attnout = qhat

        XQW = 2052
        xqvf = ab(R_D, NCC * XQW)
        E("vector", lambda e: e.memset(
            xqvf.rearrange("p (c t) -> p c t", c=NCC)[:, :, 0:1], 0.0),
          [("cv", attn_cv), ("cs", attn_cs), ("ct", attn_ct), ("sld", attn_sld)],
          "cv", 1)
        E("vector", lambda e: e.memset(
            xqvf.rearrange("p (c t) -> p c t", c=NCC)[:, :, 2049:2052], 0.0),
          [], "cv", 1)
        guards_cv = A("cv")
        for c in range(NCC):
            dma("gpsimd", ab(R_C + 1024 * c, 1024), dwoutT[128 * c:128 * (c + 1), :],
                waits=[("ct", attn_ct), ("cs", attn_cs)], sem="wo")
        wout_ld = A("wo")

        E("vector", lambda e: e.reduce_sum(
            out=af(S_WSUMS + 128, 1), in_=af(S_WSUMS, 64), axis=AX.X),
          [("cv", A("cv"))], "cv", 1)
        E("vector", lambda e: e.reduce_sum(
            out=af(S_WSUMS + 129, 1), in_=af(S_WSUMS + 64, 64), axis=AX.X),
          [("cv", A("cv"))], "cv", 1)
        wred_cv = A("cv")
        for k in range(2):
            E("tensor", lambda e, k=k: e.matmul(
                PS4[0:1, k:k + 1], onesf_t[:, 0:1], af(S_WSUMS + 128 + k, 1),
                start=True, stop=True),
              [("cv", wred_cv), ("cs", attn_cs)] +
              (free(("ps4",), ("ps45",)) if k == 0 else []), "ct", 1)
        prod(("ps4",), "ct")
        E("vector", lambda e: e.tensor_copy(
            out=AF_[0:1, S_WSUMS + 130:S_WSUMS + 132], in_=PS4[0:1, 0:2]),
          need(("ps4",)), "cv", 1)
        cons(("ps4",), "cv")
        E("vector", lambda e: e.tensor_scalar(
            out=AF_[0:1, S_WSUMS + 130:S_WSUMS + 132],
            in0=AF_[0:1, S_WSUMS + 130:S_WSUMS + 132],
            scalar1=1.0 / (C * HID), scalar2=1e-5, op0=ALU.mult, op1=ALU.max),
          [("cv", A("cv"))], "cv", 1)
        E("vector", lambda e: e.reciprocal(
            out=AF_[0:1, S_WSUMS + 132:S_WSUMS + 134],
            in_=AF_[0:1, S_WSUMS + 130:S_WSUMS + 132]),
          [("cv", A("cv"))], "cv", 1)
        mw_cv = A("cv")
        E("tensor", lambda e: e.matmul(
            PS4[:, 0:4], onesf_t[0:1, 0:128],
            AF_[0:1, S_WSUMS + 130:S_WSUMS + 134], start=True, stop=True),
          [("cv", mw_cv)] + free(("ps4",)), "ct", 1)
        prod(("ps4",), "ct")
        E("vector", lambda e: e.tensor_copy(out=af(S_WSC, 4), in_=PS4[:, 0:4]),
          need(("ps4",)), "cv", 1)
        cons(("ps4",), "cv")
        wsc_cv = A("cv")


        def p25_tp(i):
            spar = i % 3
            src = ab(R_E + 1024 * spar, 1024)
            dma("sync", src, dxqs[128 * i:128 * (i + 1), :],
                waits=free(("tsrc", spar)) +
                ([("gld", p0_gld), ("sld", attn_sld), ("ct", attn_ct),
                  ("cs", attn_cs), ("cv", attn_cv)] if i == 0 else []),
                sem="ts%d" % spar)
            prod(("tsrc", spar), "ts%d" % spar)
            w32 = af(F_S4 + 1024 * (i % 2), 1024)
            E("scalar", lambda e, src=src, w32=w32: e.activation(
                out=w32, in_=src, func=AF.Copy),
              need(("tsrc", spar)) + free(("w32", i % 2)), "cs", 1)
            cons(("tsrc", spar), "cs")
            prod(("w32", i % 2), "cs")
            for c in range(NCC):
                tb = ("tpb", c % 2)
                w0 = (need(("w32", i % 2)) if c == 0 else []) + free(tb) + \
                    ([("cs", attn_cs), ("ct", attn_ct)] if i == 0 and c < 2 else [])
                E("tensor", lambda e, c=c, w32=w32: e.transpose(
                    psq(c % 2, 0, 128, 0, 128), w32[:, 128 * c:128 * (c + 1)],
                    id_t[:]), w0, "ct", 1)
                prod(tb, "ct")
                if c == NCC - 1:
                    cons(("w32", i % 2), "ct")
                dst = xqvf[:, XQW * c + 1 + 128 * i:XQW * c + 1 + 128 * (i + 1)]
                if c % 2 == 0:
                    E("scalar", lambda e, c=c, dst=dst: e.activation(
                        out=dst, in_=psq(c % 2, 0, 128, 0, 128), func=AF.Copy),
                      need(tb) + ([("cv", guards_cv)] if i == 0 else []), "cs", 1)
                    cons(tb, "cs")
                else:
                    E("vector", lambda e, c=c, dst=dst: e.tensor_copy(
                        out=dst, in_=psq(c % 2, 0, 128, 0, 128)),
                      need(tb) + ([("cv", guards_cv)] if i == 0 else []), "cv", 1)
                    cons(tb, "cv")

        def p25_proj(i):
            par = i % 2
            xbuf = af(F_MSB + 1024 * par, 1024)
            obuf = af(F_S2 + 1024 * par, 1024)
            dma("sync", xbuf, dx[128 * i:128 * (i + 1), :],
                waits=free(("xre", par)) +
                ([("cv", attn_cv), ("cs", attn_cs)] if i < 2 else []),
                sem="xb%d" % par)
            prod(("xre", par), "xb%d" % par)
            pjr = ("psb",) if par == 0 else (("ps4",), ("ps45",))
            for c in range(NCC):
                for sp in range(2):
                    w0 = ([("wo", wout_ld), ("sld", attn_sld),
                           ("cv", attn_cv)] +
                          (free(("psb",)) if par == 0
                           else free(("ps4",), ("ps45",)))
                          if (c, sp) == (0, 0) else [])

                    def pjmm(e, i=i, c=c, sp=sp, par=par):
                        outap = (PSB[:, 512 * sp:512 * (sp + 1)] if par == 0
                                 else (PS4[:] if sp == 0 else PS5[:]))
                        return e.matmul(
                            outap,
                            attnout[:, T * c + 128 * i:T * c + 128 * (i + 1)],
                            ab(R_C + 1024 * c + 512 * sp, 512),
                            start=(c == 0), stop=(c == NCC - 1))
                    E("tensor", pjmm, w0, "ct", 1)
            if par == 0:
                prod(("psb",), "ct")
            else:
                prod(("ps4",), "ct")
                prod(("ps45",), "ct")

            def pjfin(e, xbuf=xbuf, obuf=obuf, par=par):
                if par == 0:
                    return e.scalar_tensor_tensor(
                        out=obuf, in0=PSB[:], scalar=1.0, in1=xbuf,
                        op0=ALU.mult, op1=ALU.add)
                e.scalar_tensor_tensor(
                    out=obuf[:, 0:512], in0=PS4[:], scalar=1.0,
                    in1=xbuf[:, 0:512], op0=ALU.mult, op1=ALU.add)
                return e.scalar_tensor_tensor(
                    out=obuf[:, 512:1024], in0=PS5[:], scalar=1.0,
                    in1=xbuf[:, 512:1024], op0=ALU.mult, op1=ALU.add)
            pw = (need(("psb",)) if par == 0 else need(("ps4",), ("ps45",))) + \
                need(("xre", par)) + free(("obuf", par))
            E("vector", pjfin, pw, "cv", 1)
            if par == 0:
                cons(("psb",), "cv")
            else:
                cons(("ps4",), "cv")
                cons(("ps45",), "cv")
            cons(("xre", par), "cv")
            prod(("obuf", par), "cv")
            dma("sync", daccs[128 * i:128 * (i + 1), :], obuf,
                waits=need(("obuf", par)), sem="dac")
            cons(("obuf", par), "dac")

        wsign2 = [(R_A + 1024 * hc) if hc < 16 else (R_B + 1024 * (hc - 16))
                  for hc in range(NHC)]
        w1q_g = {}
        fcnt = [0]
        cg_hw = A("cg")

        STRB = (14016, 14528, 15040, 15552)

        def item_w2(hc, h):
            k5 = fcnt[0] % 4
            first = fcnt[0] < 4
            fcnt[0] += 1
            buf = af(STRB[k5], 512)
            dma("gpsimd", buf, dw2T[128 * hc:128 * (hc + 1), 512 * h:512 * (h + 1)],
                waits=free(("fcos", k5)) +
                ([("cv", attn_cv), ("cs", attn_cs), ("sld", attn_sld),
                  ("cg", cg_hw)] if first else []), sem="sb%d" % k5)
            prod(("fcos", k5), "sb%d" % k5)
            E("vector", lambda e, buf=buf: e.tensor_scalar(
                out=i32b[:], in0=buf, scalar1=AF_[:, S_WSC + 3:S_WSC + 4],
                scalar2=None, op0=ALU.mult),
              need(("fcos", k5)) + ([("cv", wsc_cv)] if hc == 0 and h == 0
                                    else []), "cv", 1)
            cons(("fcos", k5), "cv")
            E("vector", lambda e, hc=hc, h=h: e.tensor_scalar(
                out=ab(wsign2[hc] + 512 * h, 512), in0=i32b[:],
                scalar1=-1.0, scalar2=1.0, op0=ALU.max, op1=ALU.min),
              [("sld", attn_sld), ("ct", attn_ct)] if hc == 0 and h == 0 else
              ([("ct", proj_ct), ("sld", proj_sld)] if hc == 16 and h == 0
               else []), "cv", 1)

        def item_w1(jj, h):
            b4, c = divmod(jj, 8)
            k5 = fcnt[0] % 4
            first = fcnt[0] < 4
            fcnt[0] += 1
            buf = af(STRB[k5], 512)
            dma("gpsimd", buf,
                dw1T[128 * c:128 * (c + 1),
                     1024 * b4 + 512 * h:1024 * b4 + 512 * (h + 1)],
                waits=free(("fcos", k5)) +
                ([("cv", attn_cv), ("cs", attn_cs), ("sld", attn_sld),
                  ("cg", cg_hw)] if first else []), sem="sb%d" % k5)
            prod(("fcos", k5), "sb%d" % k5)
            E("vector", lambda e, buf=buf: e.tensor_scalar(
                out=i32b[:], in0=buf, scalar1=AF_[:, S_WSC + 2:S_WSC + 3],
                scalar2=None, op0=ALU.mult), need(("fcos", k5)), "cv", 1)
            cons(("fcos", k5), "cv")
            bk = (2 * jj + h) % 8
            bout = ab(R_C + 8192 + 512 * bk, 512)
            E("vector", lambda e, bout=bout: e.tensor_scalar(
                out=bout, in0=i32b[:], scalar1=-1.0, scalar2=1.0,
                op0=ALU.max, op1=ALU.min),
              free(("bout", bk)) + ([("ct", attn_ct)] if 2 * jj + h < 8
                                    else []), "cv", 1)
            prod(("bout", bk), "cv")
            dma("sync",
                dw1q[128 * c:128 * (c + 1),
                     1024 * b4 + 512 * h:1024 * b4 + 512 * (h + 1)],
                bout, waits=need(("bout", bk)), sem="wq")
            cons(("bout", bk), "wq")
            if c == 7 and h == 1:
                w1q_g[b4] = A("wq")

        items = [(lambda hc=hc, h=h: item_w2(hc, h))
                 for hc in range(16) for h in range(2)] + \
                [(lambda jj=jj, h=h: item_w1(jj, h))
                 for jj in range(32) for h in range(2)]
        xqvf_cv = xqvf_cs = None
        for i in range(NTC):
            p25_tp(i)
            if i == NTC - 1:
                xqvf_cs = A("cs")
                xqvf_cv = A("cv")
            if i >= 2:
                p25_proj(i - 2)
            for fn in items[6 * i:6 * i + 6]:
                fn()
        tap("xqvf", xqvf)
        p25_proj(NTC - 2)
        p25_proj(NTC - 1)
        proj_sld = A("sld")
        proj_dac = A("dac")
        proj_ct = A("ct")
        for hc in range(16, 32):
            for h in range(2):
                item_w2(hc, h)
        w2q_cv = A("cv")

        yt2 = af(F_YT, 8192)

        def xq2b(ti):
            return ab(R_C + 8192 * (ti % 2), 8192)
        pend_silu = []
        pend_T = []
        pend_sT = []
        pend_BC = []
        pend_R = []
        rg_cells = [[Cell() for _ in range(4)] for _ in range(NFT)]
        s2bcs_ev = {}
        yt2ev = {}
        S2BCS = 14016

        def flush_R():
            if pend_R:
                grp = pend_R.pop(0)
                for fn in grp:
                    fn()

        def flush_T():
            while pend_T:
                qti, tw, cT = pend_T.pop(0)
                for k in range(2):
                    E("tensor", lambda e, k=k: e.transpose(
                        psq(2, 0, 128, 128 * k, 128 * (k + 1)),
                        af(F_FOLD + 128 * k, 128), id_t[:]),
                      (need(("fold",)) + tw) if k == 0 else [], "ct", 1)
                    em.fill(cT[k], "ct")
                cons(("fold",), "ct")

        def flush_sT():
            while pend_sT:
                qti, s2cv, fw, cT_s = pend_sT.pop(0)
                E("tensor", lambda e, qti=qti: e.transpose(
                    PS5[0:2, 0:128],
                    AF_[:, S_S2SC + 2 * qti:S_S2SC + 2 * qti + 2], id_t[:]),
                  [("cv", s2cv)] + fw, "ct", 1)
                em.fill(cT_s, "ct")

        def flush_BC():
            while pend_BC:
                qti, gat_sg, cBC = pend_BC.pop(0)
                E("tensor", lambda e: e.matmul(
                    PS4[:, 0:FT], onesf_t[64:65, 0:128],
                    AF_[64:65, S_S2ROW + 128:S_S2ROW + 128 + FT],
                    start=True, stop=True), [("sg", gat_sg)], "ct", 1)
                cons(("s2rg",), "ct")
                em.fill(cBC, "ct")

        def flush_silu():
            while pend_silu:
                fn, waits, key = pend_silu.pop(0)
                E("scalar", fn, waits, "cs", 1)
                yt2ev[key] = ("cs", A("cs"))

        def p3_h(ti):
            a = FT * ti
            hspan = FT + 2
            for hc in range(NHC):
                par = hc % 2
                p4 = hc % 4
                hbr = ("hb", p4)
                hb = PS[p4]
                sec = ab(R_E + 1024 * par, 1024)
                dma("sync", sec.rearrange("p (c n) -> p c n", c=NCC),
                    dw1q[:, 128 * hc:128 * (hc + 1)].rearrange(
                        "(c p) n -> p c n", c=NCC),
                    waits=[("wq", w1q_g[hc // 8]), ("cv", xqvf_cv),
                           ("cs", xqvf_cs)] + free(("w1sec", par)),
                    sem="ws%d" % par)
                prod(("w1sec", par), "ws%d" % par)
                for c in range(NCC):
                    w0 = (need(("w1sec", par)) + free(hbr) +
                          ([("cs", xqvf_cs), ("cv", xqvf_cv), ("ct", attn_ct)]
                           if ti == 0 and hc < 4 else [])
                          if c == 0 else [])
                    E("tensor", lambda e, a=a, hspan=hspan, c=c, hb=hb, sec=sec:
                        e.matmul(
                        hb[:, 0:hspan], sec[:, 128 * c:128 * (c + 1)],
                        xqvf[:, XQW * c + a:XQW * c + a + hspan],
                        start=(c == 0), stop=(c == NCC - 1)), w0, "ct", 1)
                prod(hbr, "ct")
                cons(("w1sec", par), "ct")
                csc = af(F_CONV0 + 256 * par, FT)
                cscr = ("csc", par)
                E("scalar", lambda e, hb=hb, hc=hc, csc=csc: e.activation(
                    out=csc, in_=hb[:, 2:2 + FT], func=AF.Copy,
                    scale=taps_t[:, 64 + hc:65 + hc]),
                  need(hbr) + free(cscr), "cs", 1)
                cons(hbr, "cs")
                prod(cscr, "cs")
                E("vector", lambda e, hb=hb, hc=hc, csc=csc: e.scalar_tensor_tensor(
                    out=csc, in0=hb[:, 0:FT], scalar=taps_t[:, hc:hc + 1],
                    in1=csc, op0=ALU.mult, op1=ALU.add),
                  need(cscr, hbr), "cv", 1)
                prod(cscr, "cv")
                E("vector", lambda e, hb=hb, hc=hc, csc=csc: e.scalar_tensor_tensor(
                    out=csc, in0=hb[:, 1:1 + FT], scalar=taps_t[:, 32 + hc:33 + hc],
                    in1=csc, op0=ALU.mult, op1=ALU.add),
                  need(cscr), "cv", 1)
                cons(hbr, "cv")
                prod(cscr, "cv")
                silu_waits = (need(cscr) + [("cv", wsc_cv)] +
                              ([rg_cells[ti - 1][hc // 8]] if ti >= 1 else []))
                pend_silu.append((
                    (lambda e, hc=hc, csc=csc: e.activation(
                        out=yt2[:, FT * hc:FT * (hc + 1)], in_=csc, func=AF.Silu,
                        scale=AF_[:, S_WSC:S_WSC + 1], bias=convb_t[:, hc:hc + 1])),
                    silu_waits, ti))
                cons(cscr, "cs")
                if len(pend_silu) > 1:
                    fn, waits, key = pend_silu.pop(0)
                    E("scalar", fn, waits, "cs", 1)
                    yt2ev[key] = ("cs", A("cs"))
                if hc == 1:
                    flush_T()
                    flush_R()
                elif hc == 2:
                    flush_sT()
                elif hc == 3:
                    flush_BC()
                elif hc in (5, 9, 13):
                    flush_R()

        def p3_quant(ti):
            flush_silu()
            xq2 = xq2b(ti)
            sev = yt2ev[ti]
            E("vector", lambda e: e.tensor_reduce(
                out=af(F_FOLD, FT), in_=yt2.rearrange("p (c t) -> p t c", c=NHC),
                axis=AX.X, op=ALU.max, apply_absolute_value=True),
              [sev] + free(("fold",)), "cv", 1)
            prod(("fold",), "cv")
            cT = [Cell(), Cell()]
            pend_T.append((ti, free(("hb", 2)), cT))
            for k in range(2):
                gcol = 2 * ti + k
                E("vector", lambda e, k=k: e.reduce_max(
                    out=af(S_TMX + k, 1),
                    in_=psq(2, 0, 128, 128 * k, 128 * (k + 1)), axis=AX.X),
                  [cT[k]], "cv", 1)
                cons(("hb", 2), "cv")
                E("vector", lambda e, k=k, gcol=gcol: e.tensor_scalar(
                    out=af(S_FSC + gcol, 1), in0=af(S_TMX + k, 1), scalar1=1e-5,
                    scalar2=1.0 / 127.0, op0=ALU.max, op1=ALU.mult),
                  [("cv", A("cv"))], "cv", 1)
                E("vector", lambda e, gcol=gcol: e.reciprocal(
                    out=af(S_S2SC + gcol, 1), in_=af(S_FSC + gcol, 1)),
                  [("cv", A("cv"))], "cv", 1)
                E("vector", lambda e, gcol=gcol: e.tensor_scalar(
                    out=af(S_FSC + gcol, 1), in0=af(S_FSC + gcol, 1),
                    scalar1=AF_[:, S_WSC + 1:S_WSC + 2], scalar2=None,
                    op0=ALU.mult), [("cv", A("cv"))], "cv", 1)
            s2_cv = A("cv")
            cT_s = Cell()
            cBC = Cell()
            fw = free(("w2b",))
            pend_sT.append((ti, s2_cv, fw, cT_s))
            E("vector", lambda e: e.tensor_copy(
                out=AF_[0:2, S_S2ROW:S_S2ROW + 128], in_=PS5[0:2, 0:128]),
              [cT_s] + free(("s2row",)), "cv", 1)
            prod(("s2row",), "cv")
            for k in range(2):
                E("gpsimd", lambda e, k=k: e.dma_start(
                    out=AF_[64:65, S_S2ROW + 128 + 128 * k:
                            S_S2ROW + 256 + 128 * k],
                    in_=AF_[k:k + 1, S_S2ROW:S_S2ROW + 128]),
                  (need(("s2row",)) + free(("s2rg",))) if k == 0 else [],
                  "sg", 16)
            cons(("s2row",), "sg")
            prod(("s2rg",), "sg")
            pend_BC.append((ti, A("sg"), cBC))
            E("vector", lambda e: e.tensor_copy(
                out=af(S2BCS, FT), in_=PS4[:, 0:FT]),
              [cBC] + free(("s2bcs",)) +
              ([("cv", w2q_cv)] if ti == 0 else []), "cv", 1)
            prod(("s2bcs",), "cv")
            s2bcs_ev[ti] = ("cv", A("cv"))

            def mk_round(hc, xq2=xq2, ti=ti, sev=sev):
                def emit_r():
                    E("vector", lambda e, hc=hc: e.tensor_tensor(
                        out=i32b[:, 0:FT], in0=yt2[:, FT * hc:FT * (hc + 1)],
                        in1=af(S2BCS, FT), op=ALU.mult),
                      (need(("s2bcs",)) if hc % 8 == 0 else []) + [sev],
                      "cv", 1)
                    cons(("s2bcs",), "cv")
                    E("vector", lambda e, hc=hc: e.tensor_copy(
                        out=xq2[:, FT * hc:FT * (hc + 1)], in_=i32b[:, 0:FT]),
                      [("cv", A("cv"))] + free(("xq2", ti % 2, hc)), "cv", 1)
                    prod(("xq2", ti % 2, hc), "cv")
                    if hc % 8 == 7:
                        em.fill(rg_cells[ti][hc // 8], "cv")
                return emit_r
            for g in range(4):
                pend_R.append([mk_round(8 * g + j) for j in range(8)])

        def p3_w2(ti):
            a = FT * ti
            xq2 = xq2b(ti)
            for hc in range(NHC):
                for sp in range(2):
                    w0 = (need(("xq2", ti % 2, hc)) +
                          ([("cv", w2q_cv)] if ti == 0 and hc == 0 else []) +
                          (free(("w2a",), ("psb",)) if hc == 0 and sp == 0
                           else []))
                    E("tensor", lambda e, hc=hc, sp=sp: e.matmul(
                        PSB[:, 512 * sp:512 * (sp + 1)],
                        xq2[:, FT * hc:FT * hc + 128],
                        ab(wsign2[hc] + 512 * sp, 512),
                        start=(hc == 0), stop=(hc == NHC - 1)), w0, "ct", 1)
                    w1_ = (free(("w2b",), ("ps4",), ("ps45",)) +
                           [s2bcs_ev[ti]]
                           if hc == 0 and sp == 0 else [])
                    E("tensor", lambda e, hc=hc, sp=sp: e.matmul(
                        PS4[:] if sp == 0 else PS5[:],
                        xq2[:, FT * hc + 128:FT * hc + 256],
                        ab(wsign2[hc] + 512 * sp, 512),
                        start=(hc == 0), stop=(hc == NHC - 1)), w1_, "ct", 1)
                cons(("xq2", ti % 2, hc), "ct")
            prod(("w2a",), "ct")
            prod(("w2b",), "ct")
            for k in range(2):
                gcol = 2 * ti + k
                t0 = a + 128 * k
                abuf = af(F_XNC3 + 1024 * k, 1024)
                ob = af(F_OUTC + 1024 * k, 1024)
                dma("sync", abuf, daccs[t0:t0 + 128, :],
                    waits=free(("abuf", k)) + [("dac", proj_dac)],
                    sem="ab%d" % k)
                prod(("abuf", k), "ab%d" % k)
                wres = ("w2a",) if k == 0 else ("w2b",)
                for half in range(2):
                    def fin(e, k=k, gcol=gcol, abuf=abuf, ob=ob, half=half):
                        src = (PSB[:, 512 * half:512 * (half + 1)] if k == 0
                               else (PS4[:] if half == 0 else PS5[:]))
                        return e.scalar_tensor_tensor(
                            out=ob[:, 512 * half:512 * (half + 1)], in0=src,
                            scalar=af(S_FSC + gcol, 1),
                            in1=abuf[:, 512 * half:512 * (half + 1)],
                            op0=ALU.mult, op1=ALU.add)
                    E("vector", fin,
                      need(wres, ("abuf", k)) + free(("outc", k)), "cv", 1)
                cons(wres, "cv")
                cons(("abuf", k), "cv")
                prod(("outc", k), "cv")
                dma("sync", dout[t0:t0 + 128, :], ob, waits=need(("outc", k)))
                cons(("outc", k), "sld")

        for ti in range(NFT):
            p3_h(ti)
            if ti >= 1:
                p3_w2(ti - 1)
            p3_quant(ti)
        flush_T()
        flush_sT()
        flush_BC()
        while pend_R:
            flush_R()
        p3_w2(NFT - 1)
        if DEBUG_TAPS:
            srcs = {"biasM": af(S_BIASM, 16),
                    "wsc": af(S_WSC, 4), "y0": yt2, "xq2t0": xq2b(NFT - 1),
                    "s2": af(S_S2SC, 16),
                    "abuf1": af(F_XNC3 + 1024, 1024),
                    "ob1": af(F_OUTC + 1024, 1024)}
            for name in DEBUG_TAPS:
                if name in srcs:
                    dma("sync", dbg[name], srcs[name],
                        waits=[("cv", A("cv")), ("cs", A("cs")), ("ct", A("ct")),
                               ("gld", A("gld"))])
        E("sync", lambda e: e.nop(),
          [(s, A(s)) for s in ("sld", "gld", "tsp", "ts0", "ts1", "ts2",
                               "xb0", "xb1", "ab0", "ab1", "ws0", "ws1",
                               "sb0", "sb1", "sb2", "sb3", "wo", "wq",
                               "dac", "sg")], None)

        block = st.enter_context(nc.Block())
        em.finalize(block, sems)
    return nc


def kernel(**inputs):
    in_maps = _host_inputs(**inputs)
    nc = build_nc()
    res = run_bass_kernel_spmd(nc, in_maps, core_ids=list(range(8)))
    out = np.stack([np.asarray(r["out"]) for r in res.results], axis=0)
    return out.astype(np.float32, copy=False)


# version 12
# speedup vs baseline: 1.1031x; 1.0255x over previous
import numpy as np
import ml_dtypes
from contextlib import ExitStack

import concourse.bass as bass
import concourse.mybir as mybir
from concourse.bass_utils import run_bass_kernel_spmd

F32 = mybir.dt.float32
BF16 = mybir.dt.bfloat16
I32 = mybir.dt.int32
AF = mybir.ActivationFunctionType
ALU = mybir.AluOpType
AX = mybir.AxisListType
BF = ml_dtypes.bfloat16

T = 2048
C = 1024
H = 16
D = 64
HID = 4096
NTC = T // 128
NCC = C // 128
NHC = HID // 128
EPS_LN = 1e-5
EPS_RMS = 1e-6
FT = 256
NFT = T // FT

DEBUG_TAPS = []


class Cell:
    __slots__ = ("sem", "v")

    def __init__(self):
        self.sem = None
        self.v = None


class Em:

    def __init__(self):
        self.q = {e: [] for e in ("tensor", "vector", "scalar", "gpsimd", "sync")}
        self.cnt = {}
        self.resp = {}
        self.resc = {}

    def fill(self, cell, sem):
        cell.sem = sem
        cell.v = self.at(sem)

    def at(self, sem):
        return self.cnt.get(sem, 0)

    def emit(self, eng, fn, waits=(), inc=None, incn=1):
        self.q[eng].append((fn, tuple(waits), inc, incn))
        if inc is not None:
            self.cnt[inc] = self.cnt.get(inc, 0) + incn

    def prod(self, res, sem):
        self.resp[res] = (sem, self.at(sem))
        self.resc[res] = {}

    def cons(self, res, sem):
        d = self.resc.setdefault(res, {})
        d[sem] = self.at(sem)

    def need(self, *ress):
        out = []
        for r in ress:
            p = self.resp.get(r)
            if p:
                out.append(p)
        return out

    def free(self, *ress):
        out = []
        for r in ress:
            out.extend(self.resc.get(r, {}).items())
        return out

    def finalize(self, block, sems):
        def mk(name):
            ops = self.q[name]

            def run(e):
                hw = {}
                for fn, waits, inc, incn in ops:
                    merged = {}
                    for w in waits:
                        if isinstance(w, Cell):
                            assert w.v is not None, "unfilled Cell wait"
                            s, v = w.sem, w.v
                        else:
                            s, v = w
                        if v > 0 and v > merged.get(s, 0):
                            merged[s] = v
                    for s, v in merged.items():
                        if v > hw.get(s, 0):
                            e.wait_ge(sems[s], v)
                            hw[s] = v
                    inst = fn(e)
                    if inc is not None:
                        inst.then_inc(sems[inc], incn)
            return run
        block.tensor(mk("tensor"))
        block.vector(mk("vector"))
        block.scalar(mk("scalar"))
        block.gpsimd(mk("gpsimd"))
        block.sync(mk("sync"))


def _host_consts():
    half = D // 2
    inv_freq = 1.0 / (10000.0 ** (np.arange(0, half, dtype=np.float64) * 2.0 / D))
    ang = np.arange(T, dtype=np.float64)[:, None] * inv_freq[None, :]
    cosv = np.cos(ang).astype(np.float32)
    sinv = np.sin(ang).astype(np.float32)
    cosrep = np.zeros((128, T), np.float32)
    sinpat = np.zeros((128, T), np.float32)
    for p in range(128):
        j = p % half
        cosrep[p] = cosv[:, j]
        sinpat[p] = (-sinv[:, j]) if (p % D) < half else sinv[:, j]
    trimask = np.zeros((128, 128), np.float32)
    for kt in range(128):
        trimask[kt, kt:] = 1.0
    ident = np.eye(128, dtype=np.float32)
    onesf = np.ones((128, 128), np.float32)
    ones2 = np.zeros((128, 2), np.float32)
    ones2[0:64, 0] = 1.0
    ones2[64:128, 1] = 1.0
    sel01 = np.zeros((2, 256), np.float32)
    sel01[0, 0:128] = 1.0
    sel01[1, 128:256] = 1.0
    smalls = np.zeros((128, 8), np.float32)
    smalls[:, 0] = 127.0
    smalls[:, 1] = 1e-5
    smalls[:, 2] = EPS_RMS
    smalls[:, 3] = 1.0
    smalls[:, 5] = EPS_LN
    return dict(cosrep=cosrep, sinpat=sinpat,
                trimask=np.ascontiguousarray(trimask.astype(BF)),
                ident=ident, onesf=onesf, ones2=ones2, sel01=sel01, smalls=smalls)


def _host_inputs(L0_attn_x, L0_attn_w_qkv, L0_attn_w_out, L0_attn_q_norm_w,
                 L0_attn_k_norm_w, L0_ffn_w1, L0_ffn_conv_w, L0_ffn_conv_b,
                 L0_ffn_w2):
    consts = _host_consts()
    wqkv = np.asarray(L0_attn_w_qkv)
    wqkT = np.ascontiguousarray(wqkv[0:2 * C].T)
    wvT = np.ascontiguousarray(wqkv[2 * C:3 * C].T)
    woutT = np.ascontiguousarray(np.asarray(L0_attn_w_out).T)
    w1T = np.ascontiguousarray(np.asarray(L0_ffn_w1).T)
    w2T = np.ascontiguousarray(np.asarray(L0_ffn_w2).T)
    qn = np.asarray(L0_attn_q_norm_w).astype(np.float32, copy=False)
    kn = np.asarray(L0_attn_k_norm_w).astype(np.float32, copy=False)
    qn_stat = np.zeros((2, 128), np.float32)
    qn_stat[0, 0:64] = qn
    qn_stat[1, 64:128] = qn
    kn_stat = np.zeros((2, 128), np.float32)
    kn_stat[0, 0:64] = kn
    kn_stat[1, 64:128] = kn
    cw = np.asarray(L0_ffn_conv_w).reshape(HID, 3)
    taps = np.zeros((128, 96), np.float32)
    convb = np.zeros((128, 32), np.float32)
    cb = np.asarray(L0_ffn_conv_b)
    for c in range(NHC):
        taps[:, c] = cw[128 * c:128 * (c + 1), 0]
        taps[:, 32 + c] = cw[128 * c:128 * (c + 1), 1]
        taps[:, 64 + c] = cw[128 * c:128 * (c + 1), 2]
        convb[:, c] = cb[128 * c:128 * (c + 1)]
    base = dict(wqkT=wqkT, wvT=wvT, woutT=woutT, w1T=w1T, w2T=w2T,
                qn_stat=qn_stat, kn_stat=kn_stat, taps=taps, convb=convb, **consts)
    xs = np.asarray(L0_attn_x)
    return [dict(base, x=np.ascontiguousarray(xs[b], dtype=np.float32))
            for b in range(8)]


def build_nc():
    nc = bass.Bass("TRN2", target_bir_lowering=False)

    dx = nc.dram_tensor("x", [T, C], F32, kind="ExternalInput").ap()
    dwqkT = nc.dram_tensor("wqkT", [C, 2 * C], F32, kind="ExternalInput").ap()
    dwvT = nc.dram_tensor("wvT", [C, C], F32, kind="ExternalInput").ap()
    dwoutT = nc.dram_tensor("woutT", [C, C], F32, kind="ExternalInput").ap()
    dw1T = nc.dram_tensor("w1T", [C, HID], F32, kind="ExternalInput").ap()
    dw2T = nc.dram_tensor("w2T", [HID, C], F32, kind="ExternalInput").ap()
    dcos = nc.dram_tensor("cosrep", [128, T], F32, kind="ExternalInput").ap()
    dsin = nc.dram_tensor("sinpat", [128, T], F32, kind="ExternalInput").ap()
    dtri = nc.dram_tensor("trimask", [128, 128], BF16, kind="ExternalInput").ap()
    dident = nc.dram_tensor("ident", [128, 128], F32, kind="ExternalInput").ap()
    donesf = nc.dram_tensor("onesf", [128, 128], F32, kind="ExternalInput").ap()
    dones2 = nc.dram_tensor("ones2", [128, 2], F32, kind="ExternalInput").ap()
    dsel01 = nc.dram_tensor("sel01", [2, 256], F32, kind="ExternalInput").ap()
    dsmalls = nc.dram_tensor("smalls", [128, 8], F32, kind="ExternalInput").ap()
    dqn = nc.dram_tensor("qn_stat", [2, 128], F32, kind="ExternalInput").ap()
    dkn = nc.dram_tensor("kn_stat", [2, 128], F32, kind="ExternalInput").ap()
    dtaps = nc.dram_tensor("taps", [128, 96], F32, kind="ExternalInput").ap()
    dconvb = nc.dram_tensor("convb", [128, 32], F32, kind="ExternalInput").ap()

    dxqs = nc.dram_tensor("xqs", [T, C], BF16, kind="Internal").ap()
    daccs = nc.dram_tensor("accs", [T, C], F32, kind="Internal").ap()
    dw1q = nc.dram_tensor("w1q", [C, HID], BF16, kind="Internal").ap()
    ds2 = nc.dram_tensor("ds2", [16, 256], F32, kind="Internal").ap()
    dout = nc.dram_tensor("out", [T, C], F32, kind="ExternalOutput").ap()

    dbg = {}
    if DEBUG_TAPS:
        shapes = {"xnf": ([128, NCC * T], BF16), "qhat": ([128, NCC * T], BF16),
                  "khat": ([128, NCC * T], BF16), "vt": ([128, NTC * H * 65], BF16),
                  "attnout": ([128, NCC * T], BF16), "biasM": ([128, 16], F32),
                  "wsc": ([128, 4], F32), "y0": ([128, NHC * FT], F32),
                  "xq2t0": ([128, NHC * FT], BF16), "s2": ([128, 16], F32),
                  "xqvf": ([128, NCC * 2052], BF16),
                  "abuf1": ([128, 1024], F32), "ob1": ([128, 1024], F32)}
        for name in DEBUG_TAPS:
            shp, dt = shapes[name]
            dbg[name] = nc.dram_tensor("dbg_" + name, shp, dt,
                                       kind="ExternalOutput").ap()

    em = Em()
    E, A = em.emit, em.at
    need, free, prod, cons = em.need, em.free, em.prod, em.cons

    st = ExitStack()
    with st:
        AB = st.enter_context(nc.sbuf_tensor("AB", [128, 70400], BF16))
        AF_ = st.enter_context(nc.sbuf_tensor("AFA", [128, 16064], F32))
        i32b = st.enter_context(nc.sbuf_tensor("i32b", [128, 512], I32))

        R_A, R_B, R_C, R_D, R_E = 0, 16384, 32768, 49152, 65792

        def ab(off, n):
            return AB[:, off:off + n]

        F_COS, F_SIN = 0, 2048
        F_S1, F_S2, F_S4 = 4096, 6144, 8192
        F_MSB = 10240
        F_WFS = 12288
        S_STATS = 13312
        S_S1COL = 13376
        S_R1COL = 13392
        S_MAXQ = 13408
        S_MAXK = 13424
        S_BIASM = 13440
        S_WSUMS = 13456
        S_WSC = 13590
        S_S2SC = 13594
        S_FSC = 13610
        S_TMX = 13626
        S_S2ROW = 13632
        F_WABS = 13632
        F_WRS1 = 14848
        F_WRS2 = 15360
        F_YT = 0
        F_CONV0 = 8192
        F_CONV1 = 8448
        F_FOLD = 8704
        F_XNC3 = 8960
        F_OUTC = 11008

        def af(off, n):
            return AF_[:, off:off + n]

        cos_t = af(F_COS, 2048)
        sin_t = af(F_SIN, 2048)
        tri_t = st.enter_context(nc.sbuf_tensor("tri_t", [128, 128], BF16))
        id_t = st.enter_context(nc.sbuf_tensor("id_t", [128, 128], F32))
        onesf_t = st.enter_context(nc.sbuf_tensor("onesf_t", [128, 128], F32))
        ones2_t = st.enter_context(nc.sbuf_tensor("ones2_t", [128, 2], F32))
        sel_t = st.enter_context(nc.sbuf_tensor("sel_t", [2, 256], F32))
        sm_t = st.enter_context(nc.sbuf_tensor("sm_t", [128, 8], F32))
        qn_t = st.enter_context(nc.sbuf_tensor("qn_t", [2, 128], F32))
        kn_t = st.enter_context(nc.sbuf_tensor("kn_t", [2, 128], F32))
        taps_t = st.enter_context(nc.sbuf_tensor("taps_t", [128, 96], F32))
        convb_t = st.enter_context(nc.sbuf_tensor("convb_t", [128, 32], F32))

        PSQ = st.enter_context(nc.psum_tensor("psq", [128, 2048], F32))
        PS4 = st.enter_context(nc.psum_tensor("ps4", [128, 512], F32))
        PS5 = st.enter_context(nc.psum_tensor("ps5", [128, 512], F32))
        PSB = st.enter_context(nc.psum_tensor("psb", [128, 1024], F32))
        PS = [PSQ[:, 512 * i:512 * (i + 1)] for i in range(4)] + [PS4[:], PS5[:]]

        def psq(i, p0, p1, c0, c1):
            return PSQ[p0:p1, 512 * i + c0:512 * i + c1]

        sems = {n: st.enter_context(nc.semaphore(n))
                for n in ("sld", "gld", "tsp", "ct", "cv", "cs", "cg",
                          "ts0", "ts1", "ts2", "xb0", "xb1", "ab0", "ab1",
                          "ws0", "ws1", "sb0", "sb1", "sb2", "sb3",
                          "wo", "wq", "dac", "sg")}

        def dma(eng, out, in_, waits=(), sem=None):
            sem = sem or ("gld" if eng == "gpsimd" else "sld")
            E(eng, lambda e, o=out, i=in_: e.dma_start(out=o, in_=i), waits, sem, 16)
            return sem

        def tap(name, src_ap):
            if name not in DEBUG_TAPS:
                return
            dma("sync", dbg[name], src_ap,
                waits=[("cv", A("cv")), ("cs", A("cs")), ("ct", A("ct")),
                       ("gld", A("gld")), ("tsp", A("tsp"))])
            bar = A("sld")
            for eng, s in (("tensor", "ct"), ("vector", "cv"), ("scalar", "cs")):
                E(eng, lambda e: e.nop(), [("sld", bar)], s, 1)

        for o, i in [(cos_t, dcos), (sin_t, dsin), (tri_t[:], dtri), (id_t[:], dident),
                     (onesf_t[:], donesf), (ones2_t[:], dones2), (sel_t[:], dsel01),
                     (sm_t[:], dsmalls), (qn_t[:], dqn), (kn_t[:], dkn),
                     (taps_t[:], dtaps), (convb_t[:], dconvb)]:
            dma("sync", o, i)
        consts_ld = A("sld")
        for eng in ("tensor", "vector", "scalar"):
            E(eng, lambda e: e.nop(), [("sld", consts_ld)],
              {"tensor": "ct", "vector": "cv", "scalar": "cs"}[eng], 1)
        for c in range(NCC):
            dma("gpsimd", ab(R_B + 1024 * c, 1024), dwvT[128 * c:128 * (c + 1), :])
        wv_ld = A("gld")

        xnf = ab(R_A, 16384)
        vt = ab(R_D, 16640)
        E("vector", lambda e: e.memset(
            vt.rearrange("p (j h o) -> p j h o", j=NTC, h=H)[:, :, :, 64:65], 1.0),
          [], "cv", 1)
        vones_cv = A("cv")

        def p1v_j(j, evcs):
            for c in range(NCC):
                for sp in range(2):
                    w0 = ([("gld", wv_ld), ("cs", evcs)] + free(("psb",))
                          if (c, sp) == (0, 0) else [])
                    E("tensor", lambda e, j=j, c=c, sp=sp: e.matmul(
                        PSB[:, 512 * sp:512 * (sp + 1)],
                        xnf[:, T * c + 128 * j:T * c + 128 * (j + 1)],
                        ab(R_B + 1024 * c + 512 * sp, 512),
                        start=(c == 0), stop=(c == NCC - 1)), w0, "ct", 1)
            prod(("psb",), "ct")
            E("vector", lambda e, j=j: e.tensor_copy(
                out=ab(R_D + 1040 * j, 1040).rearrange(
                    "p (h o) -> p h o", o=65)[:, :, 0:64],
                in_=PSB[:].rearrange("p (h o) -> p h o", h=H)),
              need(("psb",)), "cv", 1)
            cons(("psb",), "cv")

        for i in range(NTC):
            par = i % 2
            xbuf = af(F_S1 + 1024 * par, 1024)
            nbuf = af(F_S4 + 1024 * par, 1024)
            sqs = af(F_MSB, 1024)
            qb16 = ab(R_B + 10240 + 1024 * par, 1024)
            dma("sync", xbuf, dx[128 * i:128 * (i + 1), :],
                waits=free(("xbuf", par)))
            prod(("xbuf", par), "sld")
            E("vector", lambda e, xb=xbuf: e.reduce_sum(
                out=af(S_STATS + 0, 1), in_=xb, axis=AX.X),
              need(("xbuf", par)), "cv", 1)
            E("scalar", lambda e, xb=xbuf, sqs=sqs: e.activation(
                out=sqs, in_=xb, func=AF.Square, accum_out=af(S_STATS + 1, 1)),
              need(("xbuf", par)), "cs", 1)
            cons(("xbuf", par), "cs")
            sq_ev = ("cs", A("cs"))
            E("vector", lambda e: e.tensor_scalar(
                out=af(S_STATS + 2, 1), in0=af(S_STATS + 0, 1), scalar1=1.0 / C,
                scalar2=None, op0=ALU.mult), [("cv", A("cv"))], "cv", 1)
            E("vector", lambda e: e.scalar_tensor_tensor(
                out=af(S_STATS + 3, 1), in0=af(S_STATS + 2, 1), scalar=-1.0,
                in1=af(S_STATS + 2, 1), op0=ALU.mult, op1=ALU.mult),
              [("cv", A("cv"))], "cv", 1)
            E("vector", lambda e: e.scalar_tensor_tensor(
                out=af(S_STATS + 3, 1), in0=af(S_STATS + 1, 1), scalar=1.0 / C,
                in1=af(S_STATS + 3, 1), op0=ALU.mult, op1=ALU.add),
              [("cv", A("cv")), sq_ev], "cv", 1)
            E("scalar", lambda e: e.activation(
                out=af(S_STATS + 3, 1), in_=af(S_STATS + 3, 1), func=AF.Sqrt,
                bias=sm_t[:, 5:6]), [("cv", A("cv"))], "cs", 1)
            E("vector", lambda e: e.reciprocal(
                out=af(S_STATS + 3, 1), in_=af(S_STATS + 3, 1)),
              [("cs", A("cs"))], "cv", 1)
            E("vector", lambda e: e.scalar_tensor_tensor(
                out=af(S_STATS + 2, 1), in0=af(S_STATS + 2, 1), scalar=-1.0,
                in1=af(S_STATS + 3, 1), op0=ALU.mult, op1=ALU.mult),
              [("cv", A("cv"))], "cv", 1)
            E("vector", lambda e, xb=xbuf, nb=nbuf: e.tensor_scalar(
                out=nb, in0=xb, scalar1=af(S_STATS + 3, 1),
                scalar2=af(S_STATS + 2, 1), op0=ALU.mult, op1=ALU.add),
              [("cv", A("cv"))] + free(("nbuf", par)), "cv", 1)
            cons(("xbuf", par), "cv")
            prod(("nbuf", par), "cv")
            for c in range(NCC):
                tb = ("tpb", c % 2)
                E("tensor", lambda e, c=c, nb=nbuf: e.transpose(
                    psq(c % 2, 0, 128, 0, 128), nb[:, 128 * c:128 * (c + 1)],
                    id_t[:]),
                  (need(("nbuf", par)) if c == 0 else []) + free(tb), "ct", 1)
                prod(tb, "ct")
                if c == NCC - 1:
                    cons(("nbuf", par), "ct")
                E("scalar", lambda e, c=c, i=i: e.activation(
                    out=xnf[:, T * c + 128 * i:T * c + 128 * (i + 1)],
                    in_=psq(c % 2, 0, 128, 0, 128), func=AF.Copy),
                  need(tb), "cs", 1)
                cons(tb, "cs")
            p0_evcs = A("cs")
            E("vector", lambda e, nb=nbuf, i=i: e.reduce_max(
                out=af(S_S1COL + i, 1), in_=nb, axis=AX.X, apply_absolute_value=True),
              [], "cv", 1)
            E("vector", lambda e, i=i: e.tensor_scalar(
                out=af(S_R1COL + i, 1), in0=af(S_S1COL + i, 1), scalar1=1e-5,
                scalar2=1.0 / 127.0, op0=ALU.max, op1=ALU.mult),
              [("cv", A("cv"))], "cv", 1)
            E("vector", lambda e, i=i: e.reciprocal(
                out=af(S_S1COL + i, 1), in_=af(S_R1COL + i, 1)),
              [("cv", A("cv"))], "cv", 1)
            for hh in (0, 1):
                E("vector", lambda e, nb=nbuf, i=i, hh=hh: e.tensor_scalar(
                    out=i32b[:], in0=nb[:, 512 * hh:512 * (hh + 1)],
                    scalar1=af(S_S1COL + i, 1), scalar2=None,
                    op0=ALU.mult), [("cv", A("cv"))], "cv", 1)
                E("vector", lambda e, i=i, o=qb16, hh=hh: e.tensor_scalar(
                    out=o[:, 512 * hh:512 * (hh + 1)], in0=i32b[:],
                    scalar1=af(S_R1COL + i, 1), scalar2=None,
                    op0=ALU.mult),
                  [("cv", A("cv"))] + (free(("qb16", par)) if hh == 0 else []),
                  "cv", 1)
            cons(("nbuf", par), "cv")
            prod(("qb16", par), "cv")
            dma("sync", dxqs[128 * i:128 * (i + 1), :], qb16,
                waits=need(("qb16", par)), sem="dac")
            cons(("qb16", par), "dac")
            p1v_j(i, p0_evcs)
        cons(("qk",), "cs")
        p1v_cv = A("cv")
        p0_dac = A("dac")
        tap("xnf", ab(R_A, 16384))

        wjobs = []
        for c in range(NCC):
            for b4 in range(8):
                wjobs.append(dw1T[128 * c:128 * (c + 1), 512 * b4:512 * (b4 + 1)])
        for c in range(NHC):
            for b2 in range(2):
                wjobs.append(dw2T[128 * c:128 * (c + 1), 512 * b2:512 * (b2 + 1)])
        WBUFS = [F_WABS, F_WRS1, F_WRS2]
        wj_state = {"next": 0}

        def emit_wjobs(n):
            for _ in range(n):
                j = wj_state["next"]
                if j >= len(wjobs) + 3:
                    return
                r = j - 3
                if 0 <= r < len(wjobs):
                    E("vector", lambda e, b=af(WBUFS[r % 3], 512), r=r:
                        e.reduce_sum(out=af(S_WSUMS + r, 1), in_=b, axis=AX.X,
                                     apply_absolute_value=True),
                      need(("wbuf", r % 3)), "cv", 1)
                    cons(("wbuf", r % 3), "cv")
                if j < len(wjobs):
                    dma("gpsimd", af(WBUFS[j % 3], 512), wjobs[j],
                        waits=free(("wbuf", j % 3)))
                    prod(("wbuf", j % 3), "gld")
                wj_state["next"] = j + 1

        tap("vt", ab(R_D, 16640))

        qhat = ab(R_B, 16384)
        khat = ab(R_C, 16384)
        s1 = af(F_S1, 2048)
        s2b = af(F_S2, 2048)
        s4 = af(F_S4, 2048)

        P1 = [dict() for _ in range(16)]

        def p1_qkv(oc):
            sec = ab(R_E + 1024 * (oc % 3), 1024)
            dma("gpsimd", sec.rearrange("p (c n) -> p c n", c=NCC),
                dwqkT[:, 128 * oc:128 * (oc + 1)].rearrange("(c p) n -> p c n",
                                                            c=NCC),
                waits=free(("sec", oc % 3)))
            prod(("sec", oc % 3), "gld")
            for c in range(NCC):
                for sp in range(4):
                    w0 = (need(("sec", oc % 3)) + [("cv", p1v_cv)]
                          + free(("qk",))
                          if (c, sp) == (0, 0) else [])
                    E("tensor", lambda e, c=c, sp=sp, sec=sec: e.matmul(
                        PS[sp], sec[:, 128 * c:128 * (c + 1)],
                        xnf[:, T * c + 512 * sp:T * c + 512 * (sp + 1)],
                        start=(c == 0), stop=(c == NCC - 1)), w0, "ct", 1)
            prod(("qk",), "ct")
            cons(("sec", oc % 3), "ct")

        def p1_nonpe(oc):
            isq = oc < 8
            ch = oc % 8
            dst = qhat if isq else khat
            mxoff = S_MAXQ if isq else S_MAXK
            st_ = P1[oc]
            st_["c_ms"] = [Cell() for _ in range(4)]
            st_["c_wrs"] = [Cell() for _ in range(4)]
            st_["c_nsq"] = [Cell() for _ in range(4)]
            E("scalar", lambda e: e.copy(out=s1, in_=PSQ[:]),
              need(("qk",)) + free(("s1",)), "cs", 1)
            cons(("qk",), "cs")
            prod(("s1",), "cs")
            zfree = (P1[oc - 1]["c_nsq"][3:] if oc >= 1 else []) + free(("s4",))
            E("scalar", lambda e: e.activation(out=s4, in_=PSQ[:], func=AF.Square),
              need(("qk",)) + zfree, "cs", 1)
            cons(("qk",), "cs")
            prod(("s4",), "cs")
            st_["sq_ev"] = ("cs", A("cs"))
            for (a, b) in ((0, 32), (32, 0), (64, 96), (96, 64)):
                dma("sync", AF_[a:a + 32, F_S2:F_S2 + 2048],
                    AF_[b:b + 32, F_S1:F_S1 + 2048],
                    waits=need(("s1",)) + free(("s2b",)))
            cons(("s1",), "sld")
            prod(("s2b",), "sld")
            bwaits = need(("s2b",)) + ([("sld", consts_ld)] if oc == 0 else [])
            for sp in range(4):
                msfree = (free(("msb",)) +
                          ([P1[oc - 1]["c_wrs"][3]] if oc >= 1 else [])
                          if sp == 0 else [])
                E("vector", lambda e, sp=sp: e.tensor_copy(
                    out=AF_[0:2, F_MSB + 512 * sp:F_MSB + 512 * (sp + 1)],
                    in_=PS4[0:2, :]),
                  [st_["c_ms"][sp]] + msfree, "cv", 1)
                st_["msevac_ev_%d" % sp] = ("cv", A("cv"))
            prod(("msb",), "cv")
            st_["msevac3"] = ("cv", A("cv"))
            E("scalar", lambda e: e.activation(
                out=AF_[0:2, F_MSB:F_MSB + 2048], in_=AF_[0:2, F_MSB:F_MSB + 2048],
                func=AF.Ln, scale=1.0 / D, bias=sm_t[0:2, 2:3]),
              need(("msb",)), "cs", 1)
            E("scalar", lambda e: e.activation(
                out=AF_[0:2, F_MSB:F_MSB + 2048], in_=AF_[0:2, F_MSB:F_MSB + 2048],
                func=AF.Exp, scale=-0.5), [], "cs", 1)
            prod(("msb",), "cs")
            st_["rs_ev"] = ("cs", A("cs"))
            E("vector", lambda e: e.tensor_mul(out=s2b, in0=s2b, in1=sin_t),
              bwaits, "cv", 1)
            prod(("s2b",), "cv")
            E("vector", lambda e: e.tensor_mul(out=s4, in0=s1, in1=cos_t),
              need(("s1",)) + [st_["c_ms"][3]] + free(("s4",)), "cv", 1)
            cons(("s1",), "cv")
            prod(("s4",), "cv")
            E("vector", lambda e: e.tensor_add(out=s2b, in0=s4, in1=s2b),
              need(("s2b",), ("s4",)), "cv", 1)
            cons(("s4",), "cv")
            prod(("s2b",), "cv")
            for sp in range(4):
                E("vector", lambda e, sp=sp, dst=dst, ch=ch: e.tensor_mul(
                    out=dst[:, T * ch + 512 * sp:T * ch + 512 * (sp + 1)],
                    in0=s2b[:, 512 * sp:512 * (sp + 1)],
                    in1=PSB[:, 512 * (sp % 2):512 * (sp % 2) + 512]),
                  [st_["c_wrs"][sp]] + need(("s2b",)), "cv", 1)
                st_["tt_ev_%d" % sp] = ("cv", A("cv"))
                cons(("s2b",), "cv")
            st_["qh_ev"] = ("cv", A("cv"))
            E("scalar", lambda e, dst=dst, ch=ch: e.activation(
                out=s4, in_=dst[:, T * ch:T * (ch + 1)], func=AF.Square),
              [st_["qh_ev"]] + free(("s4",)), "cs", 1)
            prod(("s4",), "cs")
            st_["nsqsq_ev"] = ("cs", A("cs"))
            for sp in range(4):
                E("vector", lambda e, sp=sp: e.reduce_max(
                    out=AF_[0:2, S_TMX + sp:S_TMX + sp + 1],
                    in_=PS5[0:2, :], axis=AX.X), [st_["c_nsq"][sp]], "cv", 1)
                st_["nsqred_ev_%d" % sp] = ("cv", A("cv"))
            E("vector", lambda e, mxoff=mxoff, ch=ch: e.reduce_max(
                out=AF_[0:2, mxoff + ch:mxoff + ch + 1],
                in_=AF_[0:2, S_TMX:S_TMX + 4], axis=AX.X),
              [("cv", A("cv"))], "cv", 1)
            st_["done_cv"] = ("cv", A("cv"))

        def p1_pesmalls(oc):
            isq = oc < 8
            nstat = qn_t if isq else kn_t
            st_ = P1[oc]
            pst = P1[oc - 1] if oc >= 1 else None
            for sp in range(4):
                w0 = [st_["sq_ev"]]
                if sp > 0:
                    w0.append(st_["msevac_ev_%d" % (sp - 1)])
                elif pst is not None:
                    w0.append(pst["msevac_ev_3"])
                E("tensor", lambda e, sp=sp: e.matmul(
                    PS4[0:2, :], ones2_t[:], s4[:, 512 * sp:512 * (sp + 1)],
                    start=True, stop=True), w0, "ct", 1)
                em.fill(st_["c_ms"][sp], "ct")
            for sp in range(4):
                w0 = [st_["rs_ev"]]
                if sp >= 2:
                    w0.append(st_["tt_ev_%d" % (sp - 2)])
                elif pst is not None:
                    w0.append(pst["tt_ev_%d" % (sp + 2)])
                else:
                    w0 += free(("psb",))
                E("tensor", lambda e, sp=sp, nstat=nstat: e.matmul(
                    PSB[:, 512 * (sp % 2):512 * (sp % 2) + 512], nstat[:],
                    AF_[0:2, F_MSB + 512 * sp:F_MSB + 512 * (sp + 1)],
                    start=True, stop=True), w0, "ct", 1)
                em.fill(st_["c_wrs"][sp], "ct")
            for sp in range(4):
                w0 = [st_["nsqsq_ev"]]
                if sp > 0:
                    w0.append(st_["nsqred_ev_%d" % (sp - 1)])
                elif pst is not None:
                    w0.append(pst["nsqred_ev_3"])
                E("tensor", lambda e, sp=sp: e.matmul(
                    PS5[0:2, :], ones2_t[:], s4[:, 512 * sp:512 * (sp + 1)],
                    start=True, stop=True), w0, "ct", 1)
                em.fill(st_["c_nsq"][sp], "ct")

        for oc in range(16):
            p1_qkv(oc)
            if oc >= 1:
                p1_pesmalls(oc - 1)
            p1_nonpe(oc)
        p1_pesmalls(15)
        p1a_cv = A("cv")


        E("scalar", lambda e: e.activation(
            out=AF_[0:2, S_MAXQ:S_MAXQ + 8], in_=AF_[0:2, S_MAXQ:S_MAXQ + 8],
            func=AF.Sqrt), [("cv", p1a_cv)], "cs", 1)
        E("scalar", lambda e: e.activation(
            out=AF_[0:2, S_MAXK:S_MAXK + 8], in_=AF_[0:2, S_MAXK:S_MAXK + 8],
            func=AF.Sqrt), [], "cs", 1)
        E("vector", lambda e: e.tensor_mul(
            out=AF_[0:2, S_MAXQ:S_MAXQ + 8], in0=AF_[0:2, S_MAXQ:S_MAXQ + 8],
            in1=AF_[0:2, S_MAXK:S_MAXK + 8]), [("cs", A("cs"))], "cv", 1)
        E("vector", lambda e: e.tensor_scalar(
            out=AF_[0:2, S_MAXQ:S_MAXQ + 8], in0=AF_[0:2, S_MAXQ:S_MAXQ + 8],
            scalar1=-0.125, scalar2=None, op0=ALU.mult),
          [("cv", A("cv"))], "cv", 1)
        negM_cv = A("cv")
        for h in range(H):
            r, c = h % 2, h // 2
            E("tensor", lambda e, r=r, c=c, h=h: e.matmul(
                PS4[:, h:h + 1], sel_t[:, 128 * r:128 * (r + 1)],
                AF_[0:2, S_MAXQ + c:S_MAXQ + c + 1], start=True, stop=True),
              [("cv", negM_cv)] + (free(("ps4",)) if h == 0 else []), "ct", 1)
        prod(("ps4",), "ct")
        E("vector", lambda e: e.tensor_copy(
            out=af(S_BIASM, 16), in_=PS4[:, 0:16]), need(("ps4",)), "cv", 1)
        cons(("ps4",), "cv")
        biasM_cv = A("cv")
        tap("qhat", ab(R_B, 16384))
        tap("khat", ab(R_C, 16384))

        pt_off = []
        off = 0
        for j in range(8):
            pt_off.append((R_A, off))
            off += T - 128 * j
        off = 0
        for j in range(8, 16):
            pt_off.append((R_E, off))
            off += T - 128 * j

        def pt_ap(j, col0, width):
            base, o = pt_off[j]
            return ab(base + o + col0, width)

        for pair in range(8):
            for parx in range(2):
                h = 2 * pair + parx
                base = 64 * parx
                ptev = {}
                for j in range(NTC):
                    qt0 = 128 * j
                    nsp = (T - qt0 + 511) // 512
                    for spi in range(nsp):
                        c0 = qt0 + 512 * spi
                        w = min(512, T - c0)
                        bnk = ("sb", (j + spi) % 4)
                        bank = PS[(j + spi) % 4]
                        w0 = free(bnk) + ([("cv", biasM_cv)] if h == 0 else [])
                        E("tensor", lambda e, j=j, c0=c0, w=w, bank=bank, base=base,
                            pair=pair: e.matmul(
                            bank[:, 0:w],
                            khat[base:base + 64,
                                 T * pair + 128 * j:T * pair + 128 * (j + 1)],
                            qhat[base:base + 64, T * pair + c0:T * pair + c0 + w],
                            start=True, stop=True), w0, "ct", 1)
                        prod(bnk, "ct")
                        E("scalar", lambda e, j=j, c0=c0, w=w, bank=bank, qt0=qt0,
                            h=h: e.activation(
                            out=pt_ap(j, c0 - qt0, w), in_=bank[:, 0:w],
                            func=AF.Exp, scale=0.125, bias=af(S_BIASM + h, 1)),
                          need(bnk) +
                          (free(("pt", j)) + [("cv", biasM_cv)] if spi == 0 else []),
                          "cs", 1)
                        cons(bnk, "cs")
                    exp_ev = ("cs", A("cs"))
                    E("vector", lambda e, j=j: e.tensor_mul(
                        out=pt_ap(j, 0, 128), in0=pt_ap(j, 0, 128), in1=tri_t[:]),
                      [exp_ev], "cv", 1)
                    ptev[j] = [exp_ev, ("cv", A("cv"))]
                    prod(("pt", j), "cv")
                for s in range(4):
                    avb = ("avb", s % 2)
                    avbank = PS[4 + (s % 2)]
                    njs = 4 * s + 4
                    for j in range(njs):
                        coff = 512 * s - 128 * j
                        c0 = max(0, coff)
                        pcol = max(0, -coff)
                        w = 512 - pcol
                        w0 = list(ptev[j]) + (free(avb) if j == 0 else [])
                        E("tensor", lambda e, j=j, c0=c0, pcol=pcol, w=w,
                            avbank=avbank, h=h, njs=njs: e.matmul(
                            avbank[0:65, pcol:pcol + w],
                            vt[:, 1040 * j + 65 * h:1040 * j + 65 * h + 65],
                            pt_ap(j, c0, w),
                            start=(j == 0), stop=(j == njs - 1)),
                          w0 + ([("cv", vones_cv)] if h == 0 and s == 0 else []),
                          "ct", 1)
                        cons(("pt", j), "ct")
                    prod(avb, "ct")
                    E("scalar", lambda e, s=s, parx=parx, avbank=avbank: e.copy(
                        out=AF_[0:65, F_S1 + 2048 * parx + 512 * s:
                                F_S1 + 2048 * parx + 512 * (s + 1)],
                        in_=avbank[0:65, :]),
                      need(avb) + (free(("ascr", parx)) if s == 0 else []), "cs", 1)
                    cons(avb, "cs")
                prod(("ascr", parx), "cs")
            for parx in range(2):
                h = 2 * pair + parx
                E("scalar", lambda e, parx=parx: e.activation(
                    out=AF_[64:65, F_MSB:F_MSB + 2048],
                    in_=AF_[64:65, F_S1 + 2048 * parx:F_S1 + 2048 * parx + 2048],
                    func=AF.Ln),
                  need(("ascr", parx)) + free(("msb",)), "cs", 1)
                E("scalar", lambda e: e.activation(
                    out=AF_[64:65, F_MSB:F_MSB + 2048],
                    in_=AF_[64:65, F_MSB:F_MSB + 2048], func=AF.Exp, scale=-1.0),
                  [], "cs", 1)
                prod(("msb",), "cs")
                for s in range(4):
                    bnk = ("nb", s % 2)
                    bank = PSB[0:64, 512 * (s % 2):512 * (s % 2) + 512]
                    E("tensor", lambda e, s=s, bank=bank: e.matmul(
                        bank, onesf_t[64:65, 0:64],
                        AF_[64:65, F_MSB + 512 * s:F_MSB + 512 * (s + 1)],
                        start=True, stop=True),
                      need(("msb",)) + free(bnk), "ct", 1)
                    prod(bnk, "ct")
                    E("scalar", lambda e, s=s, bank=bank: e.copy(
                        out=AF_[0:64, F_MSB + 512 * s:F_MSB + 512 * (s + 1)],
                        in_=bank),
                      need(bnk) + (free(("rb",)) if s == 0 else []), "cs", 1)
                    cons(bnk, "cs")
                prod(("rb",), "cs")
                if parx == 0:
                    E("vector", lambda e, pair=pair: e.tensor_mul(
                        out=AB[0:64, R_B + T * pair:R_B + T * (pair + 1)],
                        in0=AF_[0:64, F_S1:F_S1 + 2048],
                        in1=AF_[0:64, F_MSB:F_MSB + 2048]),
                      need(("rb",), ("ascr", 0)), "cv", 1)
                    cons(("rb",), "cv")
                    cons(("ascr", 0), "cv")
                else:
                    E("vector", lambda e: e.tensor_mul(
                        out=AB[0:64, R_A + 12800:R_A + 12800 + 2048],
                        in0=AF_[0:64, F_S1 + 2048:F_S1 + 4096],
                        in1=AF_[0:64, F_MSB:F_MSB + 2048]),
                      need(("rb",), ("ascr", 1)) + free(("oscr",)), "cv", 1)
                    cons(("rb",), "cv")
                    cons(("ascr", 1), "cv")
                    prod(("oscr",), "cv")
                    E("sync", lambda e, pair=pair: e.dma_start(
                        out=AB[64:128, R_B + T * pair:R_B + T * (pair + 1)],
                        in_=AB[0:64, R_A + 12800:R_A + 12800 + 2048]),
                      need(("oscr",)), "sld", 16)
                    cons(("oscr",), "sld")
            emit_wjobs(17)
        emit_wjobs(200)
        tap("attnout", ab(R_B, 16384))
        attn_cv = A("cv")
        attn_cs = A("cs")
        attn_sld = A("sld")
        attn_ct = A("ct")
        attnout = qhat

        XQW = 2052
        xqvf = ab(R_D, NCC * XQW)
        E("vector", lambda e: e.memset(
            xqvf.rearrange("p (c t) -> p c t", c=NCC)[:, :, 0:1], 0.0),
          [("cv", attn_cv), ("cs", attn_cs), ("ct", attn_ct), ("sld", attn_sld)],
          "cv", 1)
        E("vector", lambda e: e.memset(
            xqvf.rearrange("p (c t) -> p c t", c=NCC)[:, :, 2049:2052], 0.0),
          [], "cv", 1)
        guards_cv = A("cv")
        for c in range(NCC):
            dma("gpsimd", ab(R_C + 1024 * c, 1024), dwoutT[128 * c:128 * (c + 1), :],
                waits=[("ct", attn_ct), ("cs", attn_cs)], sem="wo")
        wout_ld = A("wo")

        E("vector", lambda e: e.reduce_sum(
            out=af(S_WSUMS + 128, 1), in_=af(S_WSUMS, 64), axis=AX.X),
          [("cv", A("cv"))], "cv", 1)
        E("vector", lambda e: e.reduce_sum(
            out=af(S_WSUMS + 129, 1), in_=af(S_WSUMS + 64, 64), axis=AX.X),
          [("cv", A("cv"))], "cv", 1)
        wred_cv = A("cv")
        for k in range(2):
            E("tensor", lambda e, k=k: e.matmul(
                PS4[0:1, k:k + 1], onesf_t[:, 0:1], af(S_WSUMS + 128 + k, 1),
                start=True, stop=True),
              [("cv", wred_cv), ("cs", attn_cs)] +
              (free(("ps4",), ("ps45",)) if k == 0 else []), "ct", 1)
        prod(("ps4",), "ct")
        E("vector", lambda e: e.tensor_copy(
            out=AF_[0:1, S_WSUMS + 130:S_WSUMS + 132], in_=PS4[0:1, 0:2]),
          need(("ps4",)), "cv", 1)
        cons(("ps4",), "cv")
        E("vector", lambda e: e.tensor_scalar(
            out=AF_[0:1, S_WSUMS + 130:S_WSUMS + 132],
            in0=AF_[0:1, S_WSUMS + 130:S_WSUMS + 132],
            scalar1=1.0 / (C * HID), scalar2=1e-5, op0=ALU.mult, op1=ALU.max),
          [("cv", A("cv"))], "cv", 1)
        E("vector", lambda e: e.reciprocal(
            out=AF_[0:1, S_WSUMS + 132:S_WSUMS + 134],
            in_=AF_[0:1, S_WSUMS + 130:S_WSUMS + 132]),
          [("cv", A("cv"))], "cv", 1)
        mw_cv = A("cv")
        E("tensor", lambda e: e.matmul(
            PS4[:, 0:4], onesf_t[0:1, 0:128],
            AF_[0:1, S_WSUMS + 130:S_WSUMS + 134], start=True, stop=True),
          [("cv", mw_cv)] + free(("ps4",)), "ct", 1)
        prod(("ps4",), "ct")
        E("vector", lambda e: e.tensor_copy(out=af(S_WSC, 4), in_=PS4[:, 0:4]),
          need(("ps4",)), "cv", 1)
        cons(("ps4",), "cv")
        wsc_cv = A("cv")


        def p25_tp(i):
            spar = i % 3
            src = ab(R_E + 1024 * spar, 1024)
            dma("sync", src, dxqs[128 * i:128 * (i + 1), :],
                waits=free(("tsrc", spar)) +
                ([("dac", p0_dac), ("sld", attn_sld), ("ct", attn_ct),
                  ("cs", attn_cs), ("cv", attn_cv)] if i == 0 else []),
                sem="ts%d" % spar)
            prod(("tsrc", spar), "ts%d" % spar)
            w32 = af(F_S4 + 1024 * (i % 2), 1024)
            E("scalar", lambda e, src=src, w32=w32: e.activation(
                out=w32, in_=src, func=AF.Copy),
              need(("tsrc", spar)) + free(("w32", i % 2)), "cs", 1)
            cons(("tsrc", spar), "cs")
            prod(("w32", i % 2), "cs")
            for c in range(NCC):
                tb = ("tpb", c % 2)
                w0 = (need(("w32", i % 2)) if c == 0 else []) + free(tb) + \
                    ([("cs", attn_cs), ("ct", attn_ct)] if i == 0 and c < 2 else [])
                E("tensor", lambda e, c=c, w32=w32: e.transpose(
                    psq(c % 2, 0, 128, 0, 128), w32[:, 128 * c:128 * (c + 1)],
                    id_t[:]), w0, "ct", 1)
                prod(tb, "ct")
                if c == NCC - 1:
                    cons(("w32", i % 2), "ct")
                dst = xqvf[:, XQW * c + 1 + 128 * i:XQW * c + 1 + 128 * (i + 1)]
                if c % 2 == 0:
                    E("scalar", lambda e, c=c, dst=dst: e.activation(
                        out=dst, in_=psq(c % 2, 0, 128, 0, 128), func=AF.Copy),
                      need(tb) + ([("cv", guards_cv)] if i == 0 else []), "cs", 1)
                    cons(tb, "cs")
                else:
                    E("vector", lambda e, c=c, dst=dst: e.tensor_copy(
                        out=dst, in_=psq(c % 2, 0, 128, 0, 128)),
                      need(tb) + ([("cv", guards_cv)] if i == 0 else []), "cv", 1)
                    cons(tb, "cv")

        def p25_proj(i):
            par = i % 2
            xbuf = af(F_MSB + 1024 * par, 1024)
            obuf = af(F_S2 + 1024 * par, 1024)
            dma("sync", xbuf, dx[128 * i:128 * (i + 1), :],
                waits=free(("xre", par)) +
                ([("cv", attn_cv), ("cs", attn_cs)] if i < 2 else []),
                sem="xb%d" % par)
            prod(("xre", par), "xb%d" % par)
            pjr = ("psb",) if par == 0 else (("ps4",), ("ps45",))
            for c in range(NCC):
                for sp in range(2):
                    w0 = ([("wo", wout_ld), ("sld", attn_sld),
                           ("cv", attn_cv)] +
                          (free(("psb",)) if par == 0
                           else free(("ps4",), ("ps45",)))
                          if (c, sp) == (0, 0) else [])

                    def pjmm(e, i=i, c=c, sp=sp, par=par):
                        outap = (PSB[:, 512 * sp:512 * (sp + 1)] if par == 0
                                 else (PS4[:] if sp == 0 else PS5[:]))
                        return e.matmul(
                            outap,
                            attnout[:, T * c + 128 * i:T * c + 128 * (i + 1)],
                            ab(R_C + 1024 * c + 512 * sp, 512),
                            start=(c == 0), stop=(c == NCC - 1))
                    E("tensor", pjmm, w0, "ct", 1)
            if par == 0:
                prod(("psb",), "ct")
            else:
                prod(("ps4",), "ct")
                prod(("ps45",), "ct")

            def pjfin(e, xbuf=xbuf, obuf=obuf, par=par):
                if par == 0:
                    return e.scalar_tensor_tensor(
                        out=obuf, in0=PSB[:], scalar=1.0, in1=xbuf,
                        op0=ALU.mult, op1=ALU.add)
                e.scalar_tensor_tensor(
                    out=obuf[:, 0:512], in0=PS4[:], scalar=1.0,
                    in1=xbuf[:, 0:512], op0=ALU.mult, op1=ALU.add)
                return e.scalar_tensor_tensor(
                    out=obuf[:, 512:1024], in0=PS5[:], scalar=1.0,
                    in1=xbuf[:, 512:1024], op0=ALU.mult, op1=ALU.add)
            pw = (need(("psb",)) if par == 0 else need(("ps4",), ("ps45",))) + \
                need(("xre", par)) + free(("obuf", par))
            E("vector", pjfin, pw, "cv", 1)
            if par == 0:
                cons(("psb",), "cv")
            else:
                cons(("ps4",), "cv")
                cons(("ps45",), "cv")
            cons(("xre", par), "cv")
            prod(("obuf", par), "cv")
            dma("sync", daccs[128 * i:128 * (i + 1), :], obuf,
                waits=need(("obuf", par)), sem="dac")
            cons(("obuf", par), "dac")

        wsign2 = [(R_A + 1024 * hc) if hc < 16 else (R_B + 1024 * (hc - 16))
                  for hc in range(NHC)]
        w1q_g = {}
        fcnt = [0]
        cg_hw = A("cg")

        STRB = (14016, 14528, 15040, 15552)

        def item_w2(hc, h):
            k5 = fcnt[0] % 4
            first = fcnt[0] < 4
            fcnt[0] += 1
            buf = af(STRB[k5], 512)
            dma("gpsimd", buf, dw2T[128 * hc:128 * (hc + 1), 512 * h:512 * (h + 1)],
                waits=free(("fcos", k5)) +
                ([("cv", attn_cv), ("cs", attn_cs), ("sld", attn_sld),
                  ("cg", cg_hw)] if first else []), sem="sb%d" % k5)
            prod(("fcos", k5), "sb%d" % k5)
            E("vector", lambda e, buf=buf: e.tensor_scalar(
                out=i32b[:], in0=buf, scalar1=AF_[:, S_WSC + 3:S_WSC + 4],
                scalar2=None, op0=ALU.mult),
              need(("fcos", k5)) + ([("cv", wsc_cv)] if hc == 0 and h == 0
                                    else []), "cv", 1)
            cons(("fcos", k5), "cv")
            E("vector", lambda e, hc=hc, h=h: e.tensor_scalar(
                out=ab(wsign2[hc] + 512 * h, 512), in0=i32b[:],
                scalar1=-1.0, scalar2=1.0, op0=ALU.max, op1=ALU.min),
              [("sld", attn_sld), ("ct", attn_ct)] if hc == 0 and h == 0 else
              ([("ct", proj_ct), ("sld", proj_sld)] if hc == 16 and h == 0
               else []), "cv", 1)

        def item_w1(jj, h):
            b4, c = divmod(jj, 8)
            k5 = fcnt[0] % 4
            first = fcnt[0] < 4
            fcnt[0] += 1
            buf = af(STRB[k5], 512)
            dma("gpsimd", buf,
                dw1T[128 * c:128 * (c + 1),
                     1024 * b4 + 512 * h:1024 * b4 + 512 * (h + 1)],
                waits=free(("fcos", k5)) +
                ([("cv", attn_cv), ("cs", attn_cs), ("sld", attn_sld),
                  ("cg", cg_hw)] if first else []), sem="sb%d" % k5)
            prod(("fcos", k5), "sb%d" % k5)
            E("vector", lambda e, buf=buf: e.tensor_scalar(
                out=i32b[:], in0=buf, scalar1=AF_[:, S_WSC + 2:S_WSC + 3],
                scalar2=None, op0=ALU.mult), need(("fcos", k5)), "cv", 1)
            cons(("fcos", k5), "cv")
            bk = (2 * jj + h) % 8
            bout = ab(R_C + 8192 + 512 * bk, 512)
            E("vector", lambda e, bout=bout: e.tensor_scalar(
                out=bout, in0=i32b[:], scalar1=-1.0, scalar2=1.0,
                op0=ALU.max, op1=ALU.min),
              free(("bout", bk)) + ([("ct", attn_ct)] if 2 * jj + h < 8
                                    else []), "cv", 1)
            prod(("bout", bk), "cv")
            dma("sync",
                dw1q[128 * c:128 * (c + 1),
                     1024 * b4 + 512 * h:1024 * b4 + 512 * (h + 1)],
                bout, waits=need(("bout", bk)), sem="wq")
            cons(("bout", bk), "wq")
            if c == 7 and h == 1:
                w1q_g[b4] = A("wq")

        items = [(lambda hc=hc, h=h: item_w2(hc, h))
                 for hc in range(16) for h in range(2)] + \
                [(lambda jj=jj, h=h: item_w1(jj, h))
                 for jj in range(32) for h in range(2)]
        xqvf_cv = xqvf_cs = None
        for i in range(NTC):
            if i >= 2:
                p25_proj(i - 2)
            p25_tp(i)
            if i == NTC - 1:
                xqvf_cs = A("cs")
                xqvf_cv = A("cv")
            for fn in items[6 * i:6 * i + 6]:
                fn()
        tap("xqvf", xqvf)
        p25_proj(NTC - 2)
        p25_proj(NTC - 1)
        proj_sld = A("sld")
        proj_dac = A("dac")
        proj_ct = A("ct")
        for hc in range(16, 32):
            for h in range(2):
                item_w2(hc, h)
        w2q_cv = A("cv")

        yt2 = af(F_YT, 8192)

        def xq2b(ti):
            return ab(R_C + 8192 * (ti % 2), 8192)
        pend_silu = []
        pend_T = []
        pend_sT = []
        pend_BC = []
        pend_R = []
        rg_cells = [[Cell() for _ in range(4)] for _ in range(NFT)]
        s2bcs_ev = {}
        yt2ev = {}
        S2BCS = 14016

        def flush_R():
            if pend_R:
                grp = pend_R.pop(0)
                for fn in grp:
                    fn()

        def flush_T():
            while pend_T:
                qti, tw, cT = pend_T.pop(0)
                for k in range(2):
                    E("tensor", lambda e, k=k: e.transpose(
                        psq(2, 0, 128, 128 * k, 128 * (k + 1)),
                        af(F_FOLD + 128 * k, 128), id_t[:]),
                      (need(("fold",)) + tw) if k == 0 else [], "ct", 1)
                    em.fill(cT[k], "ct")
                cons(("fold",), "ct")

        def flush_sT():
            while pend_sT:
                qti, s2cv, fw, cT_s = pend_sT.pop(0)
                E("tensor", lambda e, qti=qti: e.transpose(
                    PS5[0:2, 0:128],
                    AF_[:, S_S2SC + 2 * qti:S_S2SC + 2 * qti + 2], id_t[:]),
                  [("cv", s2cv)] + fw, "ct", 1)
                em.fill(cT_s, "ct")

        def flush_BC():
            while pend_BC:
                qti, gat_sg, cBC = pend_BC.pop(0)
                E("tensor", lambda e: e.matmul(
                    PS4[:, 0:FT], onesf_t[64:65, 0:128],
                    AF_[64:65, S_S2ROW + 128:S_S2ROW + 128 + FT],
                    start=True, stop=True), [("sg", gat_sg)], "ct", 1)
                cons(("s2rg",), "ct")
                em.fill(cBC, "ct")

        def flush_silu():
            while pend_silu:
                fn, waits, key = pend_silu.pop(0)
                E("scalar", fn, waits, "cs", 1)
                yt2ev[key] = ("cs", A("cs"))

        def p3_h(ti):
            a = FT * ti
            hspan = FT + 2
            for hc in range(NHC):
                par = hc % 2
                p4 = hc % 4
                hbr = ("hb", p4)
                hb = PS[p4]
                sec = ab(R_E + 1024 * par, 1024)
                dma("sync", sec.rearrange("p (c n) -> p c n", c=NCC),
                    dw1q[:, 128 * hc:128 * (hc + 1)].rearrange(
                        "(c p) n -> p c n", c=NCC),
                    waits=[("wq", w1q_g[hc // 8]), ("cv", xqvf_cv),
                           ("cs", xqvf_cs)] + free(("w1sec", par)),
                    sem="ws%d" % par)
                prod(("w1sec", par), "ws%d" % par)
                for c in range(NCC):
                    w0 = (need(("w1sec", par)) + free(hbr) +
                          ([("cs", xqvf_cs), ("cv", xqvf_cv), ("ct", attn_ct)]
                           if ti == 0 and hc < 4 else [])
                          if c == 0 else [])
                    E("tensor", lambda e, a=a, hspan=hspan, c=c, hb=hb, sec=sec:
                        e.matmul(
                        hb[:, 0:hspan], sec[:, 128 * c:128 * (c + 1)],
                        xqvf[:, XQW * c + a:XQW * c + a + hspan],
                        start=(c == 0), stop=(c == NCC - 1)), w0, "ct", 1)
                prod(hbr, "ct")
                cons(("w1sec", par), "ct")
                csc = af(F_CONV0 + 256 * par, FT)
                cscr = ("csc", par)
                E("scalar", lambda e, hb=hb, hc=hc, csc=csc: e.activation(
                    out=csc, in_=hb[:, 2:2 + FT], func=AF.Copy,
                    scale=taps_t[:, 64 + hc:65 + hc]),
                  need(hbr) + free(cscr), "cs", 1)
                cons(hbr, "cs")
                prod(cscr, "cs")
                E("vector", lambda e, hb=hb, hc=hc, csc=csc: e.scalar_tensor_tensor(
                    out=csc, in0=hb[:, 0:FT], scalar=taps_t[:, hc:hc + 1],
                    in1=csc, op0=ALU.mult, op1=ALU.add),
                  need(cscr, hbr), "cv", 1)
                prod(cscr, "cv")
                E("vector", lambda e, hb=hb, hc=hc, csc=csc: e.scalar_tensor_tensor(
                    out=csc, in0=hb[:, 1:1 + FT], scalar=taps_t[:, 32 + hc:33 + hc],
                    in1=csc, op0=ALU.mult, op1=ALU.add),
                  need(cscr), "cv", 1)
                cons(hbr, "cv")
                prod(cscr, "cv")
                silu_waits = (need(cscr) + [("cv", wsc_cv)] +
                              ([rg_cells[ti - 1][hc // 8]] if ti >= 1 else []))
                pend_silu.append((
                    (lambda e, hc=hc, csc=csc: e.activation(
                        out=yt2[:, FT * hc:FT * (hc + 1)], in_=csc, func=AF.Silu,
                        scale=AF_[:, S_WSC:S_WSC + 1], bias=convb_t[:, hc:hc + 1])),
                    silu_waits, ti))
                cons(cscr, "cs")
                if len(pend_silu) > 1:
                    fn, waits, key = pend_silu.pop(0)
                    E("scalar", fn, waits, "cs", 1)
                    yt2ev[key] = ("cs", A("cs"))
                if hc == 1:
                    flush_T()
                    flush_R()
                elif hc == 2:
                    flush_sT()
                elif hc == 3:
                    flush_BC()
                elif hc in (5, 9, 13):
                    flush_R()

        def p3_quant(ti):
            flush_silu()
            xq2 = xq2b(ti)
            sev = yt2ev[ti]
            E("vector", lambda e: e.tensor_reduce(
                out=af(F_FOLD, FT), in_=yt2.rearrange("p (c t) -> p t c", c=NHC),
                axis=AX.X, op=ALU.max, apply_absolute_value=True),
              [sev] + free(("fold",)), "cv", 1)
            prod(("fold",), "cv")
            cT = [Cell(), Cell()]
            pend_T.append((ti, free(("hb", 2)), cT))
            for k in range(2):
                gcol = 2 * ti + k
                E("vector", lambda e, k=k: e.reduce_max(
                    out=af(S_TMX + k, 1),
                    in_=psq(2, 0, 128, 128 * k, 128 * (k + 1)), axis=AX.X),
                  [cT[k]], "cv", 1)
                cons(("hb", 2), "cv")
                E("vector", lambda e, k=k, gcol=gcol: e.tensor_scalar(
                    out=af(S_FSC + gcol, 1), in0=af(S_TMX + k, 1), scalar1=1e-5,
                    scalar2=1.0 / 127.0, op0=ALU.max, op1=ALU.mult),
                  [("cv", A("cv"))], "cv", 1)
                E("vector", lambda e, gcol=gcol: e.reciprocal(
                    out=af(S_S2SC + gcol, 1), in_=af(S_FSC + gcol, 1)),
                  [("cv", A("cv"))], "cv", 1)
                E("vector", lambda e, gcol=gcol: e.tensor_scalar(
                    out=af(S_FSC + gcol, 1), in0=af(S_FSC + gcol, 1),
                    scalar1=AF_[:, S_WSC + 1:S_WSC + 2], scalar2=None,
                    op0=ALU.mult), [("cv", A("cv"))], "cv", 1)
            s2_cv = A("cv")
            cT_s = Cell()
            cBC = Cell()
            fw = free(("w2b",))
            pend_sT.append((ti, s2_cv, fw, cT_s))
            E("vector", lambda e: e.tensor_copy(
                out=AF_[0:2, S_S2ROW:S_S2ROW + 128], in_=PS5[0:2, 0:128]),
              [cT_s] + free(("s2row",)), "cv", 1)
            prod(("s2row",), "cv")
            for k in range(2):
                E("gpsimd", lambda e, k=k: e.dma_start(
                    out=AF_[64:65, S_S2ROW + 128 + 128 * k:
                            S_S2ROW + 256 + 128 * k],
                    in_=AF_[k:k + 1, S_S2ROW:S_S2ROW + 128]),
                  (need(("s2row",)) + free(("s2rg",))) if k == 0 else [],
                  "sg", 16)
            cons(("s2row",), "sg")
            prod(("s2rg",), "sg")
            pend_BC.append((ti, A("sg"), cBC))
            E("vector", lambda e: e.tensor_copy(
                out=af(S2BCS, FT), in_=PS4[:, 0:FT]),
              [cBC] + free(("s2bcs",)) +
              ([("cv", w2q_cv)] if ti == 0 else []), "cv", 1)
            prod(("s2bcs",), "cv")
            s2bcs_ev[ti] = ("cv", A("cv"))

            def mk_round(hc, xq2=xq2, ti=ti, sev=sev):
                def emit_r():
                    E("vector", lambda e, hc=hc: e.tensor_tensor(
                        out=i32b[:, 0:FT], in0=yt2[:, FT * hc:FT * (hc + 1)],
                        in1=af(S2BCS, FT), op=ALU.mult),
                      (need(("s2bcs",)) if hc % 8 == 0 else []) + [sev],
                      "cv", 1)
                    cons(("s2bcs",), "cv")
                    E("vector", lambda e, hc=hc: e.tensor_copy(
                        out=xq2[:, FT * hc:FT * (hc + 1)], in_=i32b[:, 0:FT]),
                      [("cv", A("cv"))] + free(("xq2", ti % 2, hc)), "cv", 1)
                    prod(("xq2", ti % 2, hc), "cv")
                    if hc % 8 == 7:
                        em.fill(rg_cells[ti][hc // 8], "cv")
                return emit_r
            for g in range(4):
                pend_R.append([mk_round(8 * g + j) for j in range(8)])

        def p3_w2(ti):
            a = FT * ti
            xq2 = xq2b(ti)
            for hc in range(NHC):
                for sp in range(2):
                    w0 = (need(("xq2", ti % 2, hc)) +
                          ([("cv", w2q_cv)] if ti == 0 and hc == 0 else []) +
                          (free(("w2a",), ("psb",)) if hc == 0 and sp == 0
                           else []))
                    E("tensor", lambda e, hc=hc, sp=sp: e.matmul(
                        PSB[:, 512 * sp:512 * (sp + 1)],
                        xq2[:, FT * hc:FT * hc + 128],
                        ab(wsign2[hc] + 512 * sp, 512),
                        start=(hc == 0), stop=(hc == NHC - 1)), w0, "ct", 1)
                    w1_ = (free(("w2b",), ("ps4",), ("ps45",)) +
                           [s2bcs_ev[ti]]
                           if hc == 0 and sp == 0 else [])
                    E("tensor", lambda e, hc=hc, sp=sp: e.matmul(
                        PS4[:] if sp == 0 else PS5[:],
                        xq2[:, FT * hc + 128:FT * hc + 256],
                        ab(wsign2[hc] + 512 * sp, 512),
                        start=(hc == 0), stop=(hc == NHC - 1)), w1_, "ct", 1)
                cons(("xq2", ti % 2, hc), "ct")
            prod(("w2a",), "ct")
            prod(("w2b",), "ct")
            for k in range(2):
                gcol = 2 * ti + k
                t0 = a + 128 * k
                abuf = af(F_XNC3 + 1024 * k, 1024)
                ob = af(F_OUTC + 1024 * k, 1024)
                dma("sync", abuf, daccs[t0:t0 + 128, :],
                    waits=free(("abuf", k)) + [("dac", proj_dac)],
                    sem="ab%d" % k)
                prod(("abuf", k), "ab%d" % k)
                wres = ("w2a",) if k == 0 else ("w2b",)
                for half in range(2):
                    def fin(e, k=k, gcol=gcol, abuf=abuf, ob=ob, half=half):
                        src = (PSB[:, 512 * half:512 * (half + 1)] if k == 0
                               else (PS4[:] if half == 0 else PS5[:]))
                        return e.scalar_tensor_tensor(
                            out=ob[:, 512 * half:512 * (half + 1)], in0=src,
                            scalar=af(S_FSC + gcol, 1),
                            in1=abuf[:, 512 * half:512 * (half + 1)],
                            op0=ALU.mult, op1=ALU.add)
                    E("vector", fin,
                      need(wres, ("abuf", k)) + free(("outc", k)), "cv", 1)
                cons(wres, "cv")
                cons(("abuf", k), "cv")
                prod(("outc", k), "cv")
                dma("sync", dout[t0:t0 + 128, :], ob, waits=need(("outc", k)))
                cons(("outc", k), "sld")

        for ti in range(NFT):
            p3_h(ti)
            if ti >= 1:
                p3_w2(ti - 1)
            p3_quant(ti)
        flush_T()
        flush_sT()
        flush_BC()
        while pend_R:
            flush_R()
        p3_w2(NFT - 1)
        if DEBUG_TAPS:
            srcs = {"biasM": af(S_BIASM, 16),
                    "wsc": af(S_WSC, 4), "y0": yt2, "xq2t0": xq2b(NFT - 1),
                    "s2": af(S_S2SC, 16),
                    "abuf1": af(F_XNC3 + 1024, 1024),
                    "ob1": af(F_OUTC + 1024, 1024)}
            for name in DEBUG_TAPS:
                if name in srcs:
                    dma("sync", dbg[name], srcs[name],
                        waits=[("cv", A("cv")), ("cs", A("cs")), ("ct", A("ct")),
                               ("gld", A("gld"))])
        E("sync", lambda e: e.nop(),
          [(s, A(s)) for s in ("sld", "gld", "tsp", "ts0", "ts1", "ts2",
                               "xb0", "xb1", "ab0", "ab1", "ws0", "ws1",
                               "sb0", "sb1", "sb2", "sb3", "wo", "wq",
                               "dac", "sg")], None)

        block = st.enter_context(nc.Block())
        em.finalize(block, sems)
    return nc


def kernel(**inputs):
    in_maps = _host_inputs(**inputs)
    nc = build_nc()
    res = run_bass_kernel_spmd(nc, in_maps, core_ids=list(range(8)))
    out = np.stack([np.asarray(r["out"]) for r in res.results], axis=0)
    return out.astype(np.float32, copy=False)


# version 13
# speedup vs baseline: 1.1048x; 1.0015x over previous
import numpy as np
import ml_dtypes
from contextlib import ExitStack

import concourse.bass as bass
import concourse.mybir as mybir
from concourse.bass_utils import run_bass_kernel_spmd

F32 = mybir.dt.float32
BF16 = mybir.dt.bfloat16
I32 = mybir.dt.int32
AF = mybir.ActivationFunctionType
ALU = mybir.AluOpType
AX = mybir.AxisListType
BF = ml_dtypes.bfloat16

T = 2048
C = 1024
H = 16
D = 64
HID = 4096
NTC = T // 128
NCC = C // 128
NHC = HID // 128
EPS_LN = 1e-5
EPS_RMS = 1e-6
FT = 256
NFT = T // FT

DEBUG_TAPS = []


class Cell:
    __slots__ = ("sem", "v")

    def __init__(self):
        self.sem = None
        self.v = None


class Em:

    def __init__(self):
        self.q = {e: [] for e in ("tensor", "vector", "scalar", "gpsimd", "sync")}
        self.cnt = {}
        self.resp = {}
        self.resc = {}

    def fill(self, cell, sem):
        cell.sem = sem
        cell.v = self.at(sem)

    def at(self, sem):
        return self.cnt.get(sem, 0)

    def emit(self, eng, fn, waits=(), inc=None, incn=1):
        self.q[eng].append((fn, tuple(waits), inc, incn))
        if inc is not None:
            self.cnt[inc] = self.cnt.get(inc, 0) + incn

    def prod(self, res, sem):
        self.resp[res] = (sem, self.at(sem))
        self.resc[res] = {}

    def cons(self, res, sem):
        d = self.resc.setdefault(res, {})
        d[sem] = self.at(sem)

    def need(self, *ress):
        out = []
        for r in ress:
            p = self.resp.get(r)
            if p:
                out.append(p)
        return out

    def free(self, *ress):
        out = []
        for r in ress:
            out.extend(self.resc.get(r, {}).items())
        return out

    def finalize(self, block, sems):
        def mk(name):
            ops = self.q[name]

            def run(e):
                hw = {}
                for fn, waits, inc, incn in ops:
                    merged = {}
                    for w in waits:
                        if isinstance(w, Cell):
                            assert w.v is not None, "unfilled Cell wait"
                            s, v = w.sem, w.v
                        else:
                            s, v = w
                        if v > 0 and v > merged.get(s, 0):
                            merged[s] = v
                    for s, v in merged.items():
                        if v > hw.get(s, 0):
                            e.wait_ge(sems[s], v)
                            hw[s] = v
                    inst = fn(e)
                    if inc is not None:
                        inst.then_inc(sems[inc], incn)
            return run
        block.tensor(mk("tensor"))
        block.vector(mk("vector"))
        block.scalar(mk("scalar"))
        block.gpsimd(mk("gpsimd"))
        block.sync(mk("sync"))


def _host_consts():
    half = D // 2
    inv_freq = 1.0 / (10000.0 ** (np.arange(0, half, dtype=np.float64) * 2.0 / D))
    ang = np.arange(T, dtype=np.float64)[:, None] * inv_freq[None, :]
    cosv = np.cos(ang).astype(np.float32)
    sinv = np.sin(ang).astype(np.float32)
    cosrep = np.zeros((128, T), np.float32)
    sinpat = np.zeros((128, T), np.float32)
    for p in range(128):
        j = p % half
        cosrep[p] = cosv[:, j]
        sinpat[p] = (-sinv[:, j]) if (p % D) < half else sinv[:, j]
    trimask = np.zeros((128, 128), np.float32)
    for kt in range(128):
        trimask[kt, kt:] = 1.0
    ident = np.eye(128, dtype=np.float32)
    onesf = np.ones((128, 128), np.float32)
    ones2 = np.zeros((128, 2), np.float32)
    ones2[0:64, 0] = 1.0
    ones2[64:128, 1] = 1.0
    sel01 = np.zeros((2, 256), np.float32)
    sel01[0, 0:128] = 1.0
    sel01[1, 128:256] = 1.0
    smalls = np.zeros((128, 8), np.float32)
    smalls[:, 0] = 127.0
    smalls[:, 1] = 1e-5
    smalls[:, 2] = EPS_RMS
    smalls[:, 3] = 1.0
    smalls[:, 5] = EPS_LN
    return dict(cosrep=cosrep, sinpat=sinpat,
                trimask=np.ascontiguousarray(trimask.astype(BF)),
                ident=ident, onesf=onesf, ones2=ones2, sel01=sel01, smalls=smalls)


def _host_inputs(L0_attn_x, L0_attn_w_qkv, L0_attn_w_out, L0_attn_q_norm_w,
                 L0_attn_k_norm_w, L0_ffn_w1, L0_ffn_conv_w, L0_ffn_conv_b,
                 L0_ffn_w2):
    consts = _host_consts()
    wqkv = np.asarray(L0_attn_w_qkv)
    wqkT = np.ascontiguousarray(wqkv[0:2 * C].T)
    wvT = np.ascontiguousarray(wqkv[2 * C:3 * C].T)
    woutT = np.ascontiguousarray(np.asarray(L0_attn_w_out).T)
    w1T = np.ascontiguousarray(np.asarray(L0_ffn_w1).T)
    w2T = np.ascontiguousarray(np.asarray(L0_ffn_w2).T)
    qn = np.asarray(L0_attn_q_norm_w).astype(np.float32, copy=False)
    kn = np.asarray(L0_attn_k_norm_w).astype(np.float32, copy=False)
    qn_stat = np.zeros((2, 128), np.float32)
    qn_stat[0, 0:64] = qn
    qn_stat[1, 64:128] = qn
    kn_stat = np.zeros((2, 128), np.float32)
    kn_stat[0, 0:64] = kn
    kn_stat[1, 64:128] = kn
    cw = np.asarray(L0_ffn_conv_w).reshape(HID, 3)
    taps = np.zeros((128, 96), np.float32)
    convb = np.zeros((128, 32), np.float32)
    cb = np.asarray(L0_ffn_conv_b)
    for c in range(NHC):
        taps[:, c] = cw[128 * c:128 * (c + 1), 0]
        taps[:, 32 + c] = cw[128 * c:128 * (c + 1), 1]
        taps[:, 64 + c] = cw[128 * c:128 * (c + 1), 2]
        convb[:, c] = cb[128 * c:128 * (c + 1)]
    base = dict(wqkT=wqkT, wvT=wvT, woutT=woutT, w1T=w1T, w2T=w2T,
                qn_stat=qn_stat, kn_stat=kn_stat, taps=taps, convb=convb, **consts)
    xs = np.asarray(L0_attn_x)
    return [dict(base, x=np.ascontiguousarray(xs[b], dtype=np.float32))
            for b in range(8)]


def build_nc():
    nc = bass.Bass("TRN2", target_bir_lowering=False)

    dx = nc.dram_tensor("x", [T, C], F32, kind="ExternalInput").ap()
    dwqkT = nc.dram_tensor("wqkT", [C, 2 * C], F32, kind="ExternalInput").ap()
    dwvT = nc.dram_tensor("wvT", [C, C], F32, kind="ExternalInput").ap()
    dwoutT = nc.dram_tensor("woutT", [C, C], F32, kind="ExternalInput").ap()
    dw1T = nc.dram_tensor("w1T", [C, HID], F32, kind="ExternalInput").ap()
    dw2T = nc.dram_tensor("w2T", [HID, C], F32, kind="ExternalInput").ap()
    dcos = nc.dram_tensor("cosrep", [128, T], F32, kind="ExternalInput").ap()
    dsin = nc.dram_tensor("sinpat", [128, T], F32, kind="ExternalInput").ap()
    dtri = nc.dram_tensor("trimask", [128, 128], BF16, kind="ExternalInput").ap()
    dident = nc.dram_tensor("ident", [128, 128], F32, kind="ExternalInput").ap()
    donesf = nc.dram_tensor("onesf", [128, 128], F32, kind="ExternalInput").ap()
    dones2 = nc.dram_tensor("ones2", [128, 2], F32, kind="ExternalInput").ap()
    dsel01 = nc.dram_tensor("sel01", [2, 256], F32, kind="ExternalInput").ap()
    dsmalls = nc.dram_tensor("smalls", [128, 8], F32, kind="ExternalInput").ap()
    dqn = nc.dram_tensor("qn_stat", [2, 128], F32, kind="ExternalInput").ap()
    dkn = nc.dram_tensor("kn_stat", [2, 128], F32, kind="ExternalInput").ap()
    dtaps = nc.dram_tensor("taps", [128, 96], F32, kind="ExternalInput").ap()
    dconvb = nc.dram_tensor("convb", [128, 32], F32, kind="ExternalInput").ap()

    dxqs = nc.dram_tensor("xqs", [T, C], BF16, kind="Internal").ap()
    daccs = nc.dram_tensor("accs", [T, C], F32, kind="Internal").ap()
    dw1q = nc.dram_tensor("w1q", [C, HID], BF16, kind="Internal").ap()
    ds2 = nc.dram_tensor("ds2", [16, 256], F32, kind="Internal").ap()
    dout = nc.dram_tensor("out", [T, C], F32, kind="ExternalOutput").ap()

    dbg = {}
    if DEBUG_TAPS:
        shapes = {"xnf": ([128, NCC * T], BF16), "qhat": ([128, NCC * T], BF16),
                  "khat": ([128, NCC * T], BF16), "vt": ([128, NTC * H * 65], BF16),
                  "attnout": ([128, NCC * T], BF16), "biasM": ([128, 16], F32),
                  "wsc": ([128, 4], F32), "y0": ([128, NHC * FT], F32),
                  "xq2t0": ([128, NHC * FT], BF16), "s2": ([128, 16], F32),
                  "xqvf": ([128, NCC * 2052], BF16),
                  "abuf1": ([128, 1024], F32), "ob1": ([128, 1024], F32)}
        for name in DEBUG_TAPS:
            shp, dt = shapes[name]
            dbg[name] = nc.dram_tensor("dbg_" + name, shp, dt,
                                       kind="ExternalOutput").ap()

    em = Em()
    E, A = em.emit, em.at
    need, free, prod, cons = em.need, em.free, em.prod, em.cons

    st = ExitStack()
    with st:
        AB = st.enter_context(nc.sbuf_tensor("AB", [128, 70400], BF16))
        AF_ = st.enter_context(nc.sbuf_tensor("AFA", [128, 16064], F32))
        i32b = st.enter_context(nc.sbuf_tensor("i32b", [128, 512], I32))

        R_A, R_B, R_C, R_D, R_E = 0, 16384, 32768, 49152, 65792

        def ab(off, n):
            return AB[:, off:off + n]

        F_COS, F_SIN = 0, 2048
        F_S1, F_S2, F_S4 = 4096, 6144, 8192
        F_MSB = 10240
        F_WFS = 12288
        S_STATS = 13312
        S_S1COL = 13376
        S_R1COL = 13392
        S_MAXQ = 13408
        S_MAXK = 13424
        S_BIASM = 13440
        S_WSUMS = 13456
        S_WSC = 13590
        S_S2SC = 13594
        S_FSC = 13610
        S_TMX = 13626
        S_S2ROW = 13632
        F_WABS = 13632
        F_WRS1 = 14848
        F_WRS2 = 15360
        F_YT = 0
        F_CONV0 = 8192
        F_CONV1 = 8448
        F_FOLD = 8704
        F_XNC3 = 8960
        F_OUTC = 11008

        def af(off, n):
            return AF_[:, off:off + n]

        cos_t = af(F_COS, 2048)
        sin_t = af(F_SIN, 2048)
        tri_t = st.enter_context(nc.sbuf_tensor("tri_t", [128, 128], BF16))
        id_t = st.enter_context(nc.sbuf_tensor("id_t", [128, 128], F32))
        onesf_t = st.enter_context(nc.sbuf_tensor("onesf_t", [128, 128], F32))
        ones2_t = st.enter_context(nc.sbuf_tensor("ones2_t", [128, 2], F32))
        sel_t = st.enter_context(nc.sbuf_tensor("sel_t", [2, 256], F32))
        sm_t = st.enter_context(nc.sbuf_tensor("sm_t", [128, 8], F32))
        qn_t = st.enter_context(nc.sbuf_tensor("qn_t", [2, 128], F32))
        kn_t = st.enter_context(nc.sbuf_tensor("kn_t", [2, 128], F32))
        taps_t = st.enter_context(nc.sbuf_tensor("taps_t", [128, 96], F32))
        convb_t = st.enter_context(nc.sbuf_tensor("convb_t", [128, 32], F32))

        PSQ = st.enter_context(nc.psum_tensor("psq", [128, 2048], F32))
        PS4 = st.enter_context(nc.psum_tensor("ps4", [128, 512], F32))
        PS5 = st.enter_context(nc.psum_tensor("ps5", [128, 512], F32))
        PSB = st.enter_context(nc.psum_tensor("psb", [128, 1024], F32))
        PS = [PSQ[:, 512 * i:512 * (i + 1)] for i in range(4)] + [PS4[:], PS5[:]]

        def psq(i, p0, p1, c0, c1):
            return PSQ[p0:p1, 512 * i + c0:512 * i + c1]

        sems = {n: st.enter_context(nc.semaphore(n))
                for n in ("sld", "gld", "tsp", "ct", "cv", "cs", "cg",
                          "ts0", "ts1", "ts2", "xb0", "xb1", "ab0", "ab1",
                          "ws0", "ws1", "sb0", "sb1", "sb2", "sb3",
                          "wo", "wq", "dac", "sg")}

        def dma(eng, out, in_, waits=(), sem=None):
            sem = sem or ("gld" if eng == "gpsimd" else "sld")
            E(eng, lambda e, o=out, i=in_: e.dma_start(out=o, in_=i), waits, sem, 16)
            return sem

        def tap(name, src_ap):
            if name not in DEBUG_TAPS:
                return
            dma("sync", dbg[name], src_ap,
                waits=[("cv", A("cv")), ("cs", A("cs")), ("ct", A("ct")),
                       ("gld", A("gld")), ("tsp", A("tsp"))])
            bar = A("sld")
            for eng, s in (("tensor", "ct"), ("vector", "cv"), ("scalar", "cs")):
                E(eng, lambda e: e.nop(), [("sld", bar)], s, 1)

        for o, i in [(cos_t, dcos), (sin_t, dsin), (tri_t[:], dtri), (id_t[:], dident),
                     (onesf_t[:], donesf), (ones2_t[:], dones2), (sel_t[:], dsel01),
                     (sm_t[:], dsmalls), (qn_t[:], dqn), (kn_t[:], dkn),
                     (taps_t[:], dtaps), (convb_t[:], dconvb)]:
            dma("sync", o, i)
        consts_ld = A("sld")
        for eng in ("tensor", "vector", "scalar"):
            E(eng, lambda e: e.nop(), [("sld", consts_ld)],
              {"tensor": "ct", "vector": "cv", "scalar": "cs"}[eng], 1)
        for c in range(NCC):
            dma("gpsimd", ab(R_B + 1024 * c, 1024), dwvT[128 * c:128 * (c + 1), :])
        wv_ld = A("gld")

        xnf = ab(R_A, 16384)
        vt = ab(R_D, 16640)
        E("vector", lambda e: e.memset(
            vt.rearrange("p (j h o) -> p j h o", j=NTC, h=H)[:, :, :, 64:65], 1.0),
          [], "cv", 1)
        vones_cv = A("cv")

        def p1v_j(j, evcs):
            for c in range(NCC):
                for sp in range(2):
                    w0 = ([("gld", wv_ld), ("cs", evcs)] + free(("psb",))
                          if (c, sp) == (0, 0) else [])
                    E("tensor", lambda e, j=j, c=c, sp=sp: e.matmul(
                        PSB[:, 512 * sp:512 * (sp + 1)],
                        xnf[:, T * c + 128 * j:T * c + 128 * (j + 1)],
                        ab(R_B + 1024 * c + 512 * sp, 512),
                        start=(c == 0), stop=(c == NCC - 1)), w0, "ct", 1)
            prod(("psb",), "ct")
            E("vector", lambda e, j=j: e.tensor_copy(
                out=ab(R_D + 1040 * j, 1040).rearrange(
                    "p (h o) -> p h o", o=65)[:, :, 0:64],
                in_=PSB[:].rearrange("p (h o) -> p h o", h=H)),
              need(("psb",)), "cv", 1)
            cons(("psb",), "cv")

        for i in range(NTC):
            par = i % 2
            xbuf = af(F_S1 + 1024 * par, 1024)
            nbuf = af(F_S4 + 1024 * par, 1024)
            sqs = af(F_MSB, 1024)
            qb16 = ab(R_B + 10240 + 1024 * par, 1024)
            dma("sync", xbuf, dx[128 * i:128 * (i + 1), :],
                waits=free(("xbuf", par)))
            prod(("xbuf", par), "sld")
            E("vector", lambda e, xb=xbuf: e.reduce_sum(
                out=af(S_STATS + 0, 1), in_=xb, axis=AX.X),
              need(("xbuf", par)), "cv", 1)
            E("scalar", lambda e, xb=xbuf, sqs=sqs: e.activation(
                out=sqs, in_=xb, func=AF.Square, accum_out=af(S_STATS + 1, 1)),
              need(("xbuf", par)), "cs", 1)
            cons(("xbuf", par), "cs")
            sq_ev = ("cs", A("cs"))
            E("vector", lambda e: e.tensor_scalar(
                out=af(S_STATS + 2, 1), in0=af(S_STATS + 0, 1), scalar1=1.0 / C,
                scalar2=None, op0=ALU.mult), [("cv", A("cv"))], "cv", 1)
            E("vector", lambda e: e.scalar_tensor_tensor(
                out=af(S_STATS + 3, 1), in0=af(S_STATS + 2, 1), scalar=-1.0,
                in1=af(S_STATS + 2, 1), op0=ALU.mult, op1=ALU.mult),
              [("cv", A("cv"))], "cv", 1)
            E("vector", lambda e: e.scalar_tensor_tensor(
                out=af(S_STATS + 3, 1), in0=af(S_STATS + 1, 1), scalar=1.0 / C,
                in1=af(S_STATS + 3, 1), op0=ALU.mult, op1=ALU.add),
              [("cv", A("cv")), sq_ev], "cv", 1)
            E("scalar", lambda e: e.activation(
                out=af(S_STATS + 3, 1), in_=af(S_STATS + 3, 1), func=AF.Sqrt,
                bias=sm_t[:, 5:6]), [("cv", A("cv"))], "cs", 1)
            E("vector", lambda e: e.reciprocal(
                out=af(S_STATS + 3, 1), in_=af(S_STATS + 3, 1)),
              [("cs", A("cs"))], "cv", 1)
            E("vector", lambda e: e.scalar_tensor_tensor(
                out=af(S_STATS + 2, 1), in0=af(S_STATS + 2, 1), scalar=-1.0,
                in1=af(S_STATS + 3, 1), op0=ALU.mult, op1=ALU.mult),
              [("cv", A("cv"))], "cv", 1)
            E("vector", lambda e, xb=xbuf, nb=nbuf: e.tensor_scalar(
                out=nb, in0=xb, scalar1=af(S_STATS + 3, 1),
                scalar2=af(S_STATS + 2, 1), op0=ALU.mult, op1=ALU.add),
              [("cv", A("cv"))] + free(("nbuf", par)), "cv", 1)
            cons(("xbuf", par), "cv")
            prod(("nbuf", par), "cv")
            for c in range(NCC):
                tb = ("tpb", c % 2)
                E("tensor", lambda e, c=c, nb=nbuf: e.transpose(
                    psq(c % 2, 0, 128, 0, 128), nb[:, 128 * c:128 * (c + 1)],
                    id_t[:]),
                  (need(("nbuf", par)) if c == 0 else []) + free(tb), "ct", 1)
                prod(tb, "ct")
                if c == NCC - 1:
                    cons(("nbuf", par), "ct")
                E("scalar", lambda e, c=c, i=i: e.activation(
                    out=xnf[:, T * c + 128 * i:T * c + 128 * (i + 1)],
                    in_=psq(c % 2, 0, 128, 0, 128), func=AF.Copy),
                  need(tb), "cs", 1)
                cons(tb, "cs")
            p0_evcs = A("cs")
            E("vector", lambda e, nb=nbuf, i=i: e.reduce_max(
                out=af(S_S1COL + i, 1), in_=nb, axis=AX.X, apply_absolute_value=True),
              [], "cv", 1)
            E("vector", lambda e, i=i: e.tensor_scalar(
                out=af(S_R1COL + i, 1), in0=af(S_S1COL + i, 1), scalar1=1e-5,
                scalar2=1.0 / 127.0, op0=ALU.max, op1=ALU.mult),
              [("cv", A("cv"))], "cv", 1)
            E("vector", lambda e, i=i: e.reciprocal(
                out=af(S_S1COL + i, 1), in_=af(S_R1COL + i, 1)),
              [("cv", A("cv"))], "cv", 1)
            for hh in (0, 1):
                E("vector", lambda e, nb=nbuf, i=i, hh=hh: e.tensor_scalar(
                    out=i32b[:], in0=nb[:, 512 * hh:512 * (hh + 1)],
                    scalar1=af(S_S1COL + i, 1), scalar2=None,
                    op0=ALU.mult), [("cv", A("cv"))], "cv", 1)
                E("vector", lambda e, i=i, o=qb16, hh=hh: e.tensor_scalar(
                    out=o[:, 512 * hh:512 * (hh + 1)], in0=i32b[:],
                    scalar1=af(S_R1COL + i, 1), scalar2=None,
                    op0=ALU.mult),
                  [("cv", A("cv"))] + (free(("qb16", par)) if hh == 0 else []),
                  "cv", 1)
            cons(("nbuf", par), "cv")
            prod(("qb16", par), "cv")
            dma("sync", dxqs[128 * i:128 * (i + 1), :], qb16,
                waits=need(("qb16", par)), sem="dac")
            cons(("qb16", par), "dac")
            p1v_j(i, p0_evcs)
        cons(("qk",), "cs")
        p1v_cv = A("cv")
        p0_dac = A("dac")
        tap("xnf", ab(R_A, 16384))

        wjobs = []
        for c in range(NCC):
            for b4 in range(8):
                wjobs.append(dw1T[128 * c:128 * (c + 1), 512 * b4:512 * (b4 + 1)])
        for c in range(NHC):
            for b2 in range(2):
                wjobs.append(dw2T[128 * c:128 * (c + 1), 512 * b2:512 * (b2 + 1)])
        WBUFS = [F_WABS, F_WRS1, F_WRS2]
        wj_state = {"next": 0}

        def emit_wjobs(n):
            for _ in range(n):
                j = wj_state["next"]
                if j >= len(wjobs) + 3:
                    return
                r = j - 3
                if 0 <= r < len(wjobs):
                    E("vector", lambda e, b=af(WBUFS[r % 3], 512), r=r:
                        e.reduce_sum(out=af(S_WSUMS + r, 1), in_=b, axis=AX.X,
                                     apply_absolute_value=True),
                      need(("wbuf", r % 3)), "cv", 1)
                    cons(("wbuf", r % 3), "cv")
                if j < len(wjobs):
                    dma("gpsimd", af(WBUFS[j % 3], 512), wjobs[j],
                        waits=free(("wbuf", j % 3)))
                    prod(("wbuf", j % 3), "gld")
                wj_state["next"] = j + 1

        tap("vt", ab(R_D, 16640))

        qhat = ab(R_B, 16384)
        khat = ab(R_C, 16384)
        s1 = af(F_S1, 2048)
        s2b = af(F_S2, 2048)
        s4 = af(F_S4, 2048)

        P1 = [dict() for _ in range(16)]

        def p1_qkv(oc):
            sec = ab(R_E + 1024 * (oc % 3), 1024)
            dma("gpsimd", sec.rearrange("p (c n) -> p c n", c=NCC),
                dwqkT[:, 128 * oc:128 * (oc + 1)].rearrange("(c p) n -> p c n",
                                                            c=NCC),
                waits=free(("sec", oc % 3)))
            prod(("sec", oc % 3), "gld")
            for c in range(NCC):
                for sp in range(4):
                    w0 = (free(("qkb", sp)) if c == 0 else [])
                    if (c, sp) == (0, 0):
                        w0 += need(("sec", oc % 3)) + [("cv", p1v_cv)]
                    E("tensor", lambda e, c=c, sp=sp, sec=sec: e.matmul(
                        PS[sp], sec[:, 128 * c:128 * (c + 1)],
                        xnf[:, T * c + 512 * sp:T * c + 512 * (sp + 1)],
                        start=(c == 0), stop=(c == NCC - 1)), w0, "ct", 1)
            prod(("qk",), "ct")
            cons(("sec", oc % 3), "ct")

        def p1_nonpe(oc):
            isq = oc < 8
            ch = oc % 8
            dst = qhat if isq else khat
            mxoff = S_MAXQ if isq else S_MAXK
            st_ = P1[oc]
            st_["c_ms"] = [Cell() for _ in range(4)]
            st_["c_wrs"] = [Cell() for _ in range(4)]
            st_["c_nsq"] = [Cell() for _ in range(4)]
            zfree = (P1[oc - 1]["c_nsq"][3:] if oc >= 1 else []) + free(("s4",))
            for bk in range(4):
                E("scalar", lambda e, bk=bk: e.copy(
                    out=s1[:, 512 * bk:512 * (bk + 1)],
                    in_=PS[bk]),
                  (need(("qk",)) + free(("s1",))) if bk == 0 else [], "cs", 1)
                E("scalar", lambda e, bk=bk: e.activation(
                    out=s4[:, 512 * bk:512 * (bk + 1)], in_=PS[bk],
                    func=AF.Square),
                  zfree if bk == 0 else [], "cs", 1)
                cons(("qkb", bk), "cs")
            cons(("qk",), "cs")
            prod(("s1",), "cs")
            prod(("s4",), "cs")
            st_["sq_ev"] = ("cs", A("cs"))
            for (a, b) in ((0, 32), (32, 0), (64, 96), (96, 64)):
                dma("sync", AF_[a:a + 32, F_S2:F_S2 + 2048],
                    AF_[b:b + 32, F_S1:F_S1 + 2048],
                    waits=need(("s1",)) + free(("s2b",)))
            cons(("s1",), "sld")
            prod(("s2b",), "sld")
            bwaits = need(("s2b",)) + ([("sld", consts_ld)] if oc == 0 else [])
            for sp in range(4):
                msfree = (free(("msb",)) +
                          ([P1[oc - 1]["c_wrs"][3]] if oc >= 1 else [])
                          if sp == 0 else [])
                E("vector", lambda e, sp=sp: e.tensor_copy(
                    out=AF_[0:2, F_MSB + 512 * sp:F_MSB + 512 * (sp + 1)],
                    in_=PS4[0:2, :]),
                  [st_["c_ms"][sp]] + msfree, "cv", 1)
                st_["msevac_ev_%d" % sp] = ("cv", A("cv"))
            prod(("msb",), "cv")
            st_["msevac3"] = ("cv", A("cv"))
            E("scalar", lambda e: e.activation(
                out=AF_[0:2, F_MSB:F_MSB + 2048], in_=AF_[0:2, F_MSB:F_MSB + 2048],
                func=AF.Ln, scale=1.0 / D, bias=sm_t[0:2, 2:3]),
              need(("msb",)), "cs", 1)
            E("scalar", lambda e: e.activation(
                out=AF_[0:2, F_MSB:F_MSB + 2048], in_=AF_[0:2, F_MSB:F_MSB + 2048],
                func=AF.Exp, scale=-0.5), [], "cs", 1)
            prod(("msb",), "cs")
            st_["rs_ev"] = ("cs", A("cs"))
            E("vector", lambda e: e.tensor_mul(out=s2b, in0=s2b, in1=sin_t),
              bwaits, "cv", 1)
            prod(("s2b",), "cv")
            E("vector", lambda e: e.tensor_mul(out=s4, in0=s1, in1=cos_t),
              need(("s1",)) + [st_["c_ms"][3]] + free(("s4",)), "cv", 1)
            cons(("s1",), "cv")
            prod(("s4",), "cv")
            E("vector", lambda e: e.tensor_add(out=s2b, in0=s4, in1=s2b),
              need(("s2b",), ("s4",)), "cv", 1)
            cons(("s4",), "cv")
            prod(("s2b",), "cv")
            for sp in range(4):
                E("vector", lambda e, sp=sp, dst=dst, ch=ch: e.tensor_mul(
                    out=dst[:, T * ch + 512 * sp:T * ch + 512 * (sp + 1)],
                    in0=s2b[:, 512 * sp:512 * (sp + 1)],
                    in1=PSB[:, 512 * (sp % 2):512 * (sp % 2) + 512]),
                  [st_["c_wrs"][sp]] + need(("s2b",)), "cv", 1)
                st_["tt_ev_%d" % sp] = ("cv", A("cv"))
                cons(("s2b",), "cv")
            st_["qh_ev"] = ("cv", A("cv"))
            E("scalar", lambda e, dst=dst, ch=ch: e.activation(
                out=s4, in_=dst[:, T * ch:T * (ch + 1)], func=AF.Square),
              [st_["qh_ev"]] + free(("s4",)), "cs", 1)
            prod(("s4",), "cs")
            st_["nsqsq_ev"] = ("cs", A("cs"))
            for sp in range(4):
                E("vector", lambda e, sp=sp: e.reduce_max(
                    out=AF_[0:2, S_TMX + sp:S_TMX + sp + 1],
                    in_=PS5[0:2, :], axis=AX.X), [st_["c_nsq"][sp]], "cv", 1)
                st_["nsqred_ev_%d" % sp] = ("cv", A("cv"))
            E("vector", lambda e, mxoff=mxoff, ch=ch: e.reduce_max(
                out=AF_[0:2, mxoff + ch:mxoff + ch + 1],
                in_=AF_[0:2, S_TMX:S_TMX + 4], axis=AX.X),
              [("cv", A("cv"))], "cv", 1)
            st_["done_cv"] = ("cv", A("cv"))

        def p1_pesmalls(oc):
            isq = oc < 8
            nstat = qn_t if isq else kn_t
            st_ = P1[oc]
            pst = P1[oc - 1] if oc >= 1 else None
            for sp in range(4):
                w0 = [st_["sq_ev"]]
                if sp > 0:
                    w0.append(st_["msevac_ev_%d" % (sp - 1)])
                elif pst is not None:
                    w0.append(pst["msevac_ev_3"])
                E("tensor", lambda e, sp=sp: e.matmul(
                    PS4[0:2, :], ones2_t[:], s4[:, 512 * sp:512 * (sp + 1)],
                    start=True, stop=True), w0, "ct", 1)
                em.fill(st_["c_ms"][sp], "ct")
            for sp in range(4):
                w0 = [st_["rs_ev"]]
                if sp >= 2:
                    w0.append(st_["tt_ev_%d" % (sp - 2)])
                elif pst is not None:
                    w0.append(pst["tt_ev_%d" % (sp + 2)])
                else:
                    w0 += free(("psb",))
                E("tensor", lambda e, sp=sp, nstat=nstat: e.matmul(
                    PSB[:, 512 * (sp % 2):512 * (sp % 2) + 512], nstat[:],
                    AF_[0:2, F_MSB + 512 * sp:F_MSB + 512 * (sp + 1)],
                    start=True, stop=True), w0, "ct", 1)
                em.fill(st_["c_wrs"][sp], "ct")
            for sp in range(4):
                w0 = [st_["nsqsq_ev"]]
                if sp > 0:
                    w0.append(st_["nsqred_ev_%d" % (sp - 1)])
                elif pst is not None:
                    w0.append(pst["nsqred_ev_3"])
                E("tensor", lambda e, sp=sp: e.matmul(
                    PS5[0:2, :], ones2_t[:], s4[:, 512 * sp:512 * (sp + 1)],
                    start=True, stop=True), w0, "ct", 1)
                em.fill(st_["c_nsq"][sp], "ct")

        for oc in range(16):
            p1_qkv(oc)
            if oc >= 1:
                p1_pesmalls(oc - 1)
            p1_nonpe(oc)
        p1_pesmalls(15)
        p1a_cv = A("cv")


        E("scalar", lambda e: e.activation(
            out=AF_[0:2, S_MAXQ:S_MAXQ + 8], in_=AF_[0:2, S_MAXQ:S_MAXQ + 8],
            func=AF.Sqrt), [("cv", p1a_cv)], "cs", 1)
        E("scalar", lambda e: e.activation(
            out=AF_[0:2, S_MAXK:S_MAXK + 8], in_=AF_[0:2, S_MAXK:S_MAXK + 8],
            func=AF.Sqrt), [], "cs", 1)
        E("vector", lambda e: e.tensor_mul(
            out=AF_[0:2, S_MAXQ:S_MAXQ + 8], in0=AF_[0:2, S_MAXQ:S_MAXQ + 8],
            in1=AF_[0:2, S_MAXK:S_MAXK + 8]), [("cs", A("cs"))], "cv", 1)
        E("vector", lambda e: e.tensor_scalar(
            out=AF_[0:2, S_MAXQ:S_MAXQ + 8], in0=AF_[0:2, S_MAXQ:S_MAXQ + 8],
            scalar1=-0.125, scalar2=None, op0=ALU.mult),
          [("cv", A("cv"))], "cv", 1)
        negM_cv = A("cv")
        for h in range(H):
            r, c = h % 2, h // 2
            E("tensor", lambda e, r=r, c=c, h=h: e.matmul(
                PS4[:, h:h + 1], sel_t[:, 128 * r:128 * (r + 1)],
                AF_[0:2, S_MAXQ + c:S_MAXQ + c + 1], start=True, stop=True),
              [("cv", negM_cv)] + (free(("ps4",)) if h == 0 else []), "ct", 1)
        prod(("ps4",), "ct")
        E("vector", lambda e: e.tensor_copy(
            out=af(S_BIASM, 16), in_=PS4[:, 0:16]), need(("ps4",)), "cv", 1)
        cons(("ps4",), "cv")
        biasM_cv = A("cv")
        tap("qhat", ab(R_B, 16384))
        tap("khat", ab(R_C, 16384))

        pt_off = []
        off = 0
        for j in range(8):
            pt_off.append((R_A, off))
            off += T - 128 * j
        off = 0
        for j in range(8, 16):
            pt_off.append((R_E, off))
            off += T - 128 * j

        def pt_ap(j, col0, width):
            base, o = pt_off[j]
            return ab(base + o + col0, width)

        for pair in range(8):
            for parx in range(2):
                h = 2 * pair + parx
                base = 64 * parx
                ptev = {}
                for j in range(NTC):
                    qt0 = 128 * j
                    nsp = (T - qt0 + 511) // 512
                    for spi in range(nsp):
                        c0 = qt0 + 512 * spi
                        w = min(512, T - c0)
                        bnk = ("sb", (j + spi) % 4)
                        bank = PS[(j + spi) % 4]
                        w0 = free(bnk) + ([("cv", biasM_cv)] if h == 0 else [])
                        E("tensor", lambda e, j=j, c0=c0, w=w, bank=bank, base=base,
                            pair=pair: e.matmul(
                            bank[:, 0:w],
                            khat[base:base + 64,
                                 T * pair + 128 * j:T * pair + 128 * (j + 1)],
                            qhat[base:base + 64, T * pair + c0:T * pair + c0 + w],
                            start=True, stop=True), w0, "ct", 1)
                        prod(bnk, "ct")
                        E("scalar", lambda e, j=j, c0=c0, w=w, bank=bank, qt0=qt0,
                            h=h: e.activation(
                            out=pt_ap(j, c0 - qt0, w), in_=bank[:, 0:w],
                            func=AF.Exp, scale=0.125, bias=af(S_BIASM + h, 1)),
                          need(bnk) +
                          (free(("pt", j)) + [("cv", biasM_cv)] if spi == 0 else []),
                          "cs", 1)
                        cons(bnk, "cs")
                    exp_ev = ("cs", A("cs"))
                    E("vector", lambda e, j=j: e.tensor_mul(
                        out=pt_ap(j, 0, 128), in0=pt_ap(j, 0, 128), in1=tri_t[:]),
                      [exp_ev], "cv", 1)
                    ptev[j] = [exp_ev, ("cv", A("cv"))]
                    prod(("pt", j), "cv")
                for s in range(4):
                    avb = ("avb", s % 2)
                    avbank = PS[4 + (s % 2)]
                    njs = 4 * s + 4
                    for j in range(njs):
                        coff = 512 * s - 128 * j
                        c0 = max(0, coff)
                        pcol = max(0, -coff)
                        w = 512 - pcol
                        w0 = list(ptev[j]) + (free(avb) if j == 0 else [])
                        E("tensor", lambda e, j=j, c0=c0, pcol=pcol, w=w,
                            avbank=avbank, h=h, njs=njs: e.matmul(
                            avbank[0:65, pcol:pcol + w],
                            vt[:, 1040 * j + 65 * h:1040 * j + 65 * h + 65],
                            pt_ap(j, c0, w),
                            start=(j == 0), stop=(j == njs - 1)),
                          w0 + ([("cv", vones_cv)] if h == 0 and s == 0 else []),
                          "ct", 1)
                        cons(("pt", j), "ct")
                    prod(avb, "ct")
                    E("scalar", lambda e, s=s, parx=parx, avbank=avbank: e.copy(
                        out=AF_[0:65, F_S1 + 2048 * parx + 512 * s:
                                F_S1 + 2048 * parx + 512 * (s + 1)],
                        in_=avbank[0:65, :]),
                      need(avb) + (free(("ascr", parx)) if s == 0 else []), "cs", 1)
                    cons(avb, "cs")
                prod(("ascr", parx), "cs")
            for parx in range(2):
                h = 2 * pair + parx
                E("scalar", lambda e, parx=parx: e.activation(
                    out=AF_[64:65, F_MSB:F_MSB + 2048],
                    in_=AF_[64:65, F_S1 + 2048 * parx:F_S1 + 2048 * parx + 2048],
                    func=AF.Ln),
                  need(("ascr", parx)) + free(("msb",)), "cs", 1)
                E("scalar", lambda e: e.activation(
                    out=AF_[64:65, F_MSB:F_MSB + 2048],
                    in_=AF_[64:65, F_MSB:F_MSB + 2048], func=AF.Exp, scale=-1.0),
                  [], "cs", 1)
                prod(("msb",), "cs")
                for s in range(4):
                    bnk = ("nb", s % 2)
                    bank = PSB[0:64, 512 * (s % 2):512 * (s % 2) + 512]
                    E("tensor", lambda e, s=s, bank=bank: e.matmul(
                        bank, onesf_t[64:65, 0:64],
                        AF_[64:65, F_MSB + 512 * s:F_MSB + 512 * (s + 1)],
                        start=True, stop=True),
                      need(("msb",)) + free(bnk), "ct", 1)
                    prod(bnk, "ct")
                    E("scalar", lambda e, s=s, bank=bank: e.copy(
                        out=AF_[0:64, F_MSB + 512 * s:F_MSB + 512 * (s + 1)],
                        in_=bank),
                      need(bnk) + (free(("rb",)) if s == 0 else []), "cs", 1)
                    cons(bnk, "cs")
                prod(("rb",), "cs")
                if parx == 0:
                    E("vector", lambda e, pair=pair: e.tensor_mul(
                        out=AB[0:64, R_B + T * pair:R_B + T * (pair + 1)],
                        in0=AF_[0:64, F_S1:F_S1 + 2048],
                        in1=AF_[0:64, F_MSB:F_MSB + 2048]),
                      need(("rb",), ("ascr", 0)), "cv", 1)
                    cons(("rb",), "cv")
                    cons(("ascr", 0), "cv")
                else:
                    E("vector", lambda e: e.tensor_mul(
                        out=AB[0:64, R_A + 12800:R_A + 12800 + 2048],
                        in0=AF_[0:64, F_S1 + 2048:F_S1 + 4096],
                        in1=AF_[0:64, F_MSB:F_MSB + 2048]),
                      need(("rb",), ("ascr", 1)) + free(("oscr",)), "cv", 1)
                    cons(("rb",), "cv")
                    cons(("ascr", 1), "cv")
                    prod(("oscr",), "cv")
                    E("sync", lambda e, pair=pair: e.dma_start(
                        out=AB[64:128, R_B + T * pair:R_B + T * (pair + 1)],
                        in_=AB[0:64, R_A + 12800:R_A + 12800 + 2048]),
                      need(("oscr",)), "sld", 16)
                    cons(("oscr",), "sld")
            emit_wjobs(17)
        emit_wjobs(200)
        tap("attnout", ab(R_B, 16384))
        attn_cv = A("cv")
        attn_cs = A("cs")
        attn_sld = A("sld")
        attn_ct = A("ct")
        attnout = qhat

        XQW = 2052
        xqvf = ab(R_D, NCC * XQW)
        E("vector", lambda e: e.memset(
            xqvf.rearrange("p (c t) -> p c t", c=NCC)[:, :, 0:1], 0.0),
          [("cv", attn_cv), ("cs", attn_cs), ("ct", attn_ct), ("sld", attn_sld)],
          "cv", 1)
        E("vector", lambda e: e.memset(
            xqvf.rearrange("p (c t) -> p c t", c=NCC)[:, :, 2049:2052], 0.0),
          [], "cv", 1)
        guards_cv = A("cv")
        for c in range(NCC):
            dma("gpsimd", ab(R_C + 1024 * c, 1024), dwoutT[128 * c:128 * (c + 1), :],
                waits=[("ct", attn_ct), ("cs", attn_cs)], sem="wo")
        wout_ld = A("wo")

        E("vector", lambda e: e.reduce_sum(
            out=af(S_WSUMS + 128, 1), in_=af(S_WSUMS, 64), axis=AX.X),
          [("cv", A("cv"))], "cv", 1)
        E("vector", lambda e: e.reduce_sum(
            out=af(S_WSUMS + 129, 1), in_=af(S_WSUMS + 64, 64), axis=AX.X),
          [("cv", A("cv"))], "cv", 1)
        wred_cv = A("cv")
        for k in range(2):
            E("tensor", lambda e, k=k: e.matmul(
                PS4[0:1, k:k + 1], onesf_t[:, 0:1], af(S_WSUMS + 128 + k, 1),
                start=True, stop=True),
              [("cv", wred_cv), ("cs", attn_cs)] +
              (free(("ps4",), ("ps45",)) if k == 0 else []), "ct", 1)
        prod(("ps4",), "ct")
        E("vector", lambda e: e.tensor_copy(
            out=AF_[0:1, S_WSUMS + 130:S_WSUMS + 132], in_=PS4[0:1, 0:2]),
          need(("ps4",)), "cv", 1)
        cons(("ps4",), "cv")
        E("vector", lambda e: e.tensor_scalar(
            out=AF_[0:1, S_WSUMS + 130:S_WSUMS + 132],
            in0=AF_[0:1, S_WSUMS + 130:S_WSUMS + 132],
            scalar1=1.0 / (C * HID), scalar2=1e-5, op0=ALU.mult, op1=ALU.max),
          [("cv", A("cv"))], "cv", 1)
        E("vector", lambda e: e.reciprocal(
            out=AF_[0:1, S_WSUMS + 132:S_WSUMS + 134],
            in_=AF_[0:1, S_WSUMS + 130:S_WSUMS + 132]),
          [("cv", A("cv"))], "cv", 1)
        mw_cv = A("cv")
        E("tensor", lambda e: e.matmul(
            PS4[:, 0:4], onesf_t[0:1, 0:128],
            AF_[0:1, S_WSUMS + 130:S_WSUMS + 134], start=True, stop=True),
          [("cv", mw_cv)] + free(("ps4",)), "ct", 1)
        prod(("ps4",), "ct")
        E("vector", lambda e: e.tensor_copy(out=af(S_WSC, 4), in_=PS4[:, 0:4]),
          need(("ps4",)), "cv", 1)
        cons(("ps4",), "cv")
        wsc_cv = A("cv")


        def p25_tp(i):
            spar = i % 3
            src = ab(R_E + 1024 * spar, 1024)
            dma("sync", src, dxqs[128 * i:128 * (i + 1), :],
                waits=free(("tsrc", spar)) +
                ([("dac", p0_dac), ("sld", attn_sld), ("ct", attn_ct),
                  ("cs", attn_cs), ("cv", attn_cv)] if i == 0 else []),
                sem="ts%d" % spar)
            prod(("tsrc", spar), "ts%d" % spar)
            w32 = af(F_S4 + 1024 * (i % 2), 1024)
            E("scalar", lambda e, src=src, w32=w32: e.activation(
                out=w32, in_=src, func=AF.Copy),
              need(("tsrc", spar)) + free(("w32", i % 2)), "cs", 1)
            cons(("tsrc", spar), "cs")
            prod(("w32", i % 2), "cs")
            for c in range(NCC):
                tb = ("tpb", c % 2)
                w0 = (need(("w32", i % 2)) if c == 0 else []) + free(tb) + \
                    ([("cs", attn_cs), ("ct", attn_ct)] if i == 0 and c < 2 else [])
                E("tensor", lambda e, c=c, w32=w32: e.transpose(
                    psq(c % 2, 0, 128, 0, 128), w32[:, 128 * c:128 * (c + 1)],
                    id_t[:]), w0, "ct", 1)
                prod(tb, "ct")
                if c == NCC - 1:
                    cons(("w32", i % 2), "ct")
                dst = xqvf[:, XQW * c + 1 + 128 * i:XQW * c + 1 + 128 * (i + 1)]
                if c % 2 == 0:
                    E("scalar", lambda e, c=c, dst=dst: e.activation(
                        out=dst, in_=psq(c % 2, 0, 128, 0, 128), func=AF.Copy),
                      need(tb) + ([("cv", guards_cv)] if i == 0 else []), "cs", 1)
                    cons(tb, "cs")
                else:
                    E("vector", lambda e, c=c, dst=dst: e.tensor_copy(
                        out=dst, in_=psq(c % 2, 0, 128, 0, 128)),
                      need(tb) + ([("cv", guards_cv)] if i == 0 else []), "cv", 1)
                    cons(tb, "cv")

        def p25_proj(i):
            par = i % 2
            xbuf = af(F_MSB + 1024 * par, 1024)
            obuf = af(F_S2 + 1024 * par, 1024)
            dma("sync", xbuf, dx[128 * i:128 * (i + 1), :],
                waits=free(("xre", par)) +
                ([("cv", attn_cv), ("cs", attn_cs)] if i < 2 else []),
                sem="xb%d" % par)
            prod(("xre", par), "xb%d" % par)
            pjr = ("psb",) if par == 0 else (("ps4",), ("ps45",))
            for c in range(NCC):
                for sp in range(2):
                    w0 = ([("wo", wout_ld), ("sld", attn_sld),
                           ("cv", attn_cv)] +
                          (free(("psb",)) if par == 0
                           else free(("ps4",), ("ps45",)))
                          if (c, sp) == (0, 0) else [])

                    def pjmm(e, i=i, c=c, sp=sp, par=par):
                        outap = (PSB[:, 512 * sp:512 * (sp + 1)] if par == 0
                                 else (PS4[:] if sp == 0 else PS5[:]))
                        return e.matmul(
                            outap,
                            attnout[:, T * c + 128 * i:T * c + 128 * (i + 1)],
                            ab(R_C + 1024 * c + 512 * sp, 512),
                            start=(c == 0), stop=(c == NCC - 1))
                    E("tensor", pjmm, w0, "ct", 1)
            if par == 0:
                prod(("psb",), "ct")
            else:
                prod(("ps4",), "ct")
                prod(("ps45",), "ct")

            def pjfin(e, xbuf=xbuf, obuf=obuf, par=par):
                if par == 0:
                    return e.scalar_tensor_tensor(
                        out=obuf, in0=PSB[:], scalar=1.0, in1=xbuf,
                        op0=ALU.mult, op1=ALU.add)
                e.scalar_tensor_tensor(
                    out=obuf[:, 0:512], in0=PS4[:], scalar=1.0,
                    in1=xbuf[:, 0:512], op0=ALU.mult, op1=ALU.add)
                return e.scalar_tensor_tensor(
                    out=obuf[:, 512:1024], in0=PS5[:], scalar=1.0,
                    in1=xbuf[:, 512:1024], op0=ALU.mult, op1=ALU.add)
            pw = (need(("psb",)) if par == 0 else need(("ps4",), ("ps45",))) + \
                need(("xre", par)) + free(("obuf", par))
            E("vector", pjfin, pw, "cv", 1)
            if par == 0:
                cons(("psb",), "cv")
            else:
                cons(("ps4",), "cv")
                cons(("ps45",), "cv")
            cons(("xre", par), "cv")
            prod(("obuf", par), "cv")
            dma("sync", daccs[128 * i:128 * (i + 1), :], obuf,
                waits=need(("obuf", par)), sem="dac")
            cons(("obuf", par), "dac")

        wsign2 = [(R_A + 1024 * hc) if hc < 16 else (R_B + 1024 * (hc - 16))
                  for hc in range(NHC)]
        w1q_g = {}
        fcnt = [0]
        cg_hw = A("cg")

        STRB = (14016, 14528, 15040, 15552)

        def item_w2(hc, h):
            k5 = fcnt[0] % 4
            first = fcnt[0] < 4
            fcnt[0] += 1
            buf = af(STRB[k5], 512)
            dma("gpsimd", buf, dw2T[128 * hc:128 * (hc + 1), 512 * h:512 * (h + 1)],
                waits=free(("fcos", k5)) +
                ([("cv", attn_cv), ("cs", attn_cs), ("sld", attn_sld),
                  ("cg", cg_hw)] if first else []), sem="sb%d" % k5)
            prod(("fcos", k5), "sb%d" % k5)
            E("vector", lambda e, buf=buf: e.tensor_scalar(
                out=i32b[:], in0=buf, scalar1=AF_[:, S_WSC + 3:S_WSC + 4],
                scalar2=None, op0=ALU.mult),
              need(("fcos", k5)) + ([("cv", wsc_cv)] if hc == 0 and h == 0
                                    else []), "cv", 1)
            cons(("fcos", k5), "cv")
            E("vector", lambda e, hc=hc, h=h: e.tensor_scalar(
                out=ab(wsign2[hc] + 512 * h, 512), in0=i32b[:],
                scalar1=-1.0, scalar2=1.0, op0=ALU.max, op1=ALU.min),
              [("sld", attn_sld), ("ct", attn_ct)] if hc == 0 and h == 0 else
              ([("ct", proj_ct), ("sld", proj_sld)] if hc == 16 and h == 0
               else []), "cv", 1)

        def item_w1(jj, h):
            b4, c = divmod(jj, 8)
            k5 = fcnt[0] % 4
            first = fcnt[0] < 4
            fcnt[0] += 1
            buf = af(STRB[k5], 512)
            dma("gpsimd", buf,
                dw1T[128 * c:128 * (c + 1),
                     1024 * b4 + 512 * h:1024 * b4 + 512 * (h + 1)],
                waits=free(("fcos", k5)) +
                ([("cv", attn_cv), ("cs", attn_cs), ("sld", attn_sld),
                  ("cg", cg_hw)] if first else []), sem="sb%d" % k5)
            prod(("fcos", k5), "sb%d" % k5)
            E("vector", lambda e, buf=buf: e.tensor_scalar(
                out=i32b[:], in0=buf, scalar1=AF_[:, S_WSC + 2:S_WSC + 3],
                scalar2=None, op0=ALU.mult), need(("fcos", k5)), "cv", 1)
            cons(("fcos", k5), "cv")
            bk = (2 * jj + h) % 8
            bout = ab(R_C + 8192 + 512 * bk, 512)
            E("vector", lambda e, bout=bout: e.tensor_scalar(
                out=bout, in0=i32b[:], scalar1=-1.0, scalar2=1.0,
                op0=ALU.max, op1=ALU.min),
              free(("bout", bk)) + ([("ct", attn_ct)] if 2 * jj + h < 8
                                    else []), "cv", 1)
            prod(("bout", bk), "cv")
            dma("sync",
                dw1q[128 * c:128 * (c + 1),
                     1024 * b4 + 512 * h:1024 * b4 + 512 * (h + 1)],
                bout, waits=need(("bout", bk)), sem="wq")
            cons(("bout", bk), "wq")
            if c == 7 and h == 1:
                w1q_g[b4] = A("wq")

        items = [(lambda hc=hc, h=h: item_w2(hc, h))
                 for hc in range(16) for h in range(2)] + \
                [(lambda jj=jj, h=h: item_w1(jj, h))
                 for jj in range(32) for h in range(2)]
        xqvf_cv = xqvf_cs = None
        for i in range(NTC):
            p25_tp(i)
            if i == NTC - 1:
                xqvf_cs = A("cs")
                xqvf_cv = A("cv")
            if i >= 2:
                p25_proj(i - 2)
            for fn in items[6 * i:6 * i + 6]:
                fn()
        tap("xqvf", xqvf)
        p25_proj(NTC - 2)
        p25_proj(NTC - 1)
        proj_sld = A("sld")
        proj_dac = A("dac")
        proj_ct = A("ct")
        for hc in range(16, 32):
            for h in range(2):
                item_w2(hc, h)
        w2q_cv = A("cv")

        yt2 = af(F_YT, 8192)

        def xq2b(ti):
            return ab(R_C + 8192 * (ti % 2), 8192)
        pend_silu = []
        pend_T = []
        pend_sT = []
        pend_BC = []
        pend_R = []
        rg_cells = [[Cell() for _ in range(4)] for _ in range(NFT)]
        s2bcs_ev = {}
        yt2ev = {}
        S2BCS = 14016

        def flush_R():
            if pend_R:
                grp = pend_R.pop(0)
                for fn in grp:
                    fn()

        def flush_T():
            while pend_T:
                qti, tw, cT = pend_T.pop(0)
                for k in range(2):
                    E("tensor", lambda e, k=k: e.transpose(
                        psq(2, 0, 128, 128 * k, 128 * (k + 1)),
                        af(F_FOLD + 128 * k, 128), id_t[:]),
                      (need(("fold",)) + tw) if k == 0 else [], "ct", 1)
                    em.fill(cT[k], "ct")
                cons(("fold",), "ct")

        def flush_sT():
            while pend_sT:
                qti, s2cv, fw, cT_s = pend_sT.pop(0)
                E("tensor", lambda e, qti=qti: e.transpose(
                    PS5[0:2, 0:128],
                    AF_[:, S_S2SC + 2 * qti:S_S2SC + 2 * qti + 2], id_t[:]),
                  [("cv", s2cv)] + fw, "ct", 1)
                em.fill(cT_s, "ct")

        def flush_BC():
            while pend_BC:
                qti, gat_sg, cBC = pend_BC.pop(0)
                E("tensor", lambda e: e.matmul(
                    PS4[:, 0:FT], onesf_t[64:65, 0:128],
                    AF_[64:65, S_S2ROW + 128:S_S2ROW + 128 + FT],
                    start=True, stop=True), [("sg", gat_sg)], "ct", 1)
                cons(("s2rg",), "ct")
                em.fill(cBC, "ct")

        def flush_silu():
            while pend_silu:
                fn, waits, key = pend_silu.pop(0)
                E("scalar", fn, waits, "cs", 1)
                yt2ev[key] = ("cs", A("cs"))

        def p3_h(ti):
            a = FT * ti
            hspan = FT + 2
            for hc in range(NHC):
                par = hc % 2
                p4 = hc % 4
                hbr = ("hb", p4)
                hb = PS[p4]
                sec = ab(R_E + 1024 * par, 1024)
                dma("sync", sec.rearrange("p (c n) -> p c n", c=NCC),
                    dw1q[:, 128 * hc:128 * (hc + 1)].rearrange(
                        "(c p) n -> p c n", c=NCC),
                    waits=[("wq", w1q_g[hc // 8]), ("cv", xqvf_cv),
                           ("cs", xqvf_cs)] + free(("w1sec", par)),
                    sem="ws%d" % par)
                prod(("w1sec", par), "ws%d" % par)
                for c in range(NCC):
                    w0 = (need(("w1sec", par)) + free(hbr) +
                          ([("cs", xqvf_cs), ("cv", xqvf_cv), ("ct", attn_ct)]
                           if ti == 0 and hc < 4 else [])
                          if c == 0 else [])
                    E("tensor", lambda e, a=a, hspan=hspan, c=c, hb=hb, sec=sec:
                        e.matmul(
                        hb[:, 0:hspan], sec[:, 128 * c:128 * (c + 1)],
                        xqvf[:, XQW * c + a:XQW * c + a + hspan],
                        start=(c == 0), stop=(c == NCC - 1)), w0, "ct", 1)
                prod(hbr, "ct")
                cons(("w1sec", par), "ct")
                csc = af(F_CONV0 + 256 * par, FT)
                cscr = ("csc", par)
                E("scalar", lambda e, hb=hb, hc=hc, csc=csc: e.activation(
                    out=csc, in_=hb[:, 2:2 + FT], func=AF.Copy,
                    scale=taps_t[:, 64 + hc:65 + hc]),
                  need(hbr) + free(cscr), "cs", 1)
                cons(hbr, "cs")
                prod(cscr, "cs")
                E("vector", lambda e, hb=hb, hc=hc, csc=csc: e.scalar_tensor_tensor(
                    out=csc, in0=hb[:, 0:FT], scalar=taps_t[:, hc:hc + 1],
                    in1=csc, op0=ALU.mult, op1=ALU.add),
                  need(cscr, hbr), "cv", 1)
                prod(cscr, "cv")
                E("vector", lambda e, hb=hb, hc=hc, csc=csc: e.scalar_tensor_tensor(
                    out=csc, in0=hb[:, 1:1 + FT], scalar=taps_t[:, 32 + hc:33 + hc],
                    in1=csc, op0=ALU.mult, op1=ALU.add),
                  need(cscr), "cv", 1)
                cons(hbr, "cv")
                prod(cscr, "cv")
                silu_waits = (need(cscr) + [("cv", wsc_cv)] +
                              ([rg_cells[ti - 1][hc // 8]] if ti >= 1 else []))
                pend_silu.append((
                    (lambda e, hc=hc, csc=csc: e.activation(
                        out=yt2[:, FT * hc:FT * (hc + 1)], in_=csc, func=AF.Silu,
                        scale=AF_[:, S_WSC:S_WSC + 1], bias=convb_t[:, hc:hc + 1])),
                    silu_waits, ti))
                cons(cscr, "cs")
                if len(pend_silu) > 1:
                    fn, waits, key = pend_silu.pop(0)
                    E("scalar", fn, waits, "cs", 1)
                    yt2ev[key] = ("cs", A("cs"))
                if hc == 1:
                    flush_T()
                    flush_R()
                elif hc == 2:
                    flush_sT()
                elif hc == 3:
                    flush_BC()
                elif hc in (5, 9, 13):
                    flush_R()

        def p3_quant(ti):
            flush_silu()
            xq2 = xq2b(ti)
            sev = yt2ev[ti]
            E("vector", lambda e: e.tensor_reduce(
                out=af(F_FOLD, FT), in_=yt2.rearrange("p (c t) -> p t c", c=NHC),
                axis=AX.X, op=ALU.max, apply_absolute_value=True),
              [sev] + free(("fold",)), "cv", 1)
            prod(("fold",), "cv")
            cT = [Cell(), Cell()]
            pend_T.append((ti, free(("hb", 2)), cT))
            for k in range(2):
                gcol = 2 * ti + k
                E("vector", lambda e, k=k: e.reduce_max(
                    out=af(S_TMX + k, 1),
                    in_=psq(2, 0, 128, 128 * k, 128 * (k + 1)), axis=AX.X),
                  [cT[k]], "cv", 1)
                cons(("hb", 2), "cv")
                E("vector", lambda e, k=k, gcol=gcol: e.tensor_scalar(
                    out=af(S_FSC + gcol, 1), in0=af(S_TMX + k, 1), scalar1=1e-5,
                    scalar2=1.0 / 127.0, op0=ALU.max, op1=ALU.mult),
                  [("cv", A("cv"))], "cv", 1)
                E("vector", lambda e, gcol=gcol: e.reciprocal(
                    out=af(S_S2SC + gcol, 1), in_=af(S_FSC + gcol, 1)),
                  [("cv", A("cv"))], "cv", 1)
                E("vector", lambda e, gcol=gcol: e.tensor_scalar(
                    out=af(S_FSC + gcol, 1), in0=af(S_FSC + gcol, 1),
                    scalar1=AF_[:, S_WSC + 1:S_WSC + 2], scalar2=None,
                    op0=ALU.mult), [("cv", A("cv"))], "cv", 1)
            s2_cv = A("cv")
            cT_s = Cell()
            cBC = Cell()
            fw = free(("w2b",))
            pend_sT.append((ti, s2_cv, fw, cT_s))
            E("vector", lambda e: e.tensor_copy(
                out=AF_[0:2, S_S2ROW:S_S2ROW + 128], in_=PS5[0:2, 0:128]),
              [cT_s] + free(("s2row",)), "cv", 1)
            prod(("s2row",), "cv")
            for k in range(2):
                E("gpsimd", lambda e, k=k: e.dma_start(
                    out=AF_[64:65, S_S2ROW + 128 + 128 * k:
                            S_S2ROW + 256 + 128 * k],
                    in_=AF_[k:k + 1, S_S2ROW:S_S2ROW + 128]),
                  (need(("s2row",)) + free(("s2rg",))) if k == 0 else [],
                  "sg", 16)
            cons(("s2row",), "sg")
            prod(("s2rg",), "sg")
            pend_BC.append((ti, A("sg"), cBC))
            E("vector", lambda e: e.tensor_copy(
                out=af(S2BCS, FT), in_=PS4[:, 0:FT]),
              [cBC] + free(("s2bcs",)) +
              ([("cv", w2q_cv)] if ti == 0 else []), "cv", 1)
            prod(("s2bcs",), "cv")
            s2bcs_ev[ti] = ("cv", A("cv"))

            def mk_round(hc, xq2=xq2, ti=ti, sev=sev):
                def emit_r():
                    E("vector", lambda e, hc=hc: e.tensor_tensor(
                        out=i32b[:, 0:FT], in0=yt2[:, FT * hc:FT * (hc + 1)],
                        in1=af(S2BCS, FT), op=ALU.mult),
                      (need(("s2bcs",)) if hc % 8 == 0 else []) + [sev],
                      "cv", 1)
                    cons(("s2bcs",), "cv")
                    E("vector", lambda e, hc=hc: e.tensor_copy(
                        out=xq2[:, FT * hc:FT * (hc + 1)], in_=i32b[:, 0:FT]),
                      [("cv", A("cv"))] + free(("xq2", ti % 2, hc)), "cv", 1)
                    prod(("xq2", ti % 2, hc), "cv")
                    if hc % 8 == 7:
                        em.fill(rg_cells[ti][hc // 8], "cv")
                return emit_r
            for g in range(4):
                pend_R.append([mk_round(8 * g + j) for j in range(8)])

        def p3_w2(ti):
            a = FT * ti
            xq2 = xq2b(ti)
            for hc in range(NHC):
                for sp in range(2):
                    w0 = (need(("xq2", ti % 2, hc)) +
                          ([("cv", w2q_cv)] if ti == 0 and hc == 0 else []) +
                          (free(("w2a",), ("psb",)) if hc == 0 and sp == 0
                           else []))
                    E("tensor", lambda e, hc=hc, sp=sp: e.matmul(
                        PSB[:, 512 * sp:512 * (sp + 1)],
                        xq2[:, FT * hc:FT * hc + 128],
                        ab(wsign2[hc] + 512 * sp, 512),
                        start=(hc == 0), stop=(hc == NHC - 1)), w0, "ct", 1)
                    w1_ = (free(("w2b",), ("ps4",), ("ps45",)) +
                           [s2bcs_ev[ti]]
                           if hc == 0 and sp == 0 else [])
                    E("tensor", lambda e, hc=hc, sp=sp: e.matmul(
                        PS4[:] if sp == 0 else PS5[:],
                        xq2[:, FT * hc + 128:FT * hc + 256],
                        ab(wsign2[hc] + 512 * sp, 512),
                        start=(hc == 0), stop=(hc == NHC - 1)), w1_, "ct", 1)
                cons(("xq2", ti % 2, hc), "ct")
            prod(("w2a",), "ct")
            prod(("w2b",), "ct")
            for k in range(2):
                gcol = 2 * ti + k
                t0 = a + 128 * k
                abuf = af(F_XNC3 + 1024 * k, 1024)
                ob = af(F_OUTC + 1024 * k, 1024)
                dma("sync", abuf, daccs[t0:t0 + 128, :],
                    waits=free(("abuf", k)) + [("dac", proj_dac)],
                    sem="ab%d" % k)
                prod(("abuf", k), "ab%d" % k)
                wres = ("w2a",) if k == 0 else ("w2b",)
                for half in range(2):
                    def fin(e, k=k, gcol=gcol, abuf=abuf, ob=ob, half=half):
                        src = (PSB[:, 512 * half:512 * (half + 1)] if k == 0
                               else (PS4[:] if half == 0 else PS5[:]))
                        return e.scalar_tensor_tensor(
                            out=ob[:, 512 * half:512 * (half + 1)], in0=src,
                            scalar=af(S_FSC + gcol, 1),
                            in1=abuf[:, 512 * half:512 * (half + 1)],
                            op0=ALU.mult, op1=ALU.add)
                    E("vector", fin,
                      need(wres, ("abuf", k)) + free(("outc", k)), "cv", 1)
                cons(wres, "cv")
                cons(("abuf", k), "cv")
                prod(("outc", k), "cv")
                dma("sync", dout[t0:t0 + 128, :], ob, waits=need(("outc", k)))
                cons(("outc", k), "sld")

        for ti in range(NFT):
            p3_h(ti)
            if ti >= 1:
                p3_w2(ti - 1)
            p3_quant(ti)
        flush_T()
        flush_sT()
        flush_BC()
        while pend_R:
            flush_R()
        p3_w2(NFT - 1)
        if DEBUG_TAPS:
            srcs = {"biasM": af(S_BIASM, 16),
                    "wsc": af(S_WSC, 4), "y0": yt2, "xq2t0": xq2b(NFT - 1),
                    "s2": af(S_S2SC, 16),
                    "abuf1": af(F_XNC3 + 1024, 1024),
                    "ob1": af(F_OUTC + 1024, 1024)}
            for name in DEBUG_TAPS:
                if name in srcs:
                    dma("sync", dbg[name], srcs[name],
                        waits=[("cv", A("cv")), ("cs", A("cs")), ("ct", A("ct")),
                               ("gld", A("gld"))])
        E("sync", lambda e: e.nop(),
          [(s, A(s)) for s in ("sld", "gld", "tsp", "ts0", "ts1", "ts2",
                               "xb0", "xb1", "ab0", "ab1", "ws0", "ws1",
                               "sb0", "sb1", "sb2", "sb3", "wo", "wq",
                               "dac", "sg")], None)

        block = st.enter_context(nc.Block())
        em.finalize(block, sems)
    return nc


def kernel(**inputs):
    in_maps = _host_inputs(**inputs)
    nc = build_nc()
    res = run_bass_kernel_spmd(nc, in_maps, core_ids=list(range(8)))
    out = np.stack([np.asarray(r["out"]) for r in res.results], axis=0)
    return out.astype(np.float32, copy=False)
